# version 07
# baseline (speedup 1.0000x reference)
import numpy as np
import ml_dtypes
from contextlib import ExitStack

import concourse.bass as bass
import concourse.mybir as mybir
from concourse.bass_utils import run_bass_kernel_spmd

F32 = mybir.dt.float32
BF16 = mybir.dt.bfloat16
I32 = mybir.dt.int32
AT = mybir.ActivationFunctionType
ALU = mybir.AluOpType
AX = mybir.AxisListType

NCORES = 8
D = 1024
DI = 2048
NST = 16
DTR = 64
TN = 4096
TA = 512
NTILES = TN // TA
NDK = D // 128
NIK = DI // 128
TB = 512
NSUB = TN // TB
TC = 512
NCT = TN // TC
EPS1 = 1e-5
EPS2 = 1e-3
QB = 128.0
QEPS = 1e-5
CLIP_HI = QB - QEPS
CLIP_LO = -QB + QEPS
SUMR = 96
BF = ml_dtypes.bfloat16
DEBUG = False


class Sem:
    def __init__(self, nc, ctx, name):
        self.h = ctx.enter_context(nc.semaphore(name))
        self.n = 0


class Arena:
    def __init__(self, nc, ctx, words):
        self.t = ctx.enter_context(nc.sbuf_tensor("ARENA", [128, words], F32))
        self.words = words
        self.off = 0
        self.marks = []

    def mark(self):
        self.marks.append(self.off)

    def reset(self):
        self.off = self.marks.pop()

    def f32(self, n, parts=128):
        o = self.off
        assert o + n <= self.words, f"arena overflow {o}+{n}>{self.words}"
        self.off += n
        return self.t.ap()[0:parts, o:o + n]

    def bf16(self, n, parts=128):
        nw = (n + 1) // 2
        o = self.off
        assert o + nw <= self.words, f"arena overflow {o}+{nw}>{self.words}"
        self.off += nw
        return self.t.ap().bitcast(BF16)[0:parts, 2 * o:2 * o + n]


def build_nc():
    nc = bass.Bass("TRN2", target_bir_lowering=False)

    x_h = nc.dram_tensor("x", [TN + 3, D], F32, kind="ExternalInput").ap()
    winT_h = nc.dram_tensor("winT", [32, 128, NDK * 128], F32, kind="ExternalInput").ap()
    woutT_h = nc.dram_tensor("woutT", [NIK, 128, D], F32, kind="ExternalInput").ap()
    ffnw_h = nc.dram_tensor("ffnw", [NDK, 128, D], F32, kind="ExternalInput").ap()
    xprojT_h = nc.dram_tensor("xprojT", [2, 128, 8 * 96], F32, kind="ExternalInput").ap()
    dtT_h = nc.dram_tensor("dtT", [DTR, DI], F32, kind="ExternalInput").ap()
    convd_h = nc.dram_tensor("convd", [8, 128, 1024], F32, kind="ExternalInput").ap()
    alog_h = nc.dram_tensor("alogT", [128, NIK * NST], F32, kind="ExternalInput").ap()
    dpar_h = nc.dram_tensor("dparT", [128, NIK], F32, kind="ExternalInput").ap()
    inb_h = nc.dram_tensor("inbT", [128, 32], F32, kind="ExternalInput").ap()
    convb_h = nc.dram_tensor("convbT", [128, NIK], F32, kind="ExternalInput").ap()
    dtb_h = nc.dram_tensor("dtbT", [128, NIK], F32, kind="ExternalInput").ap()
    outb_h = nc.dram_tensor("outb", [1, D], F32, kind="ExternalInput").ap()
    gamln_h = nc.dram_tensor("gamlnT", [128, NDK], F32, kind="ExternalInput").ap()
    betln_h = nc.dram_tensor("betlnT", [128, NDK], F32, kind="ExternalInput").ap()
    ident_h = nc.dram_tensor("identity", [128, 128], F32, kind="ExternalInput").ap()
    maskf_h = nc.dram_tensor("maskf", [128, 16], F32, kind="ExternalInput").ap()
    meti_h = nc.dram_tensor("meti", [1, 8], I32, kind="ExternalInput").ap()

    out_h = nc.dram_tensor("out", [TN, D], F32, kind="ExternalOutput").ap()
    dbg = {}
    if DEBUG:
        for nm, shp, dt in [("d_m", [NIK, 128, TN], F32), ("d_u", [NIK, 128, TN], F32),
                            ("d_sr", [NIK, 128, TN], F32), ("d_sm", [NIK, 128, TN], F32),
                            ("d_y", [NIK, 128, TN], F32), ("d_xn2", [NDK, 128, TN], BF16),
                            ("d_b", [NST, TN], F32), ("d_c", [NST, TN], F32),
                            ("d_hin", [128, NIK * NST], F32), ("d_hst", [128, NIK * NST], F32),
                            ("d_summ", [128, 528], F32), ("d_gsc", [128, 4], F32),
                            ("d_ccout", [NCORES * SUMR, 1024], F32),
                            ("d_xbe", [NIK, 128, 16], BF16), ("d_xnh", [128, NDK * 4], BF16),
                            ("d_stath", [3, 12], F32), ("d_drb", [128, 512], BF16),
                            ("d_yf", [NIK, 128, TN], F32), ("d_g", [NIK, 128, TN], BF16),
                            ("d_xq", [NDK, 128, TN], BF16), ("d_corr", [NIK, 128, TN], F32)]:
            dbg[nm] = nc.dram_tensor(nm, shp, dt, kind="ExternalOutput").ap()

    m_sp = nc.dram_tensor("m_sp", [NIK, 128, TN], F32, kind="Internal").ap()
    u_sp = nc.dram_tensor("u_sp", [NIK, 128, TN], F32, kind="Internal").ap()
    sr_sp = nc.dram_tensor("sr_sp", [NIK, 128, TN], F32, kind="Internal").ap()
    sm_sp = nc.dram_tensor("sm_sp", [NIK, 128, TN], F32, kind="Internal").ap()
    y_sp = nc.dram_tensor("y_sp", [NIK, 128, TN], F32, kind="Internal").ap()
    xn2_sp = nc.dram_tensor("xn2_sp", [NDK, 128, TN], BF16, kind="Internal").ap()
    b_sp = nc.dram_tensor("b_sp", [NST, TN], F32, kind="Internal").ap()
    c_sp = nc.dram_tensor("c_sp", [NST, TN], F32, kind="Internal").ap()
    winbf_sp = nc.dram_tensor("winbf_sp", [32, 128, NDK * 128], BF16, kind="Internal").ap()
    woutbf_sp = nc.dram_tensor("woutbf_sp", [NIK, 128, D], BF16, kind="Internal").ap()
    wqbf_sp = nc.dram_tensor("wqbf_sp", [NDK, 128, D], BF16, kind="Internal").ap()
    bounce = nc.dram_tensor("bounce", [1, 128], F32, kind="Internal").ap()
    cc_in = nc.dram_tensor("cc_in", [SUMR, 1024], F32, kind="Internal")
    cc_out = nc.dram_tensor("cc_out", [NCORES * SUMR, 1024], F32, kind="Internal")

    ctx = ExitStack()
    with ctx:
        sb = lambda name, shape, dt=F32: ctx.enter_context(nc.sbuf_tensor(name, shape, dt))

        aP = sb("aP", [128, NIK * NST])
        dP = sb("dP", [128, NIK])
        inb = sb("inb", [128, 32])
        inbm = sb("inbm", [128, 16])
        convb = sb("convb", [128, NIK])
        dtbn = sb("dtbn", [128, NIK])
        gamln = sb("gamln", [128, NDK])
        betln = sb("betln", [128, NDK])
        identf = sb("identf", [128, 128])
        identb = sb("identb", [128, 128], BF16)
        onesA = sb("onesA", [128, TA])
        boutb = sb("boutb", [128, D])
        maskf = sb("maskf_sb", [128, 16])
        metis = sb("metis", [1, 8], I32)
        convd = sb("convd_sb", [128, 8 * 1024], BF16)
        xprojTb = sb("xprojTb", [128, 2 * 768], BF16)
        dtTb = sb("dtTb", [DTR, DI], BF16)
        hstate = sb("hstate", [128, NIK * NST])
        hin = sb("hin", [128, NIK * NST])
        smtot = sb("smtot", [128, NIK])
        gmax = sb("gmax", [128, 1])
        gred = sb("gred", [128, 12])
        gscal = sb("gscal", [128, 4])
        summ = sb("summ", [128, 2 * NIK * NST + 16])
        beta2 = sb("beta2", [128, 2])
        stat = sb("stat", [128, 48])
        xnh = sb("xnh", [128, NDK * 4], BF16)
        stath = sb("stath", [3, 12])
        epsv = sb("epsv", [128, 2])
        ggrow = sb("ggrow", [1, 1024])
        gtmp = sb("gtmp", [1, 4])
        sgm = sb("sgm", [128, TA])
        xhalo = sb("xhalo", [3, 1024])

        AR = Arena(nc, ctx, 38400)

        p512 = [ctx.enter_context(nc.psum_tensor(f"p{i}", [128, 512], F32)) for i in range(8)]
        pb6 = p512[6].ap().bitcast(BF16)
        pb7 = p512[7].ap().bitcast(BF16)

        sLD = Sem(nc, ctx, "sLD")
        sSP = Sem(nc, ctx, "sSP")
        sT = Sem(nc, ctx, "sT")
        sV = Sem(nc, ctx, "sV")
        sS = Sem(nc, ctx, "sS")
        sCC = Sem(nc, ctx, "sCC")
        sOU = Sem(nc, ctx, "sOU")
        mk = lambda nm: Sem(nc, ctx, nm)
        pfsl = [mk("pf0"), mk("pf1")]
        pbsl = [mk("pb0"), mk("pb1")]
        xsl = [mk("x0"), mk("x1")]
        wofsl = [mk("w0"), mk("w1")]
        msl = [mk("m0"), mk("m1")]
        usl = [mk("u0"), mk("u1")]
        srsl = [mk("sr0"), mk("sr1")]
        xnsl = mk("xnsp")
        bcsl = mk("bcsp")
        hlsl = mk("halo")
        bbsl = [mk("bb0"), mk("bb1")]
        bmsl = [mk("bm0"), mk("bm1"), mk("bm2"), mk("bm3")]
        busl = [mk("bu0"), mk("bu1"), mk("bu2"), mk("bu3")]
        yevsl = [mk("ye0"), mk("ye1")]
        yldsl = mk("yld")
        ccst = mk("ccst")
        agld = mk("agld")
        misc = mk("miscs")
        cldsl = [mk("cl0"), mk("cl1")]
        cbsl2 = mk("ccb")
        x2sl = [mk("x20"), mk("x21")]
        xtsl = [mk("xt0"), mk("xt1")]
        cwsl = mk("cw")

        prog = {k: [] for k in ("sync", "vector", "scalar", "tensor", "gpsimd")}

        def em(eng, fn):
            prog[eng].append(fn)

        def dma(eng, sem, out, in_):
            def f(e, out=out, in_=in_, sem=sem):
                e.dma_start(out=out, in_=in_).then_inc(sem.h, 16)
            em(eng, f)
            sem.n += 16
            return sem.n

        def wait(eng, sem, val):
            if val <= 0:
                return
            def f(e, sem=sem, val=val):
                e.wait_ge(sem.h, val)
            em(eng, f)

        def op(eng, sem, fn):
            def f(e, fn=fn, sem=sem):
                fn(e).then_inc(sem.h, 1)
            em(eng, f)
            sem.n += 1
            return sem.n

        for dst, src in [
            (aP[:, :], alog_h[:, :]), (dP[:, :], dpar_h[:, :]),
            (inb[:, :], inb_h[:, :]), (convb[:, :], convb_h[:, :]),
            (dtbn[:, :], dtb_h[:, :]), (gamln[:, :], gamln_h[:, :]),
            (betln[:, :], betln_h[:, :]), (identf[:, :], ident_h[:, :]),
            (maskf[:, :], maskf_h[:, :]), (metis[:, :], meti_h[:, :]),
            (boutb[:, :], outb_h[0:1, :].partition_broadcast(128)),
        ]:
            dma("sync", sLD, dst, src)
        lc0 = sLD.n

        wait("vector", sLD, lc0)
        op("vector", sV, lambda v: v.memset(onesA[:, :], 1.0))
        op("vector", sV, lambda v: v.memset(hstate[:, :], 0.0))
        op("vector", sV, lambda v: v.memset(smtot[:, :], 0.0))
        op("vector", sV, lambda v: v.memset(gmax[:, :], 0.0))
        op("vector", sV, lambda v: v.tensor_copy(identb[:, :], identf[:, :]))
        op("vector", sV, lambda v: v.tensor_scalar(dtbn[:, :], dtbn[:, :], -1.0, None, ALU.mult))
        op("vector", sV, lambda v: v.tensor_scalar(inbm[:, :], inb[:, 0:16], maskf[:, 0:1], None, ALU.mult))
        op("vector", sV, lambda v: v.memset(epsv[:, 0:1], float(EPS1)))
        op("vector", sV, lambda v: v.memset(epsv[:, 1:2], float(EPS2)))
        pro_v0 = sV.n
        wait("scalar", sLD, lc0)
        op("scalar", sS, lambda s: s.activation(aP[:, :], aP[:, :], AT.Exp))

        AR.mark()
        pf = [AR.f32(2048), AR.f32(2048)]
        pb = [AR.bf16(2048), AR.bf16(2048)]

        jobs = []
        for of in range(32):
            jobs.append((winT_h[of, :, :], None, winbf_sp[of, :, :], 128, 1024))
        for ik in range(NIK):
            jobs.append((woutT_h[ik, :, :], None, woutbf_sp[ik, :, :], 128, 1024))
        for g in range(8):
            jobs.append((convd_h[g, :, :], convd[:, 1024 * g:1024 * (g + 1)], None, 128, 1024))
        for g in range(2):
            jobs.append((xprojT_h[g, :, :], xprojTb[:, 768 * g:768 * (g + 1)], None, 128, 768))
        jobs.append((dtT_h[:, :], dtTb[0:DTR, :], None, DTR, 2048))

        vdone = []
        stdone = []
        for j, (src, sdst, ddst, parts, ncol) in enumerate(jobs):
            if j >= 2:
                wait("sync", sV, vdone[j - 2])
            dma("sync", pfsl[j % 2], pf[j % 2][0:parts, 0:ncol], src)
            wait("vector", pfsl[j % 2], pfsl[j % 2].n)
            if j >= 2 and stdone[j - 2] > 0:
                wait("vector", pbsl[j % 2], stdone[j - 2])
            if sdst is not None:
                vdone.append(op("vector", sV, lambda v, a=sdst, b=pf[j % 2], p=parts, n=ncol:
                                v.tensor_copy(a, b[0:p, 0:n])))
                stdone.append(0)
            else:
                vdone.append(op("vector", sV, lambda v, a=pb[j % 2], b=pf[j % 2], p=parts, n=ncol:
                                v.tensor_copy(a[0:p, 0:n], b[0:p, 0:n])))
                wait("sync", sV, vdone[-1])
                stdone.append(dma("sync", pbsl[j % 2], ddst, pb[j % 2][0:parts, 0:ncol]))

        for g in range(NDK):
            if g >= 2:
                wait("sync", sV, sV.n)
            dma("sync", pfsl[g % 2], pf[g % 2][:, 0:1024], ffnw_h[g, :, :])
            wait("vector", pfsl[g % 2], pfsl[g % 2].n)
            op("vector", sV, lambda v, g=g: v.tensor_reduce(
                gred[:, g:g + 1], pf[g % 2][:, 0:1024], AX.X, ALU.add, apply_absolute_value=True))
        op("vector", sV, lambda v: v.tensor_copy(gred[:, 9:10], onesA[:, 0:1]))
        op("vector", sV, lambda v: v.tensor_copy(gred[:, 9:10], onesA[:, 0:1]))
        op("vector", sV, lambda v: v.tensor_reduce(
            gred[:, 8:9], gred[:, 0:8], AX.X, ALU.add))
        abs_v = sV.n
        wait("tensor", sV, abs_v)
        op("tensor", sT, lambda t: t.matmul(p512[7].ap()[0:1, 0:1], gred[:, 8:9], onesA[:, 0:1],
                                            start=True, stop=True))
        wait("scalar", sT, sT.n)
        op("scalar", sS, lambda s: s.activation(gtmp[0:1, 1:2], p512[7].ap()[0:1, 0:1], AT.Copy,
                                                scale=float(1.0 / (1024.0 * 1024.0))))
        wait("vector", sS, sS.n)
        op("vector", sV, lambda v: v.tensor_scalar(gtmp[0:1, 3:4], gtmp[0:1, 1:2], float(QEPS), None, ALU.add))
        op("vector", sV, lambda v: v.tensor_copy(gred[0:1, 11:12], onesA[0:1, 0:1]))
        op("vector", sV, lambda v: v.tensor_copy(gred[0:1, 11:12], onesA[0:1, 0:1]))
        op("vector", sV, lambda v: v.reciprocal(gtmp[0:1, 0:1], gtmp[0:1, 3:4]))
        wait("sync", sS, sS.n)
        wait("sync", sV, sV.n)
        dma("sync", misc, bounce[0:1, 0:2], gtmp[0:1, 0:2])
        wait("sync", misc, misc.n)
        dma("sync", misc, beta2[:, 0:2], bounce[0:1, 0:2].partition_broadcast(128))
        beta_ld = misc.n
        wsgn = AR.f32(1024)
        wabs = AR.f32(1024)
        for g in range(NDK):
            wait("sync", sV, sV.n)
            dma("sync", pfsl[g % 2], pf[g % 2][:, 0:1024], ffnw_h[g, :, :])
            wait("scalar", pfsl[g % 2], pfsl[g % 2].n)
            wait("scalar", misc, beta_ld)
            op("scalar", sS, lambda s, g=g: s.activation(wabs, pf[g % 2][:, 0:1024], AT.Abs,
                                                         scale=beta2[:, 0:1]))
            op("scalar", sS, lambda s, g=g: s.activation(wsgn, pf[g % 2][:, 0:1024], AT.Sign,
                                                         scale=beta2[:, 0:1]))
            wait("vector", sS, sS.n)
            op("vector", sV, lambda v: v.tensor_scalar(wabs, wabs, 0.5, None, ALU.is_gt))
            op("vector", sV, lambda v: v.tensor_tensor(wsgn, wsgn, wabs, ALU.mult))
            wq_v = op("vector", sV, lambda v, g=g: v.tensor_scalar(
                pb[g % 2][:, 0:1024], wsgn, beta2[:, 1:2], None, ALU.mult))
            wait("sync", sV, wq_v)
            dma("sync", pbsl[g % 2], wqbf_sp[g, :, :], pb[g % 2][:, 0:1024])
            wait("vector", pbsl[g % 2], pbsl[g % 2].n)
        AR.reset()

        AR.mark()
        xt = [[AR.f32(D) for _ in range(4)] for _ in range(2)]
        sqb = AR.f32(D)
        xnt = [[AR.bf16(D) for _ in range(4)] for _ in range(2)]
        xn2t = [[AR.bf16(D) for _ in range(4)] for _ in range(2)]
        xnf = [AR.bf16(TA) for _ in range(NDK)]
        xn2f = [AR.bf16(TA) for _ in range(NDK)]
        wof = [AR.bf16(NDK * 128) for _ in range(2)]
        xbe = [AR.bf16(TA + 4) for _ in range(NIK)]
        ufr = [AR.f32(TA) for _ in range(2)]
        ub = [AR.bf16(TA) for _ in range(NIK)]
        drb = AR.bf16(TA, parts=128)
        bcrow = AR.f32(TA)
        mrotA = [AR.f32(TA) for _ in range(2)]
        smrot = [AR.f32(TA) for _ in range(2)]
        srot = [AR.f32(TA) for _ in range(2)]

        wait("vector", sV, pro_v0)
        for ik in range(NIK):
            op("vector", sV, lambda v, ik=ik: v.memset(xbe[ik][:, 0:4], 0.0))
        dma("sync", hlsl, xhalo[0:3, :], x_h[0:3, :])
        halo_ld = hlsl.n
        wait("scalar", hlsl, halo_ld)
        wait("scalar", sV, pro_v0)
        op("scalar", sS, lambda s: s.activation(sqb[0:3, :], xhalo[0:3, :], AT.Square,
                                                accum_out=stath[:, 0:1]))
        sq_h = sS.n
        wait("vector", hlsl, halo_ld)
        op("vector", sV, lambda v: v.tensor_reduce(stath[:, 1:2], xhalo[0:3, :], AX.X, ALU.add))
        wait("vector", sS, sq_h)
        op("vector", sV, lambda v: v.tensor_scalar(stath[:, 2:3], stath[:, 1:2], float(1.0 / D), None, ALU.mult))
        op("vector", sV, lambda v: v.tensor_scalar(stath[:, 3:4], stath[:, 0:1], float(1.0 / D), None, ALU.mult))
        op("vector", sV, lambda v: v.tensor_scalar(stath[:, 6:7], stath[:, 2:3], -1.0, None, ALU.mult))
        op("vector", sV, lambda v: v.tensor_tensor(stath[:, 4:5], stath[:, 2:3], stath[:, 2:3], ALU.mult))
        op("vector", sV, lambda v: v.tensor_copy(stath[:, 9:10], stath[:, 0:1]))
        op("vector", sV, lambda v: v.tensor_copy(stath[:, 9:10], stath[:, 0:1]))
        op("vector", sV, lambda v: v.tensor_tensor(stath[:, 5:6], stath[:, 3:4], stath[:, 4:5], ALU.subtract))
        var_h = sV.n
        wait("scalar", sV, var_h)
        op("scalar", sS, lambda s: s.activation(stath[:, 8:9], stath[:, 5:6], AT.Sqrt, bias=epsv[0:3, 0:1]))
        r1_h = sS.n
        wait("vector", sS, r1_h)
        op("vector", sV, lambda v: v.reciprocal(stath[:, 7:8], stath[:, 8:9]))
        op("vector", sV, lambda v: v.tensor_scalar(
            xnt[0][0][0:3, :], xhalo[0:3, :], stath[:, 6:7], stath[:, 7:8], ALU.add, ALU.mult))
        hxn_v = sV.n
        wait("tensor", sV, hxn_v)
        for dk in range(NDK):
            op("tensor", sT, lambda t, dk=dk: t.transpose(
                pb6[:, 4 * dk:4 * dk + 3], xnt[0][0][0:3, 128 * dk:128 * dk + 128],
                identb[0:3, 0:3]))
        htr_t = sT.n
        wait("vector", sT, htr_t)
        for dk in range(NDK):
            op("vector", sV, lambda v, dk=dk: v.tensor_copy(
                xnh[:, 4 * dk:4 * dk + 3], pb6[:, 4 * dk:4 * dk + 3]))
        halo_xn_v = sV.n

        for j in range(4):
            dma("sync", xsl[0], xt[0][j][:, :], x_h[3 + 128 * j: 3 + 128 * (j + 1), :])
        xld = [xsl[0].n]
        xn_v_hist = [0, 0]
        trans_T_hist = [0, 0]

        evac_bank = {0: 0, 1: 0, 2: 0}
        wof_cons = [0, 0]
        m_spill = {}
        u_spill = {}
        sr_spill = {}
        conv_evac = {}
        dt_evac = {}
        halo_evac_S = 0
        xn2sp_prev = 0
        bc_spill_prev = 0
        xnf_freeT = 0
        carryV_prev = 0
        xp_freeV_prev = 0

        for i in range(NTILES):
            t0 = TA * i
            buf = i % 2
            if i + 1 < NTILES:
                if i >= 1:
                    wait("sync", sV, xn_v_hist[(i - 1) % 2])
                for j in range(4):
                    dma("sync", xsl[(i + 1) % 2], xt[1 - buf][j][:, :],
                        x_h[3 + TA * (i + 1) + 128 * j: 3 + TA * (i + 1) + 128 * (j + 1), :])
                xld.append(xsl[(i + 1) % 2].n)

            wait("scalar", xsl[i % 2], xld[i])
            for j in range(4):
                op("scalar", sS, lambda s, j=j, buf=buf: s.activation(
                    sqb[:, :], xt[buf][j][:, :], AT.Square, accum_out=stat[:, 12 * j:12 * j + 1]))
            ssq = sS.n
            wait("vector", xsl[i % 2], xld[i])
            for j in range(4):
                op("vector", sV, lambda v, j=j, buf=buf: v.tensor_reduce(
                    stat[:, 12 * j + 1:12 * j + 2], xt[buf][j][:, :], AX.X, ALU.add))
            wait("vector", sS, ssq)
            for j in range(4):
                op("vector", sV, lambda v, j=j: v.tensor_scalar(
                    stat[:, 12 * j + 2:12 * j + 3], stat[:, 12 * j + 1:12 * j + 2],
                    float(1.0 / D), None, ALU.mult))
            for j in range(4):
                op("vector", sV, lambda v, j=j: v.tensor_scalar(
                    stat[:, 12 * j + 3:12 * j + 4], stat[:, 12 * j + 0:12 * j + 1],
                    float(1.0 / D), None, ALU.mult))
            for j in range(4):
                op("vector", sV, lambda v, j=j: v.tensor_scalar(
                    stat[:, 12 * j + 6:12 * j + 7], stat[:, 12 * j + 2:12 * j + 3],
                    -1.0, None, ALU.mult))
            for j in range(4):
                op("vector", sV, lambda v, j=j: v.tensor_tensor(
                    stat[:, 12 * j + 4:12 * j + 5], stat[:, 12 * j + 2:12 * j + 3],
                    stat[:, 12 * j + 2:12 * j + 3], ALU.mult))
            for j in range(4):
                op("vector", sV, lambda v, j=j: v.tensor_tensor(
                    stat[:, 12 * j + 5:12 * j + 6], stat[:, 12 * j + 3:12 * j + 4],
                    stat[:, 12 * j + 4:12 * j + 5], ALU.subtract))
            var_v = sV.n
            wait("scalar", sV, var_v)
            for j in range(4):
                op("scalar", sS, lambda s, j=j: s.activation(
                    stat[:, 12 * j + 8:12 * j + 9], stat[:, 12 * j + 5:12 * j + 6],
                    AT.Sqrt, bias=epsv[:, 0:1]))
            r1_s = sS.n
            wait("vector", sS, r1_s)
            for j in range(4):
                op("vector", sV, lambda v, j=j: v.reciprocal(
                    stat[:, 12 * j + 7:12 * j + 8], stat[:, 12 * j + 8:12 * j + 9]))
            for j in range(4):
                op("vector", sV, lambda v, j=j: v.tensor_tensor(
                    stat[:, 12 * j + 8:12 * j + 9], stat[:, 12 * j + 7:12 * j + 8],
                    stat[:, 12 * j + 7:12 * j + 8], ALU.mult))
            for j in range(4):
                op("vector", sV, lambda v, j=j: v.tensor_tensor(
                    stat[:, 12 * j + 9:12 * j + 10], stat[:, 12 * j + 5:12 * j + 6],
                    stat[:, 12 * j + 8:12 * j + 9], ALU.mult))
            var2_v = sV.n
            wait("scalar", sV, var2_v)
            for j in range(4):
                op("scalar", sS, lambda s, j=j: s.activation(
                    stat[:, 12 * j + 9:12 * j + 10], stat[:, 12 * j + 9:12 * j + 10],
                    AT.Sqrt, bias=epsv[:, 1:2]))
            r2_s = sS.n
            wait("vector", sS, r2_s)
            for j in range(4):
                op("vector", sV, lambda v, j=j: v.reciprocal(
                    stat[:, 12 * j + 10:12 * j + 11], stat[:, 12 * j + 9:12 * j + 10]))
            for j in range(4):
                op("vector", sV, lambda v, j=j: v.tensor_tensor(
                    stat[:, 12 * j + 11:12 * j + 12], stat[:, 12 * j + 7:12 * j + 8],
                    stat[:, 12 * j + 10:12 * j + 11], ALU.mult))
            if i >= 2:
                wait("vector", sT, trans_T_hist[i % 2])
            for j in range(4):
                op("vector", sV, lambda v, j=j, buf=buf, i=i: v.tensor_scalar(
                    xnt[i % 2][j], xt[buf][j][:, :], stat[:, 12 * j + 6:12 * j + 7],
                    stat[:, 12 * j + 7:12 * j + 8], ALU.add, ALU.mult))
            for j in range(4):
                op("vector", sV, lambda v, j=j, buf=buf, i=i: v.tensor_scalar(
                    xn2t[i % 2][j], xt[buf][j][:, :], stat[:, 12 * j + 6:12 * j + 7],
                    stat[:, 12 * j + 11:12 * j + 12], ALU.add, ALU.mult))
            xn_v = sV.n
            xn_v_hist[i % 2] = xn_v

            wait("tensor", sV, xn_v)
            if i == 1:
                wait("tensor", sS, halo_evac_S)
            for dk in range(NDK):
                for j in range(4):
                    op("tensor", sT, lambda t, dk=dk, j=j, i=i: t.transpose(
                        pb6[:, 128 * j:128 * (j + 1)],
                        xnt[i % 2][j][:, 128 * dk:128 * (dk + 1)], identb[:, :]))
                trA = sT.n
                for j in range(4):
                    op("tensor", sT, lambda t, dk=dk, j=j, i=i: t.transpose(
                        pb7[:, 128 * j:128 * (j + 1)],
                        xn2t[i % 2][j][:, 128 * dk:128 * (dk + 1)], identb[:, :]))
                trB = sT.n
                wait("scalar", sT, trA)
                if i > 0:
                    wait("scalar", xnsl, xn2sp_prev)
                op("scalar", sS, lambda s, dk=dk: s.activation(
                    xnf[dk], pb6[:, 0:512], AT.Copy))
                wait("scalar", sT, trB)
                evB = op("scalar", sS, lambda s, dk=dk: s.activation(
                    xn2f[dk], pb7[:, 0:512], AT.Identity,
                    scale=gamln[:, dk:dk + 1], bias=betln[:, dk:dk + 1]))
                wait("tensor", sS, evB)
            trans_S = sS.n
            trans_T_hist[i % 2] = sT.n
            wait("sync", sS, trans_S)
            for dk in range(NDK):
                dma("sync", xnsl, xn2_sp[dk, :, t0:t0 + TA], xn2f[dk])
            xn2sp_prev = xnsl.n
            wait("vector", sS, trans_S)
            for dk in range(NDK):
                op("vector", sV, lambda v, dk=dk: v.tensor_reduce(
                    gred[:, dk:dk + 1], xn2f[dk], AX.X, ALU.max, apply_absolute_value=True))
            op("vector", sV, lambda v: v.tensor_copy(gred[:, 9:10], gmax[:, 0:1]))
            op("vector", sV, lambda v: v.tensor_copy(gred[:, 10:11], gmax[:, 0:1]))
            op("vector", sV, lambda v: v.tensor_reduce(
                gred[:, 8:9], gred[:, 0:8], AX.X, ALU.max))
            op("vector", sV, lambda v: v.tensor_copy(gred[:, 9:10], gmax[:, 0:1]))
            op("vector", sV, lambda v: v.tensor_copy(gred[:, 10:11], gmax[:, 0:1]))
            op("vector", sV, lambda v: v.tensor_tensor(gmax[:, :], gmax[:, :], gred[:, 8:9], ALU.max))

            for of in range(32):
                slot = of % 2
                wait("sync", sT, wof_cons[slot])
                dma("sync", wofsl[slot], wof[slot], winbf_sp[of, :, :])
                wld = wofsl[slot].n
                bank = of % 3
                wait("tensor", wofsl[slot], wld)
                wait("tensor", sS, evac_bank[bank])
                if i == 0 and of == 0:
                    wait("tensor", sV, halo_xn_v)
                for dk in range(NDK):
                    op("tensor", sT, lambda t, of=of, dk=dk, bank=bank, slot=slot: t.matmul(
                        p512[bank].ap()[:, 0:TA], wof[slot][:, 128 * dk:128 * (dk + 1)],
                        xnf[dk], start=(dk == 0), stop=(dk == NDK - 1)))
                if i == 0 and of < 16:
                    for dk in range(NDK):
                        op("tensor", sT, lambda t, of=of, dk=dk, slot=slot: t.matmul(
                            p512[6].ap()[:, 3 * of:3 * of + 3],
                            wof[slot][:, 128 * dk:128 * (dk + 1)],
                            xnh[:, 4 * dk:4 * dk + 3], start=(dk == 0), stop=(dk == NDK - 1)))
                mm_T = sT.n
                wof_cons[slot] = mm_T
                wait("scalar", sT, mm_T)
                if of < 16:
                    ev = op("scalar", sS, lambda s, of=of, bank=bank: s.activation(
                        xbe[of][:, 4:4 + TA], p512[bank].ap()[:, 0:TA], AT.Identity,
                        bias=inb[:, of:of + 1]))
                else:
                    gof = i * 16 + (of - 16)
                    if gof >= 2:
                        wait("scalar", srsl[gof % 2], sr_spill[gof - 2])
                    ev = op("scalar", sS, lambda s, of=of, bank=bank: s.activation(
                        srot[of % 2], p512[bank].ap()[:, 0:TA], AT.Silu,
                        bias=inb[:, of:of + 1]))
                    wait("sync", sS, ev)
                    sr_spill[gof] = dma("sync", srsl[gof % 2], sr_sp[of - 16, :, t0:t0 + TA], srot[of % 2])
                evac_bank[bank] = ev
            if i == 0:
                wait("scalar", sT, sT.n)
                for of in range(16):
                    op("scalar", sS, lambda s, of=of: s.activation(
                        xbe[of][:, 1:4], p512[6].ap()[:, 3 * of:3 * of + 3], AT.Identity,
                        scale=maskf[:, 0:1], bias=inbm[:, of:of + 1]))
                halo_evac_S = sS.n
                if DEBUG:
                    wait("sync", sS, halo_evac_S)
                    for of in range(16):
                        dma("sync", sOU, dbg["d_xbe"][of, :, 0:16], xbe[of][:, 0:16])
                    dbg_halo_tap = sOU.n
            xnf_freeT = sT.n

            wait("tensor", sS, sS.n)
            if i > 0:
                wait("tensor", sV, carryV_prev)
            for ik in range(NIK):
                bank = 3 + (ik % 2)
                gk = i * NIK + ik
                if gk >= 2:
                    wait("tensor", sS, conv_evac[gk - 2])
                for tap in range(4):
                    g = tap * 16 + ik
                    op("tensor", sT, lambda t, ik=ik, tap=tap, bank=bank, g=g: t.matmul(
                        p512[bank].ap()[:, 0:TA],
                        convd[:, 1024 * (g // 8) + 128 * (g % 8):1024 * (g // 8) + 128 * (g % 8) + 128],
                        xbe[ik][:, 1 + tap:1 + tap + TA],
                        start=(tap == 0), stop=(tap == 3)))
                cT = sT.n
                wait("vector", sT, cT)
                if DEBUG and i == 0 and ik == 0:
                    wait("vector", sOU, dbg_halo_tap)
                op("vector", sV, lambda v, ik=ik: v.tensor_copy(
                    xbe[ik][:, 1:4], xbe[ik][:, TA + 1:TA + 4]))
                wait("scalar", sT, cT)
                if gk >= 2:
                    wait("scalar", usl[gk % 2], u_spill[gk - 2])
                conv_evac[gk] = op("scalar", sS, lambda s, ik=ik, bank=bank: s.activation(
                    ufr[ik % 2], p512[bank].ap()[:, 0:TA], AT.Silu, bias=convb[:, ik:ik + 1]))
                wait("vector", sS, conv_evac[gk])
                op("vector", sV, lambda v, ik=ik: v.tensor_copy(ub[ik], ufr[ik % 2]))
                wait("sync", sS, conv_evac[gk])
                u_spill[gk] = dma("sync", usl[gk % 2], u_sp[ik, :, t0:t0 + TA], ufr[ik % 2])
            carryV_prev = sV.n
            conv_evS = sS.n

            wait("tensor", sV, sV.n)
            wait("tensor", sV, xp_freeV_prev)
            for ik in range(NIK):
                g, r = ik // 8, ik % 8
                op("tensor", sT, lambda t, ik=ik, g=g, r=r: t.matmul(
                    p512[5].ap()[0:96, 0:TA], xprojTb[:, 768 * g + 96 * r:768 * g + 96 * r + 96],
                    ub[ik], start=(ik == 0), stop=(ik == NIK - 1)))
            xp_T = sT.n
            wait("vector", sT, xp_T)
            if i > 0:
                wait("vector", bcsl, bc_spill_prev)
            op("vector", sV, lambda v: v.tensor_copy(drb[0:64, :], p512[5].ap()[0:64, 0:TA]))
            op("vector", sV, lambda v: v.tensor_scalar(
                bcrow[64:96, :], p512[5].ap()[64:96, 0:TA], maskf[64:96, 1:2], None, ALU.mult))
            bc_v = sV.n
            xp_freeV_prev = bc_v
            wait("sync", sV, bc_v)
            dma("sync", bcsl, b_sp[:, t0:t0 + TA], bcrow[64:80, :])
            bc_spill_prev = dma("sync", bcsl, c_sp[:, t0:t0 + TA], bcrow[80:96, :])

            for of in range(NIK):
                bank = 3 + (of % 2)
                gk = i * NIK + of
                if of < 2:
                    wait("tensor", sV, bc_v)
                    wait("tensor", sS, conv_evS)
                if gk >= 2:
                    wait("tensor", sS, dt_evac[gk - 2])
                op("tensor", sT, lambda t, of=of, bank=bank: t.matmul(
                    p512[bank].ap()[:, 0:TA], dtTb[0:DTR, 128 * of:128 * (of + 1)],
                    drb[0:64, :], start=True, stop=True))
                dt_T = sT.n
                wait("scalar", sT, dt_T)
                op("scalar", sS, lambda s, of=of, bank=bank: s.activation(
                    sgm[:, :], p512[bank].ap()[:, 0:TA], AT.Sigmoid, scale=-1.0,
                    bias=dtbn[:, of:of + 1]))
                dt_evac[gk] = sS.n
                m_s = op("scalar", sS, lambda s, of=of: s.activation(
                    mrotA[of % 2], sgm[:, :], AT.Ln))
                wait("vector", sS, m_s)
                if gk >= 2:
                    wait("vector", msl[gk % 2], m_spill[gk - 2])
                op("vector", sV, lambda v, of=of: v.tensor_tensor_scan(
                    smrot[of % 2], onesA[:, :], mrotA[of % 2], smtot[:, of:of + 1],
                    ALU.mult, ALU.add))
                if of >= 1:
                    op("vector", sV, lambda v, of=of: v.tensor_copy(
                        smtot[:, of - 1:of], smrot[(of - 1) % 2][:, TA - 1:TA]))
                if of == NIK - 1:
                    op("vector", sV, lambda v: v.tensor_copy(gred[:, 11:12], onesA[:, 0:1]))
                    op("vector", sV, lambda v: v.tensor_copy(gred[:, 11:12], onesA[:, 0:1]))
                    op("vector", sV, lambda v, of=of: v.tensor_copy(
                        smtot[:, of:of + 1], smrot[of % 2][:, TA - 1:TA]))
                smv = sV.n
                wait("sync", sV, smv)
                dma("sync", msl[gk % 2], m_sp[of, :, t0:t0 + TA], mrotA[of % 2])
                m_spill[gk] = dma("sync", msl[gk % 2], sm_sp[of, :, t0:t0 + TA], smrot[of % 2])

        a_S, a_V, a_T = sS.n, sV.n, sT.n
        a_spills = [(msl[0], msl[0].n), (msl[1], msl[1].n), (usl[0], usl[0].n),
                    (usl[1], usl[1].n), (srsl[0], srsl[0].n), (srsl[1], srsl[1].n),
                    (xnsl, xnsl.n), (bcsl, bcsl.n)]
        AR.reset()

        AR.mark()
        bbc = [AR.f32(TB) for _ in range(NST)]
        cbc = [AR.f32(TB) for _ in range(NST)]
        du2 = [AR.f32(TB) for _ in range(2)]
        mrotB = [AR.f32(TB) for _ in range(4)]
        uB = [AR.f32(TB) for _ in range(4)]
        dAb = [AR.f32(TB) for _ in range(16)]
        dBb = [AR.f32(TB) for _ in range(2)]
        hBb = [AR.f32(TB) for _ in range(8)]
        hcb = [AR.bf16(TB) for _ in range(8)]
        yev = [AR.f32(TB) for _ in range(2)]
        yld = AR.f32(TB)

        for sl, v in a_spills:
            wait("sync", sl, v)

        half_useV = {}
        mload = {}
        uload = {}
        duV = {}
        scanV = {}
        hc_T = {}
        ysp = {}
        exp_S = {}
        yfinV = {}
        key_by_idx = {}
        state_S = {}
        scan_svs = {}
        yev_guard = [0, 0]
        yld_guard = [0]

        for s in range(NSUB):
            t0 = TB * s
            for half in range(2):
                if s > 0:
                    wait("sync", sV, half_useV[(s - 1, half)])
                for n in range(8 * half, 8 * half + 8):
                    dma("sync", bbsl[half], bbc[n][:, :], b_sp[n:n + 1, t0:t0 + TB].partition_broadcast(128))
                    dma("sync", bbsl[half], cbc[n][:, :], c_sp[n:n + 1, t0:t0 + TB].partition_broadcast(128))
                bc_ld = bbsl[half].n

                for ik in range(NIK):
                    bank = ik % 2
                    key = (s, half, ik)
                    kidx = (s * 2 + half) * NIK + ik
                    pk4 = key_by_idx.get(kidx - 4)
                    if pk4 is not None:
                        wait("sync", sS, exp_S[pk4])
                        wait("sync", sV, duV[pk4])
                    msem = bmsl[kidx % 4]
                    usem = busl[kidx % 4]
                    mload[key] = dma("sync", msem, mrotB[kidx % 4][:, :], m_sp[ik, :, t0:t0 + TB])
                    uload[key] = dma("sync", usem, uB[kidx % 4][:, :], u_sp[ik, :, t0:t0 + TB])
                    key_by_idx[kidx] = key
                    wait("vector", usem, uload[key])
                    wait("vector", msem, mload[key])
                    duV[key] = op("vector", sV, lambda v, sl=kidx % 4, d=kidx % 2: v.tensor_tensor(
                        du2[d], mrotB[sl], uB[sl], ALU.mult))
                    wait("scalar", msem, mload[key])
                    pk2 = key_by_idx.get(kidx - 2)
                    if pk2 is not None:
                        wait("scalar", sV, scanV[pk2])
                    ab = 8 * (kidx % 2)
                    for nn in range(8):
                        n = 8 * half + nn
                        op("scalar", sS, lambda s_, ik=ik, n=n, nn=nn, ab=ab, sl=kidx % 4: s_.activation(
                            dAb[ab + nn], mrotB[sl], AT.Exp,
                            scale=aP[:, NST * ik + n:NST * ik + n + 1]))
                    exp_S[key] = sS.n
                    pk1 = key_by_idx.get(kidx - 1)
                    if pk1 is not None:
                        h_prev, ik_prev = pk1[1], pk1[2]
                        svsp = scan_svs[pk1]
                        for nn in range(8):
                            wait("scalar", sV,
                                 svsp[nn + 2] if nn < 6 else duV[key])
                            op("scalar", sS, lambda s_, ikp=ik_prev, n=8 * h_prev + nn, nn=nn: s_.copy(
                                hstate[:, NST * ikp + n:NST * ikp + n + 1],
                                hBb[nn][:, TB - 1:TB]))
                        state_S[pk1] = sS.n
                    wait("vector", sS, exp_S[key])
                    wait("vector", bbsl[half], bc_ld)
                    pk_hc = pk1
                    if pk_hc is not None and pk_hc in hc_T:
                        wait("vector", sT, hc_T[pk_hc])
                    if pk_hc is not None and pk_hc in state_S:
                        wait("vector", sS, state_S[pk_hc])
                    svs = []
                    for nn in range(8):
                        n = 8 * half + nn
                        op("vector", sV, lambda v, ik=ik, n=n, nn=nn, d=kidx % 2: v.tensor_tensor(
                            dBb[nn % 2], du2[d], bbc[n], ALU.mult))
                        op("vector", sV, lambda v, ik=ik, n=n, nn=nn, ab=ab: v.tensor_tensor_scan(
                            hBb[nn], dAb[ab + nn], dBb[nn % 2],
                            hstate[:, NST * ik + n:NST * ik + n + 1], ALU.mult, ALU.add))
                        op("vector", sV, lambda v, ik=ik, n=n, nn=nn: v.tensor_tensor(
                            hcb[nn], hBb[nn], cbc[n], ALU.mult))
                        svs.append(sV.n)
                    scanV[key] = sV.n
                    scan_svs[key] = svs
                    wait("tensor", sV, scanV[key])
                    if half == 0:
                        pk = (s - 1, ik)
                        if s > 0:
                            wait("tensor", sV, yfinV[pk])
                    for nn in range(8):
                        n = 8 * half + nn
                        op("tensor", sT, lambda t, ik=ik, n=n, nn=nn, bank=bank, half=half: t.matmul(
                            p512[bank].ap()[:, 0:TB], identb[:, :], hcb[nn],
                            start=(nn == 0), stop=(nn == 7)))
                    hc_T[key] = sT.n
                    if half == 0:
                        wait("scalar", sT, hc_T[key])
                        if yev_guard[ik % 2] > 0:
                            wait("scalar", yevsl[ik % 2], yev_guard[ik % 2])
                        pv = op("scalar", sS, lambda s_, ik=ik, bank=bank: s_.activation(
                            yev[ik % 2], p512[bank].ap()[:, 0:TB], AT.Copy))
                        wait("tensor", sS, pv)
                        wait("sync", sS, pv)
                        ysp[(s, 0, ik)] = dma("sync", yevsl[ik % 2], y_sp[ik, :, t0:t0 + TB], yev[ik % 2])
                        yev_guard[ik % 2] = ysp[(s, 0, ik)]
                    else:
                        wait("sync", yevsl[ik % 2], ysp[(s, 0, ik)])
                        if yld_guard[0] > 0:
                            wait("sync", sV, yld_guard[0])
                        yl = dma("sync", yldsl, yld[:, :], y_sp[ik, :, t0:t0 + TB])
                        wait("vector", sT, hc_T[key])
                        wait("vector", yldsl, yl)
                        if yev_guard[ik % 2] > 0:
                            wait("vector", yevsl[ik % 2], yev_guard[ik % 2])
                        yfinV[(s, ik)] = op("vector", sV, lambda v, ik=ik, bank=bank: v.scalar_tensor_tensor(
                            yev[ik % 2], p512[bank].ap()[:, 0:TB], 1.0, yld[:, :],
                            ALU.mult, ALU.add))
                        wait("tensor", sV, yfinV[(s, ik)])
                        wait("sync", sV, yfinV[(s, ik)])
                        ysp[(s, 1, ik)] = dma("sync", yevsl[ik % 2], y_sp[ik, :, t0:t0 + TB], yev[ik % 2])
                        yev_guard[ik % 2] = ysp[(s, 1, ik)]
                        yld_guard[0] = yfinV[(s, ik)]
                half_useV[(s, half)] = sV.n

        lk = key_by_idx[NSUB * 2 * NIK - 1]
        svsp = scan_svs[lk]
        for nn in range(8):
            wait("scalar", sV,
                 svsp[nn + 2] if nn < 6 else yfinV[(NSUB - 1, lk[2])])
            op("scalar", sS, lambda s_, ikp=lk[2], n=8 * lk[1] + nn, nn=nn: s_.copy(
                hstate[:, NST * ikp + n:NST * ikp + n + 1],
                hBb[nn][:, TB - 1:TB]))
        state_S[lk] = sS.n

        b_V, b_S, b_T = sV.n, sS.n, sT.n
        b_yspills = [(yevsl[0], yevsl[0].n), (yevsl[1], yevsl[1].n)]

        wait("scalar", sV, b_V)
        for ik in range(NIK):
            op("scalar", sS, lambda s, ik=ik: s.activation(
                summ[:, NST * ik:NST * ik + NST], aP[:, NST * ik:NST * ik + NST],
                AT.Exp, scale=smtot[:, ik:ik + 1]))
        wait("vector", sS, sS.n)
        op("vector", sV, lambda v: v.tensor_copy(summ[:, 256:512], hstate[:, :]))
        wait("tensor", sV, b_V)
        op("tensor", sT, lambda t: t.transpose(p512[5].ap()[0:1, 0:128], gmax[:, 0:1], identf[:, :]))
        wait("vector", sT, sT.n)
        op("vector", sV, lambda v: v.tensor_reduce(
            summ[0:1, 512:513], p512[5].ap()[0:1, 0:128], AX.X, ALU.max))
        summ_v = sV.n
        wait("sync", sV, summ_v)
        dma("sync", ccst, cc_in.ap()[0:32, :], summ[:, 0:256])
        dma("sync", ccst, cc_in.ap()[32:64, :], summ[:, 256:512])
        dma("sync", ccst, cc_in.ap()[64:65, 0:1], summ[0:1, 512:513])
        cc_stage = ccst.n
        wait("gpsimd", ccst, cc_stage)
        for sl, v in b_yspills:
            wait("gpsimd", sl, v)

        def cc_fn(g):
            g.collective_compute(
                "AllGather", ALU.bypass,
                replica_groups=[list(range(NCORES))],
                ins=[cc_in.ap()], outs=[cc_out.ap()],
            ).then_inc(sCC.h, 1)
        em("gpsimd", cc_fn)
        sCC.n += 1
        cc_done = sCC.n

        AR.reset()
        AR.mark()
        decp = [AR.f32(256) for _ in range(NCORES)]
        hfp = [AR.f32(256) for _ in range(NCORES)]
        Hp = [AR.f32(256) for _ in range(NCORES)]
        tkb = AR.f32(256)
        wait("sync", sCC, cc_done)
        for r in range(NCORES):
            dma("sync", agld, ggrow[0:1, r:r + 1],
                cc_out.ap()[SUMR * r + 64:SUMR * r + 65, 0:1])
        for p in range(NCORES):
            dma("sync", agld, decp[p], cc_out.ap()[SUMR * p:SUMR * p + 32, :])
            dma("sync", agld, hfp[p], cc_out.ap()[SUMR * p + 32:SUMR * p + 64, :])
        pred_ld = agld.n
        wait("vector", agld, pred_ld)
        op("vector", sV, lambda v: v.tensor_reduce(gtmp[0:1, 2:3], ggrow[0:1, 0:NCORES], AX.X, ALU.max))
        op("vector", sV, lambda v: v.tensor_copy(gred[0:1, 11:12], onesA[0:1, 0:1]))
        op("vector", sV, lambda v: v.tensor_copy(gred[0:1, 11:12], onesA[0:1, 0:1]))
        gm_v = sV.n
        op("vector", sV, lambda v: v.reciprocal(gtmp[0:1, 0:1], gtmp[0:1, 2:3]))
        wait("scalar", sV, gm_v)
        op("scalar", sS, lambda s: s.activation(gtmp[0:1, 1:2], gtmp[0:1, 2:3], AT.Copy,
                                                scale=float(1.0 / QB)))
        g_s = sS.n
        wait("sync", sS, g_s)
        wait("sync", sV, sV.n)
        dma("sync", misc, bounce[0:1, 0:2], gtmp[0:1, 0:2])
        wait("sync", misc, misc.n)
        dma("sync", misc, gscal[:, 0:2], bounce[0:1, 0:2].partition_broadcast(128))
        gscal_ld = misc.n
        wait("vector", agld, pred_ld)
        for p in range(NCORES):
            if p % 4 == 0:
                op("vector", sV, lambda v, p=p: v.tensor_copy(Hp[p], hfp[p]))
            else:
                op("vector", sV, lambda v, p=p: v.tensor_tensor(tkb, decp[p], Hp[p - 1], ALU.mult))
                op("vector", sV, lambda v, p=p: v.tensor_tensor(Hp[p], tkb, hfp[p], ALU.add))
        op("vector", sV, lambda v: v.memset(hin[:, :], 0.0))
        for p in range(NCORES):
            op("vector", sV, lambda v, p=p: v.scalar_tensor_tensor(
                hin[:, :], Hp[p], maskf[:, 8 + p:9 + p], hin[:, :], ALU.mult, ALU.add))
        hin_v = sV.n
        AR.reset()

        AR.mark()
        woutb = [AR.bf16(D) for _ in range(NIK)]
        wqb = [AR.bf16(D) for _ in range(NDK)]
        cbcC = [AR.f32(TC) for _ in range(NST)]
        ySc = [AR.f32(TC) for _ in range(2)]
        smC = [AR.f32(TC) for _ in range(2)]
        srwhy = [AR.f32(TC) for _ in range(2)]
        uCv = [AR.f32(TC) for _ in range(2)]
        eCv = [AR.f32(TC) for _ in range(4)]
        zCv = [AR.bf16(TC) for _ in range(4)]
        gbuf = [AR.bf16(TC) for _ in range(NIK)]
        xqs = [AR.bf16(TC) for _ in range(NDK)]
        xn2C = [AR.bf16(TC) for _ in range(2)]
        xtC = [AR.f32(D) for _ in range(2)]
        outCb = AR.f32(512)

        for ik in range(NIK):
            dma("sync", cwsl, woutb[ik], woutbf_sp[ik, :, :])
        for g in range(NDK):
            dma("sync", cwsl, wqb[g], wqbf_sp[g, :, :])
        cw_ld = cwsl.n

        cbc_useV = 0
        xn2_useV = {}
        exp_useS = {}
        g_V = {}
        ev_V = {}
        zV = {}
        zT = {}
        ev_grp = {}
        gbuf_freeT = 0

        for i in range(NCT):
            t0 = TC * i
            wait("sync", sV, cbc_useV)
            for n in range(NST):
                dma("sync", cbsl2, cbcC[n][:, :], c_sp[n:n + 1, t0:t0 + TC].partition_broadcast(128))
            cbc_ld = cbsl2.n
            wait("vector", misc, gscal_ld)
            wait("vector", sT, gbuf_freeT)
            for dk in range(NDK):
                if dk >= 2:
                    wait("sync", sV, xn2_useV[dk - 2])
                dma("sync", x2sl[dk % 2], xn2C[dk % 2], xn2_sp[dk, :, t0:t0 + TC])
                wait("vector", x2sl[dk % 2], x2sl[dk % 2].n)
                op("vector", sV, lambda v, dk=dk: v.tensor_scalar(
                    xqs[dk], xn2C[dk % 2], gscal[:, 0:1], None, ALU.mult))
                op("vector", sV, lambda v, dk=dk: v.tensor_scalar(
                    xqs[dk], xqs[dk], float(QB), float(CLIP_HI), ALU.mult, ALU.min))
                xn2_useV[dk] = op("vector", sV, lambda v, dk=dk: v.tensor_scalar(
                    xqs[dk], xqs[dk], float(CLIP_LO), gscal[:, 1:2], ALU.max, ALU.mult))
                if DEBUG:
                    wait("sync", sV, xn2_useV[dk])
                    dma("sync", sOU, dbg["d_xq"][dk, :, t0:t0 + TC], xqs[dk])
            xq_v = sV.n

            wait("scalar", sV, hin_v)
            wait("vector", sV, hin_v)
            for ik in range(NIK):
                bank = ik % 2
                gk = i * NIK + ik
                if gk >= 2:
                    wait("sync", sS, exp_useS[gk - 2])
                    wait("sync", sV, g_V[gk - 2])
                dma("sync", cldsl[ik % 2], smC[ik % 2], sm_sp[ik, :, t0:t0 + TC])
                dma("sync", cldsl[ik % 2], ySc[ik % 2], y_sp[ik, :, t0:t0 + TC])
                dma("sync", cldsl[ik % 2], srwhy[ik % 2], sr_sp[ik, :, t0:t0 + TC])
                dma("sync", cldsl[ik % 2], uCv[ik % 2], u_sp[ik, :, t0:t0 + TC])
                all_ld = cldsl[ik % 2].n
                wait("scalar", cldsl[ik % 2], all_ld)
                for n in range(NST):
                    fn = gk * NST + n
                    if fn >= 4:
                        wait("scalar", sV, zV[fn - 4])
                    op("scalar", sS, lambda s, ik=ik, n=n: s.activation(
                        eCv[n % 4], smC[ik % 2], AT.Exp,
                        scale=aP[:, NST * ik + n:NST * ik + n + 1]))
                    eS = sS.n
                    wait("vector", sS, eS)
                    wait("vector", cbsl2, cbc_ld)
                    if fn >= 4:
                        wait("vector", sT, zT[fn - 4])
                    zV[fn] = op("vector", sV, lambda v, ik=ik, n=n: v.scalar_tensor_tensor(
                        zCv[n % 4], eCv[n % 4], hin[:, NST * ik + n:NST * ik + n + 1],
                        cbcC[n], ALU.mult, ALU.mult))
                    wait("tensor", sV, zV[fn])
                    if n == 0 and (prev_ev := ev_V.get(gk - 2)) is not None:
                        wait("tensor", sV, prev_ev)
                    zT[fn] = op("tensor", sT, lambda t, ik=ik, n=n, bank=bank: t.matmul(
                        p512[bank].ap()[:, 0:TC], identb[:, :], zCv[n % 4],
                        start=(n == 0), stop=(n == NST - 1)))
                exp_useS[gk] = sS.n
                nsum_T = sT.n
                wait("vector", sT, nsum_T)
                wait("vector", cldsl[ik % 2], all_ld)
                if DEBUG:
                    dcv = op("vector", sV, lambda v, ik=ik, bank=bank: v.tensor_copy(
                        eCv[3], p512[bank].ap()[:, 0:TC]))
                    wait("sync", sV, dcv)
                    dma("sync", sOU, dbg["d_corr"][ik, :, t0:t0 + TC], eCv[3])
                    wait("scalar", sOU, sOU.n)
                op("vector", sV, lambda v, ik=ik, bank=bank: v.scalar_tensor_tensor(
                    ySc[ik % 2], p512[bank].ap()[:, 0:TC], 1.0, ySc[ik % 2], ALU.mult, ALU.add))
                op("vector", sV, lambda v, ik=ik: v.tensor_scalar(
                    uCv[ik % 2], uCv[ik % 2], dP[:, ik:ik + 1], None, ALU.mult))
                op("vector", sV, lambda v, ik=ik: v.tensor_tensor(
                    ySc[ik % 2], ySc[ik % 2], uCv[ik % 2], ALU.add))
                if DEBUG:
                    dyv = sV.n
                    wait("sync", sV, dyv)
                    dma("sync", sOU, dbg["d_yf"][ik, :, t0:t0 + TC], ySc[ik % 2])
                ev_V[gk] = op("vector", sV, lambda v, ik=ik: v.tensor_tensor(
                    gbuf[ik], ySc[ik % 2], srwhy[ik % 2], ALU.mult))
                g_V[gk] = ev_V[gk]
                if DEBUG:
                    wait("sync", sV, ev_V[gk])
                    dma("sync", sOU, dbg["d_g"][ik, :, t0:t0 + TC], gbuf[ik])
                    wait("vector", sOU, sOU.n)
            corr_v = sV.n
            cbc_useV = corr_v

            wait("tensor", sV, corr_v)
            wait("tensor", sV, xq_v)
            wait("tensor", cwsl, cw_ld)
            xld_c = {}
            for tc4 in range(4):
                gt = i * 4 + tc4
                if gt >= 2:
                    wait("sync", sV, ev_grp[(gt - 2) * 2 + 1])
                dma("sync", xtsl[tc4 % 2], xtC[tc4 % 2][:, :],
                    x_h[3 + t0 + 128 * tc4: 3 + t0 + 128 * (tc4 + 1), :])
                xld_c[tc4] = xtsl[tc4 % 2].n
                wait("vector", xtsl[tc4 % 2], xld_c[tc4])
                xpb_v = op("vector", sV, lambda v, tc4=tc4: v.tensor_tensor(
                    xtC[tc4 % 2][:, :], xtC[tc4 % 2][:, :], boutb[:, :], ALU.add))
                for hf in range(2):
                    gg = gt * 2 + hf
                    bank = 2 + (gg % 2)
                    if gg >= 2:
                        wait("tensor", sV, ev_grp[gg - 2])
                    for ik in range(NIK):
                        op("tensor", sT, lambda t, ik=ik, tc4=tc4, hf=hf, bank=bank: t.matmul(
                            p512[bank].ap()[:, 0:512],
                            gbuf[ik][:, 128 * tc4:128 * (tc4 + 1)],
                            woutb[ik][:, 512 * hf:512 * (hf + 1)],
                            start=(ik == 0), stop=False))
                    for g in range(NDK):
                        op("tensor", sT, lambda t, g=g, tc4=tc4, hf=hf, bank=bank: t.matmul(
                            p512[bank].ap()[:, 0:512],
                            xqs[g][:, 128 * tc4:128 * (tc4 + 1)],
                            wqb[g][:, 512 * hf:512 * (hf + 1)],
                            start=False, stop=(g == NDK - 1)))
                    grp_T = sT.n
                    wait("vector", sT, grp_T)
                    wait("vector", sV, xpb_v)
                    if gg >= 1:
                        wait("vector", sOU, sOU.n)
                    ev_grp[gg] = op("vector", sV, lambda v, tc4=tc4, hf=hf, bank=bank:
                                    v.scalar_tensor_tensor(
                        outCb[:, :], p512[bank].ap()[:, 0:512], 1.0,
                        xtC[tc4 % 2][:, 512 * hf:512 * (hf + 1)], ALU.mult, ALU.add))
                    wait("sync", sV, ev_grp[gg])
                    dma("sync", sOU, out_h[t0 + 128 * tc4: t0 + 128 * (tc4 + 1),
                                           512 * hf:512 * (hf + 1)], outCb[:, :])
            gbuf_freeT = sT.n
        AR.reset()

        if DEBUG:
            wait("sync", sOU, sOU.n)
            wait("sync", sSP, sSP.n)
            wait("sync", sV, sV.n)
            wait("sync", sS, sS.n)
            for nm, src in [("d_m", m_sp[:, :, :]), ("d_u", u_sp[:, :, :]),
                            ("d_sr", sr_sp[:, :, :]), ("d_sm", sm_sp[:, :, :]),
                            ("d_y", y_sp[:, :, :]), ("d_xn2", xn2_sp[:, :, :]),
                            ("d_b", b_sp[:, :]), ("d_c", c_sp[:, :]),
                            ("d_ccout", cc_out.ap()[:, :])]:
                dma("sync", sOU, dbg[nm][tuple([slice(None)] * len(dbg[nm].shape))], src)
            def _tapsb(e):
                e.dma_start(out=dbg["d_xnh"], in_=xnh[:, :]).then_inc(sOU.h, 16)
                e.dma_start(out=dbg["d_stath"], in_=stath[:, :]).then_inc(sOU.h, 16)
                e.dma_start(out=dbg["d_hin"], in_=hin[:, :]).then_inc(sOU.h, 16)
                e.dma_start(out=dbg["d_hst"], in_=hstate[:, :]).then_inc(sOU.h, 16)
                e.dma_start(out=dbg["d_summ"], in_=summ[:, 0:528]).then_inc(sOU.h, 16)
                e.dma_start(out=dbg["d_gsc"], in_=gscal[:, :]).then_inc(sOU.h, 16)
                e.dma_start(out=dbg["d_drb"], in_=drb[0:128, 0:512]).then_inc(sOU.h, 16)
            em("sync", _tapsb)
            sOU.n += 112
        wait("sync", sOU, sOU.n)

        import os as _os
        _frac = float(_os.environ.get("KFRAC", "1"))
        _feng = _os.environ.get("KENG", "")
        if _frac < 1:
            for _k in prog:
                if not _feng or _k == _feng:
                    prog[_k] = prog[_k][:int(len(prog[_k]) * _frac)]
        if _os.environ.get("KDUMP"):
            L = prog["sync"]
            lo, hi = int(len(L) * 0.745), int(len(L) * 0.765)
            for idx in range(lo, hi):
                f = L[idx]
                code = f.__code__
                print("sync idx", idx, code.co_filename, code.co_firstlineno,
                      {k: (v.cell_contents if hasattr(v, "cell_contents") else v)
                       for k, v in zip(code.co_freevars, f.__closure__ or [])} if False else "")
        blk = ctx.enter_context(nc.Block())

        @blk.sync
        def _(sync):
            for f in prog["sync"]:
                f(sync)

        @blk.vector
        def _(vector):
            for f in prog["vector"]:
                f(vector)

        @blk.scalar
        def _(scalar):
            for f in prog["scalar"]:
                f(scalar)

        @blk.tensor
        def _(tensor):
            for f in prog["tensor"]:
                f(tensor)

        @blk.gpsimd
        def _(gpsimd):
            for f in prog["gpsimd"]:
                f(gpsimd)

    return nc


def _prep_host(inputs):
    x = np.asarray(inputs["L0_attn_x"])
    inw = np.asarray(inputs["L0_attn_in_proj_w"])
    inbi = np.asarray(inputs["L0_attn_in_proj_b"])
    convw = np.asarray(inputs["L0_attn_conv_w"])
    convbi = np.asarray(inputs["L0_attn_conv_b"])
    xpw = np.asarray(inputs["L0_attn_x_proj_w"])
    dtw = np.asarray(inputs["L0_attn_dt_proj_w"])
    dtbi = np.asarray(inputs["L0_attn_dt_proj_b"])
    alog = np.asarray(inputs["L0_attn_A_log"])
    dpar = np.asarray(inputs["L0_attn_D_param"])
    outw = np.asarray(inputs["L0_attn_out_proj_w"])
    outbi = np.asarray(inputs["L0_attn_out_proj_b"])
    ffnw = np.asarray(inputs["L0_ffn_weight"])
    lng = np.asarray(inputs["L0_ffn_ln_gamma"])
    lnb = np.asarray(inputs["L0_ffn_ln_beta"])

    winT = np.ascontiguousarray(
        inw.reshape(32, 128, 8, 128).transpose(0, 3, 2, 1).reshape(32, 128, 8 * 128))
    woutT = np.ascontiguousarray(outw.T.reshape(NIK, 128, D))
    ffnwT = np.ascontiguousarray(ffnw.reshape(NDK, 128, D))
    xpT = np.ascontiguousarray(
        xpw.T.reshape(2, 8, 128, 96).transpose(0, 2, 1, 3).reshape(2, 128, 8 * 96))
    dtT = np.ascontiguousarray(dtw.T)
    convd = np.zeros((64, 128, 128), dtype=np.float32)
    for tap in range(4):
        for ik in range(NIK):
            np.fill_diagonal(convd[tap * 16 + ik], convw[ik * 128:(ik + 1) * 128, 0, tap])
    convd = np.ascontiguousarray(convd.reshape(8, 8, 128, 128).transpose(0, 2, 1, 3).reshape(8, 128, 1024))
    alogT = np.ascontiguousarray(alog.reshape(NIK, 128, NST).transpose(1, 0, 2).reshape(128, NIK * NST))
    dparT = np.ascontiguousarray(dpar.reshape(NIK, 128).T)
    inbT = np.ascontiguousarray(inbi.reshape(32, 128).T)
    convbT = np.ascontiguousarray(convbi.reshape(NIK, 128).T)
    dtbT = np.ascontiguousarray(dtbi.reshape(NIK, 128).T)
    gamlnT = np.ascontiguousarray(lng.reshape(NDK, 128).T)
    betlnT = np.ascontiguousarray(lnb.reshape(NDK, 128).T)
    ident = np.eye(128, dtype=np.float32)
    outb2 = np.ascontiguousarray(outbi.reshape(1, D))

    common = dict(winT=winT, woutT=woutT, ffnw=ffnwT, xprojT=xpT, dtT=dtT,
                  convd=convd, alogT=alogT, dparT=dparT, inbT=inbT,
                  convbT=convbT, dtbT=dtbT, outb=outb2, gamlnT=gamlnT,
                  betlnT=betlnT, identity=ident)

    in_maps = []
    for c in range(NCORES):
        seq, chunk = c // 4, c % 4
        t0 = chunk * TN
        xc = np.zeros((TN + 3, D), dtype=np.float32)
        if chunk == 0:
            xc[3:] = x[seq, 0:TN]
        else:
            xc[0:3] = x[seq, t0 - 3:t0]
            xc[3:] = x[seq, t0:t0 + TN]
        maskf = np.zeros((128, 16), dtype=np.float32)
        maskf[:, 0] = 0.0 if chunk == 0 else 1.0
        maskf[64:80, 1] = -1.0
        maskf[80:96, 1] = 1.0
        if chunk > 0:
            maskf[:, 8 + (c - 1)] = 1.0
        meti = np.zeros((1, 8), dtype=np.int32)
        for j in range(1, 4):
            p = (c - j) % NCORES
            meti[0, j - 1] = p * 3
            meti[0, 3 + j - 1] = p * 3 + 1
        m = dict(common)
        m["x"] = xc
        m["maskf"] = maskf
        m["meti"] = meti
        in_maps.append(m)
    return in_maps


def kernel(**inputs):
    in_maps = _prep_host(inputs)
    nc = build_nc()
    res = run_bass_kernel_spmd(nc, in_maps, core_ids=list(range(NCORES)))
    outs = [np.asarray(res.results[c]["out"], dtype=np.float32) for c in range(NCORES)]
    full = np.concatenate(outs, axis=0).reshape(2, 16384, 1024)
    if DEBUG:
        return full, res
    return full


# version 09
# speedup vs baseline: 1.1532x; 1.1532x over previous
import numpy as np
import ml_dtypes
from contextlib import ExitStack

import concourse.bass as bass
import concourse.mybir as mybir
from concourse.bass_utils import run_bass_kernel_spmd

F32 = mybir.dt.float32
BF16 = mybir.dt.bfloat16
I32 = mybir.dt.int32
AT = mybir.ActivationFunctionType
ALU = mybir.AluOpType
AX = mybir.AxisListType

NCORES = 8
D = 1024
DI = 2048
NST = 16
DTR = 64
TN = 4096
TA = 512
NTILES = TN // TA
NDK = D // 128
NIK = DI // 128
TB = 512
NSUB = TN // TB
TC = 512
NCT = TN // TC
EPS1 = 1e-5
EPS2 = 1e-3
QB = 128.0
QEPS = 1e-5
CLIP_HI = QB - QEPS
CLIP_LO = -QB + QEPS
SUMR = 96
BF = ml_dtypes.bfloat16
DEBUG = False


class Sem:
    def __init__(self, nc, ctx, name):
        self.h = ctx.enter_context(nc.semaphore(name))
        self.n = 0


class Arena:
    def __init__(self, nc, ctx, words):
        self.t = ctx.enter_context(nc.sbuf_tensor("ARENA", [128, words], F32))
        self.words = words
        self.off = 0
        self.marks = []

    def mark(self):
        self.marks.append(self.off)

    def reset(self):
        self.off = self.marks.pop()

    def f32(self, n, parts=128):
        o = self.off
        assert o + n <= self.words, f"arena overflow {o}+{n}>{self.words}"
        self.off += n
        return self.t.ap()[0:parts, o:o + n]

    def bf16(self, n, parts=128):
        nw = (n + 1) // 2
        o = self.off
        assert o + nw <= self.words, f"arena overflow {o}+{nw}>{self.words}"
        self.off += nw
        return self.t.ap().bitcast(BF16)[0:parts, 2 * o:2 * o + n]


def build_nc():
    nc = bass.Bass("TRN2", target_bir_lowering=False)

    x_h = nc.dram_tensor("x", [TN + 3, D], F32, kind="ExternalInput").ap()
    winT_h = nc.dram_tensor("winT", [32, 128, NDK * 128], F32, kind="ExternalInput").ap()
    woutT_h = nc.dram_tensor("woutT", [NIK, 128, D], F32, kind="ExternalInput").ap()
    ffnw_h = nc.dram_tensor("ffnw", [NDK, 128, D], F32, kind="ExternalInput").ap()
    xprojT_h = nc.dram_tensor("xprojT", [2, 128, 8 * 96], F32, kind="ExternalInput").ap()
    dtT_h = nc.dram_tensor("dtT", [DTR, DI], F32, kind="ExternalInput").ap()
    convd_h = nc.dram_tensor("convd", [8, 128, 1024], F32, kind="ExternalInput").ap()
    alog_h = nc.dram_tensor("alogT", [128, NIK * NST], F32, kind="ExternalInput").ap()
    dpar_h = nc.dram_tensor("dparT", [128, NIK], F32, kind="ExternalInput").ap()
    inb_h = nc.dram_tensor("inbT", [128, 32], F32, kind="ExternalInput").ap()
    convb_h = nc.dram_tensor("convbT", [128, NIK], F32, kind="ExternalInput").ap()
    dtb_h = nc.dram_tensor("dtbT", [128, NIK], F32, kind="ExternalInput").ap()
    outb_h = nc.dram_tensor("outb", [1, D], F32, kind="ExternalInput").ap()
    gamln_h = nc.dram_tensor("gamlnT", [128, NDK], F32, kind="ExternalInput").ap()
    betln_h = nc.dram_tensor("betlnT", [128, NDK], F32, kind="ExternalInput").ap()
    ident_h = nc.dram_tensor("identity", [128, 128], F32, kind="ExternalInput").ap()
    maskf_h = nc.dram_tensor("maskf", [128, 16], F32, kind="ExternalInput").ap()
    meti_h = nc.dram_tensor("meti", [1, 8], I32, kind="ExternalInput").ap()

    out_h = nc.dram_tensor("out", [TN, D], F32, kind="ExternalOutput").ap()
    dbg = {}
    if DEBUG:
        for nm, shp, dt in [("d_m", [NIK, 128, TN], F32), ("d_u", [NIK, 128, TN], F32),
                            ("d_sr", [NIK, 128, TN], F32), ("d_sm", [NIK, 128, TN], F32),
                            ("d_y", [NIK, 128, TN], F32), ("d_xn2", [NDK, 128, TN], BF16),
                            ("d_b", [NST, TN], F32), ("d_c", [NST, TN], F32),
                            ("d_hin", [128, NIK * NST], F32), ("d_hst", [128, NIK * NST], F32),
                            ("d_summ", [128, 528], F32), ("d_gsc", [128, 4], F32),
                            ("d_ccout", [NCORES * SUMR, 1024], F32),
                            ("d_xbe", [NIK, 128, 16], BF16), ("d_xnh", [128, NDK * 4], BF16),
                            ("d_stath", [3, 12], F32), ("d_drb", [128, 512], BF16),
                            ("d_yf", [NIK, 128, TN], F32), ("d_g", [NIK, 128, TN], BF16),
                            ("d_xq", [NDK, 128, TN], BF16), ("d_corr", [NIK, 128, TN], F32)]:
            dbg[nm] = nc.dram_tensor(nm, shp, dt, kind="ExternalOutput").ap()

    m_sp = nc.dram_tensor("m_sp", [NIK, 128, TN], F32, kind="Internal").ap()
    u_sp = nc.dram_tensor("u_sp", [NIK, 128, TN], F32, kind="Internal").ap()
    sr_sp = nc.dram_tensor("sr_sp", [NIK, 128, TN], F32, kind="Internal").ap()
    sm_sp = nc.dram_tensor("sm_sp", [NIK, 128, TN], F32, kind="Internal").ap()
    y_sp = nc.dram_tensor("y_sp", [NIK, 128, TN], F32, kind="Internal").ap()
    xn2_sp = nc.dram_tensor("xn2_sp", [NDK, 128, TN], BF16, kind="Internal").ap()
    b_sp = nc.dram_tensor("b_sp", [NST, TN], F32, kind="Internal").ap()
    c_sp = nc.dram_tensor("c_sp", [NST, TN], F32, kind="Internal").ap()
    winbf_sp = nc.dram_tensor("winbf_sp", [32, 128, NDK * 128], BF16, kind="Internal").ap()
    woutbf_sp = nc.dram_tensor("woutbf_sp", [NIK, 128, D], BF16, kind="Internal").ap()
    wqbf_sp = nc.dram_tensor("wqbf_sp", [NDK, 128, D], BF16, kind="Internal").ap()
    bounce = nc.dram_tensor("bounce", [1, 128], F32, kind="Internal").ap()
    cc_in = nc.dram_tensor("cc_in", [SUMR, 1024], F32, kind="Internal")
    cc_out = nc.dram_tensor("cc_out", [NCORES * SUMR, 1024], F32, kind="Internal")

    ctx = ExitStack()
    with ctx:
        sb = lambda name, shape, dt=F32: ctx.enter_context(nc.sbuf_tensor(name, shape, dt))

        aP = sb("aP", [128, NIK * NST])
        dP = sb("dP", [128, NIK])
        inb = sb("inb", [128, 32])
        inbm = sb("inbm", [128, 16])
        convb = sb("convb", [128, NIK])
        dtbn = sb("dtbn", [128, NIK])
        gamln = sb("gamln", [128, NDK])
        betln = sb("betln", [128, NDK])
        identf = sb("identf", [128, 128])
        identb = sb("identb", [128, 128], BF16)
        onesA = sb("onesA", [128, TA])
        boutb = sb("boutb", [128, D])
        maskf = sb("maskf_sb", [128, 16])
        metis = sb("metis", [1, 8], I32)
        convd = sb("convd_sb", [128, 8 * 1024], BF16)
        xprojTb = sb("xprojTb", [128, 2 * 768], BF16)
        dtTb = sb("dtTb", [DTR, DI], BF16)
        hstate = sb("hstate", [128, NIK * NST])
        hin = sb("hin", [128, NIK * NST])
        smtot = sb("smtot", [128, NIK])
        gmax = sb("gmax", [128, 1])
        gred = sb("gred", [128, 12])
        gscal = sb("gscal", [128, 4])
        summ = sb("summ", [128, 2 * NIK * NST + 16])
        beta2 = sb("beta2", [128, 2])
        stat = sb("stat", [128, 48])
        xnh = sb("xnh", [128, NDK * 4], BF16)
        stath = sb("stath", [3, 12])
        epsv = sb("epsv", [128, 2])
        ggrow = sb("ggrow", [1, 1024])
        gtmp = sb("gtmp", [1, 4])
        sgm = sb("sgm", [128, TA])
        xhalo = sb("xhalo", [3, 1024])

        AR = Arena(nc, ctx, 38400)

        p512 = [ctx.enter_context(nc.psum_tensor(f"p{i}", [128, 512], F32)) for i in range(8)]
        pb6 = p512[6].ap().bitcast(BF16)
        pb7 = p512[7].ap().bitcast(BF16)

        sLD = Sem(nc, ctx, "sLD")
        sSP = Sem(nc, ctx, "sSP")
        sT = Sem(nc, ctx, "sT")
        sV = Sem(nc, ctx, "sV")
        sS = Sem(nc, ctx, "sS")
        sCC = Sem(nc, ctx, "sCC")
        sOU = Sem(nc, ctx, "sOU")
        mk = lambda nm: Sem(nc, ctx, nm)
        pfsl = [mk("pf0"), mk("pf1")]
        pbsl = [mk("pb0"), mk("pb1")]
        xsl = [mk("x0"), mk("x1")]
        wofsl = [mk("w0"), mk("w1")]
        msl = [mk("m0"), mk("m1")]
        usl = [mk("u0"), mk("u1")]
        srsl = [mk("sr0"), mk("sr1")]
        xnsl = mk("xnsp")
        bcsl = mk("bcsp")
        hlsl = mk("halo")
        bbsl = [mk("bb0"), mk("bb1")]
        bmsl = [mk("bm0"), mk("bm1"), mk("bm2"), mk("bm3")]
        busl = [mk("bu0"), mk("bu1"), mk("bu2"), mk("bu3")]
        yevsl = [mk("ye0"), mk("ye1")]
        yldsl = mk("yld")
        ccst = mk("ccst")
        agld = mk("agld")
        misc = mk("miscs")
        cldsl = [mk("cl0"), mk("cl1")]
        cbsl2 = mk("ccb")
        x2sl = [mk("x20"), mk("x21")]
        xtsl = [mk("xt0"), mk("xt1")]
        cwsl = mk("cw")

        prog = {k: [] for k in ("sync", "vector", "scalar", "tensor", "gpsimd")}

        def em(eng, fn):
            prog[eng].append(fn)

        def dma(eng, sem, out, in_):
            def f(e, out=out, in_=in_, sem=sem):
                e.dma_start(out=out, in_=in_).then_inc(sem.h, 16)
            em(eng, f)
            sem.n += 16
            return sem.n

        def wait(eng, sem, val):
            if val <= 0:
                return
            def f(e, sem=sem, val=val):
                e.wait_ge(sem.h, val)
            em(eng, f)

        def op(eng, sem, fn):
            def f(e, fn=fn, sem=sem):
                fn(e).then_inc(sem.h, 1)
            em(eng, f)
            sem.n += 1
            return sem.n

        for dst, src in [
            (aP[:, :], alog_h[:, :]), (dP[:, :], dpar_h[:, :]),
            (inb[:, :], inb_h[:, :]), (convb[:, :], convb_h[:, :]),
            (dtbn[:, :], dtb_h[:, :]), (gamln[:, :], gamln_h[:, :]),
            (betln[:, :], betln_h[:, :]), (identf[:, :], ident_h[:, :]),
            (maskf[:, :], maskf_h[:, :]), (metis[:, :], meti_h[:, :]),
            (boutb[:, :], outb_h[0:1, :].partition_broadcast(128)),
        ]:
            dma("sync", sLD, dst, src)
        lc0 = sLD.n

        wait("vector", sLD, lc0)
        op("vector", sV, lambda v: v.memset(onesA[:, :], 1.0))
        op("vector", sV, lambda v: v.memset(hstate[:, :], 0.0))
        op("vector", sV, lambda v: v.memset(smtot[:, :], 0.0))
        op("vector", sV, lambda v: v.memset(gmax[:, :], 0.0))
        op("vector", sV, lambda v: v.tensor_copy(identb[:, :], identf[:, :]))
        op("vector", sV, lambda v: v.tensor_scalar(dtbn[:, :], dtbn[:, :], -1.0, None, ALU.mult))
        op("vector", sV, lambda v: v.tensor_scalar(inbm[:, :], inb[:, 0:16], maskf[:, 0:1], None, ALU.mult))
        op("vector", sV, lambda v: v.memset(epsv[:, 0:1], float(EPS1)))
        op("vector", sV, lambda v: v.memset(epsv[:, 1:2], float(EPS2)))
        pro_v0 = sV.n
        wait("scalar", sLD, lc0)
        op("scalar", sS, lambda s: s.activation(aP[:, :], aP[:, :], AT.Exp))

        AR.mark()
        pf = [AR.f32(2048), AR.f32(2048)]
        pb = [AR.bf16(2048), AR.bf16(2048)]

        jobs = []
        for of in range(32):
            jobs.append((winT_h[of, :, :], None, winbf_sp[of, :, :], 128, 1024))
        for ik in range(NIK):
            jobs.append((woutT_h[ik, :, :], None, woutbf_sp[ik, :, :], 128, 1024))
        for g in range(8):
            jobs.append((convd_h[g, :, :], convd[:, 1024 * g:1024 * (g + 1)], None, 128, 1024))
        for g in range(2):
            jobs.append((xprojT_h[g, :, :], xprojTb[:, 768 * g:768 * (g + 1)], None, 128, 768))
        jobs.append((dtT_h[:, :], dtTb[0:DTR, :], None, DTR, 2048))

        vdone = []
        stdone = []
        for j, (src, sdst, ddst, parts, ncol) in enumerate(jobs):
            if j >= 2:
                wait("sync", sV, vdone[j - 2])
            dma("sync", pfsl[j % 2], pf[j % 2][0:parts, 0:ncol], src)
            wait("vector", pfsl[j % 2], pfsl[j % 2].n)
            if j >= 2 and stdone[j - 2] > 0:
                wait("vector", pbsl[j % 2], stdone[j - 2])
            if sdst is not None:
                vdone.append(op("vector", sV, lambda v, a=sdst, b=pf[j % 2], p=parts, n=ncol:
                                v.tensor_copy(a, b[0:p, 0:n])))
                stdone.append(0)
            else:
                vdone.append(op("vector", sV, lambda v, a=pb[j % 2], b=pf[j % 2], p=parts, n=ncol:
                                v.tensor_copy(a[0:p, 0:n], b[0:p, 0:n])))
                wait("sync", sV, vdone[-1])
                stdone.append(dma("sync", pbsl[j % 2], ddst, pb[j % 2][0:parts, 0:ncol]))

        for g in range(NDK):
            if g >= 2:
                wait("sync", sV, sV.n)
            dma("sync", pfsl[g % 2], pf[g % 2][:, 0:1024], ffnw_h[g, :, :])
            wait("vector", pfsl[g % 2], pfsl[g % 2].n)
            op("vector", sV, lambda v, g=g: v.tensor_reduce(
                gred[:, g:g + 1], pf[g % 2][:, 0:1024], AX.X, ALU.add, apply_absolute_value=True))
        op("vector", sV, lambda v: v.tensor_copy(gred[:, 9:10], onesA[:, 0:1]))
        op("vector", sV, lambda v: v.tensor_copy(gred[:, 9:10], onesA[:, 0:1]))
        op("vector", sV, lambda v: v.tensor_reduce(
            gred[:, 8:9], gred[:, 0:8], AX.X, ALU.add))
        abs_v = sV.n
        wait("tensor", sV, abs_v)
        op("tensor", sT, lambda t: t.matmul(p512[7].ap()[0:1, 0:1], gred[:, 8:9], onesA[:, 0:1],
                                            start=True, stop=True))
        wait("scalar", sT, sT.n)
        op("scalar", sS, lambda s: s.activation(gtmp[0:1, 1:2], p512[7].ap()[0:1, 0:1], AT.Copy,
                                                scale=float(1.0 / (1024.0 * 1024.0))))
        wait("vector", sS, sS.n)
        op("vector", sV, lambda v: v.tensor_scalar(gtmp[0:1, 3:4], gtmp[0:1, 1:2], float(QEPS), None, ALU.add))
        op("vector", sV, lambda v: v.tensor_copy(gred[0:1, 11:12], onesA[0:1, 0:1]))
        op("vector", sV, lambda v: v.tensor_copy(gred[0:1, 11:12], onesA[0:1, 0:1]))
        op("vector", sV, lambda v: v.reciprocal(gtmp[0:1, 0:1], gtmp[0:1, 3:4]))
        wait("sync", sS, sS.n)
        wait("sync", sV, sV.n)
        dma("sync", misc, bounce[0:1, 0:2], gtmp[0:1, 0:2])
        wait("sync", misc, misc.n)
        dma("sync", misc, beta2[:, 0:2], bounce[0:1, 0:2].partition_broadcast(128))
        beta_ld = misc.n
        wsgn = AR.f32(1024)
        wabs = AR.f32(1024)
        for g in range(NDK):
            wait("sync", sV, sV.n)
            dma("sync", pfsl[g % 2], pf[g % 2][:, 0:1024], ffnw_h[g, :, :])
            wait("scalar", pfsl[g % 2], pfsl[g % 2].n)
            wait("scalar", misc, beta_ld)
            op("scalar", sS, lambda s, g=g: s.activation(wabs, pf[g % 2][:, 0:1024], AT.Abs,
                                                         scale=beta2[:, 0:1]))
            op("scalar", sS, lambda s, g=g: s.activation(wsgn, pf[g % 2][:, 0:1024], AT.Sign,
                                                         scale=beta2[:, 0:1]))
            wait("vector", sS, sS.n)
            op("vector", sV, lambda v: v.tensor_scalar(wabs, wabs, 0.5, None, ALU.is_gt))
            op("vector", sV, lambda v: v.tensor_tensor(wsgn, wsgn, wabs, ALU.mult))
            wq_v = op("vector", sV, lambda v, g=g: v.tensor_scalar(
                pb[g % 2][:, 0:1024], wsgn, beta2[:, 1:2], None, ALU.mult))
            wait("sync", sV, wq_v)
            dma("sync", pbsl[g % 2], wqbf_sp[g, :, :], pb[g % 2][:, 0:1024])
            wait("vector", pbsl[g % 2], pbsl[g % 2].n)
        AR.reset()

        AR.mark()
        xt = [[AR.f32(D) for _ in range(4)] for _ in range(2)]
        sqb = AR.f32(D)
        xnt = [[AR.bf16(D) for _ in range(4)] for _ in range(2)]
        xn2t = [[AR.bf16(D) for _ in range(4)] for _ in range(2)]
        xnf = [AR.bf16(TA) for _ in range(NDK)]
        xn2f = [AR.bf16(TA) for _ in range(NDK)]
        wof = [AR.bf16(NDK * 128) for _ in range(2)]
        xbe = [AR.bf16(TA + 4) for _ in range(NIK)]
        ufr = [AR.f32(TA) for _ in range(2)]
        ub = [AR.bf16(TA) for _ in range(NIK)]
        drb = AR.bf16(TA, parts=128)
        bcrow = AR.f32(TA)
        mrotA = [AR.f32(TA) for _ in range(2)]
        smrot = [AR.f32(TA) for _ in range(2)]
        srot = [AR.f32(TA) for _ in range(2)]

        wait("vector", sV, pro_v0)
        for ik in range(NIK):
            op("vector", sV, lambda v, ik=ik: v.memset(xbe[ik][:, 0:4], 0.0))
        dma("sync", hlsl, xhalo[0:3, :], x_h[0:3, :])
        halo_ld = hlsl.n
        wait("scalar", hlsl, halo_ld)
        wait("scalar", sV, pro_v0)
        op("scalar", sS, lambda s: s.activation(sqb[0:3, :], xhalo[0:3, :], AT.Square,
                                                accum_out=stath[:, 0:1]))
        sq_h = sS.n
        wait("vector", hlsl, halo_ld)
        op("vector", sV, lambda v: v.tensor_reduce(stath[:, 1:2], xhalo[0:3, :], AX.X, ALU.add))
        wait("vector", sS, sq_h)
        op("vector", sV, lambda v: v.tensor_scalar(stath[:, 2:3], stath[:, 1:2], float(1.0 / D), None, ALU.mult))
        op("vector", sV, lambda v: v.tensor_scalar(stath[:, 3:4], stath[:, 0:1], float(1.0 / D), None, ALU.mult))
        op("vector", sV, lambda v: v.tensor_scalar(stath[:, 6:7], stath[:, 2:3], -1.0, None, ALU.mult))
        op("vector", sV, lambda v: v.tensor_tensor(stath[:, 4:5], stath[:, 2:3], stath[:, 2:3], ALU.mult))
        op("vector", sV, lambda v: v.tensor_copy(stath[:, 9:10], stath[:, 0:1]))
        op("vector", sV, lambda v: v.tensor_copy(stath[:, 9:10], stath[:, 0:1]))
        op("vector", sV, lambda v: v.tensor_tensor(stath[:, 5:6], stath[:, 3:4], stath[:, 4:5], ALU.subtract))
        var_h = sV.n
        wait("scalar", sV, var_h)
        op("scalar", sS, lambda s: s.activation(stath[:, 8:9], stath[:, 5:6], AT.Sqrt, bias=epsv[0:3, 0:1]))
        r1_h = sS.n
        wait("vector", sS, r1_h)
        op("vector", sV, lambda v: v.reciprocal(stath[:, 7:8], stath[:, 8:9]))
        op("vector", sV, lambda v: v.tensor_scalar(
            xnt[0][0][0:3, :], xhalo[0:3, :], stath[:, 6:7], stath[:, 7:8], ALU.add, ALU.mult))
        hxn_v = sV.n
        wait("tensor", sV, hxn_v)
        for dk in range(NDK):
            op("tensor", sT, lambda t, dk=dk: t.transpose(
                pb6[:, 4 * dk:4 * dk + 3], xnt[0][0][0:3, 128 * dk:128 * dk + 128],
                identb[0:3, 0:3]))
        htr_t = sT.n
        wait("vector", sT, htr_t)
        for dk in range(NDK):
            op("vector", sV, lambda v, dk=dk: v.tensor_copy(
                xnh[:, 4 * dk:4 * dk + 3], pb6[:, 4 * dk:4 * dk + 3]))
        halo_xn_v = sV.n

        for j in range(4):
            dma("sync", xsl[0], xt[0][j][:, :], x_h[3 + 128 * j: 3 + 128 * (j + 1), :])
        xld = [xsl[0].n]
        xn_v_hist = [0, 0]
        trans_T_hist = [0, 0]

        evac_bank = {0: 0, 1: 0, 2: 0}
        wof_cons = [0, 0]
        m_spill = {}
        u_spill = {}
        sr_spill = {}
        conv_evac = {}
        dt_evac = {}
        halo_evac_S = 0
        xn2sp_prev = 0
        bc_spill_prev = 0
        xnf_freeT = 0
        carryV_prev = 0
        xp_freeV_prev = 0

        for i in range(NTILES):
            t0 = TA * i
            buf = i % 2
            if i + 1 < NTILES:
                if i >= 1:
                    wait("sync", sV, xn_v_hist[(i - 1) % 2])
                for j in range(4):
                    dma("sync", xsl[(i + 1) % 2], xt[1 - buf][j][:, :],
                        x_h[3 + TA * (i + 1) + 128 * j: 3 + TA * (i + 1) + 128 * (j + 1), :])
                xld.append(xsl[(i + 1) % 2].n)

            wait("scalar", xsl[i % 2], xld[i])
            for j in range(4):
                op("scalar", sS, lambda s, j=j, buf=buf: s.activation(
                    sqb[:, :], xt[buf][j][:, :], AT.Square, accum_out=stat[:, 12 * j:12 * j + 1]))
            ssq = sS.n
            wait("vector", xsl[i % 2], xld[i])
            for j in range(4):
                op("vector", sV, lambda v, j=j, buf=buf: v.tensor_reduce(
                    stat[:, 12 * j + 1:12 * j + 2], xt[buf][j][:, :], AX.X, ALU.add))
            wait("vector", sS, ssq)
            for j in range(4):
                op("vector", sV, lambda v, j=j: v.tensor_scalar(
                    stat[:, 12 * j + 2:12 * j + 3], stat[:, 12 * j + 1:12 * j + 2],
                    float(1.0 / D), None, ALU.mult))
            for j in range(4):
                op("vector", sV, lambda v, j=j: v.tensor_scalar(
                    stat[:, 12 * j + 3:12 * j + 4], stat[:, 12 * j + 0:12 * j + 1],
                    float(1.0 / D), None, ALU.mult))
            for j in range(4):
                op("vector", sV, lambda v, j=j: v.tensor_scalar(
                    stat[:, 12 * j + 6:12 * j + 7], stat[:, 12 * j + 2:12 * j + 3],
                    -1.0, None, ALU.mult))
            for j in range(4):
                op("vector", sV, lambda v, j=j: v.tensor_tensor(
                    stat[:, 12 * j + 4:12 * j + 5], stat[:, 12 * j + 2:12 * j + 3],
                    stat[:, 12 * j + 2:12 * j + 3], ALU.mult))
            for j in range(4):
                op("vector", sV, lambda v, j=j: v.tensor_tensor(
                    stat[:, 12 * j + 5:12 * j + 6], stat[:, 12 * j + 3:12 * j + 4],
                    stat[:, 12 * j + 4:12 * j + 5], ALU.subtract))
            var_v = sV.n
            wait("scalar", sV, var_v)
            for j in range(4):
                op("scalar", sS, lambda s, j=j: s.activation(
                    stat[:, 12 * j + 8:12 * j + 9], stat[:, 12 * j + 5:12 * j + 6],
                    AT.Sqrt, bias=epsv[:, 0:1]))
            r1_s = sS.n
            wait("vector", sS, r1_s)
            for j in range(4):
                op("vector", sV, lambda v, j=j: v.reciprocal(
                    stat[:, 12 * j + 7:12 * j + 8], stat[:, 12 * j + 8:12 * j + 9]))
            for j in range(4):
                op("vector", sV, lambda v, j=j: v.tensor_tensor(
                    stat[:, 12 * j + 8:12 * j + 9], stat[:, 12 * j + 7:12 * j + 8],
                    stat[:, 12 * j + 7:12 * j + 8], ALU.mult))
            for j in range(4):
                op("vector", sV, lambda v, j=j: v.tensor_tensor(
                    stat[:, 12 * j + 9:12 * j + 10], stat[:, 12 * j + 5:12 * j + 6],
                    stat[:, 12 * j + 8:12 * j + 9], ALU.mult))
            var2_v = sV.n
            wait("scalar", sV, var2_v)
            for j in range(4):
                op("scalar", sS, lambda s, j=j: s.activation(
                    stat[:, 12 * j + 9:12 * j + 10], stat[:, 12 * j + 9:12 * j + 10],
                    AT.Sqrt, bias=epsv[:, 1:2]))
            r2_s = sS.n
            wait("vector", sS, r2_s)
            for j in range(4):
                op("vector", sV, lambda v, j=j: v.reciprocal(
                    stat[:, 12 * j + 10:12 * j + 11], stat[:, 12 * j + 9:12 * j + 10]))
            for j in range(4):
                op("vector", sV, lambda v, j=j: v.tensor_tensor(
                    stat[:, 12 * j + 11:12 * j + 12], stat[:, 12 * j + 7:12 * j + 8],
                    stat[:, 12 * j + 10:12 * j + 11], ALU.mult))
            if i >= 2:
                wait("vector", sT, trans_T_hist[i % 2])
            for j in range(4):
                op("vector", sV, lambda v, j=j, buf=buf, i=i: v.tensor_scalar(
                    xnt[i % 2][j], xt[buf][j][:, :], stat[:, 12 * j + 6:12 * j + 7],
                    stat[:, 12 * j + 7:12 * j + 8], ALU.add, ALU.mult))
            for j in range(4):
                op("vector", sV, lambda v, j=j, buf=buf, i=i: v.tensor_scalar(
                    xn2t[i % 2][j], xt[buf][j][:, :], stat[:, 12 * j + 6:12 * j + 7],
                    stat[:, 12 * j + 11:12 * j + 12], ALU.add, ALU.mult))
            xn_v = sV.n
            xn_v_hist[i % 2] = xn_v

            wait("tensor", sV, xn_v)
            if i == 1:
                wait("tensor", sS, halo_evac_S)
            for dk in range(NDK):
                for j in range(4):
                    op("tensor", sT, lambda t, dk=dk, j=j, i=i: t.transpose(
                        pb6[:, 128 * j:128 * (j + 1)],
                        xnt[i % 2][j][:, 128 * dk:128 * (dk + 1)], identb[:, :]))
                trA = sT.n
                for j in range(4):
                    op("tensor", sT, lambda t, dk=dk, j=j, i=i: t.transpose(
                        pb7[:, 128 * j:128 * (j + 1)],
                        xn2t[i % 2][j][:, 128 * dk:128 * (dk + 1)], identb[:, :]))
                trB = sT.n
                wait("scalar", sT, trA)
                if i > 0:
                    wait("scalar", xnsl, xn2sp_prev)
                op("scalar", sS, lambda s, dk=dk: s.activation(
                    xnf[dk], pb6[:, 0:512], AT.Copy))
                wait("scalar", sT, trB)
                evB = op("scalar", sS, lambda s, dk=dk: s.activation(
                    xn2f[dk], pb7[:, 0:512], AT.Identity,
                    scale=gamln[:, dk:dk + 1], bias=betln[:, dk:dk + 1]))
                wait("tensor", sS, evB)
            trans_S = sS.n
            trans_T_hist[i % 2] = sT.n
            wait("sync", sS, trans_S)
            for dk in range(NDK):
                dma("sync", xnsl, xn2_sp[dk, :, t0:t0 + TA], xn2f[dk])
            xn2sp_prev = xnsl.n
            wait("vector", sS, trans_S)
            for dk in range(NDK):
                op("vector", sV, lambda v, dk=dk: v.tensor_reduce(
                    gred[:, dk:dk + 1], xn2f[dk], AX.X, ALU.max, apply_absolute_value=True))
            op("vector", sV, lambda v: v.tensor_copy(gred[:, 9:10], gmax[:, 0:1]))
            op("vector", sV, lambda v: v.tensor_copy(gred[:, 10:11], gmax[:, 0:1]))
            op("vector", sV, lambda v: v.tensor_reduce(
                gred[:, 8:9], gred[:, 0:8], AX.X, ALU.max))
            op("vector", sV, lambda v: v.tensor_copy(gred[:, 9:10], gmax[:, 0:1]))
            op("vector", sV, lambda v: v.tensor_copy(gred[:, 10:11], gmax[:, 0:1]))
            op("vector", sV, lambda v: v.tensor_tensor(gmax[:, :], gmax[:, :], gred[:, 8:9], ALU.max))

            for of in range(32):
                slot = of % 2
                wait("sync", sT, wof_cons[slot])
                dma("sync", wofsl[slot], wof[slot], winbf_sp[of, :, :])
                wld = wofsl[slot].n
                bank = of % 3
                wait("tensor", wofsl[slot], wld)
                wait("tensor", sS, evac_bank[bank])
                if i == 0 and of == 0:
                    wait("tensor", sV, halo_xn_v)
                for dk in range(NDK):
                    op("tensor", sT, lambda t, of=of, dk=dk, bank=bank, slot=slot: t.matmul(
                        p512[bank].ap()[:, 0:TA], wof[slot][:, 128 * dk:128 * (dk + 1)],
                        xnf[dk], start=(dk == 0), stop=(dk == NDK - 1)))
                if i == 0 and of < 16:
                    for dk in range(NDK):
                        op("tensor", sT, lambda t, of=of, dk=dk, slot=slot: t.matmul(
                            p512[6].ap()[:, 3 * of:3 * of + 3],
                            wof[slot][:, 128 * dk:128 * (dk + 1)],
                            xnh[:, 4 * dk:4 * dk + 3], start=(dk == 0), stop=(dk == NDK - 1)))
                mm_T = sT.n
                wof_cons[slot] = mm_T
                wait("scalar", sT, mm_T)
                if of < 16:
                    ev = op("scalar", sS, lambda s, of=of, bank=bank: s.activation(
                        xbe[of][:, 4:4 + TA], p512[bank].ap()[:, 0:TA], AT.Identity,
                        bias=inb[:, of:of + 1]))
                else:
                    gof = i * 16 + (of - 16)
                    if gof >= 2:
                        wait("scalar", srsl[gof % 2], sr_spill[gof - 2])
                    ev = op("scalar", sS, lambda s, of=of, bank=bank: s.activation(
                        srot[of % 2], p512[bank].ap()[:, 0:TA], AT.Silu,
                        bias=inb[:, of:of + 1]))
                    wait("sync", sS, ev)
                    sr_spill[gof] = dma("sync", srsl[gof % 2], sr_sp[of - 16, :, t0:t0 + TA], srot[of % 2])
                evac_bank[bank] = ev
            if i == 0:
                wait("scalar", sT, sT.n)
                for of in range(16):
                    op("scalar", sS, lambda s, of=of: s.activation(
                        xbe[of][:, 1:4], p512[6].ap()[:, 3 * of:3 * of + 3], AT.Identity,
                        scale=maskf[:, 0:1], bias=inbm[:, of:of + 1]))
                halo_evac_S = sS.n
                if DEBUG:
                    wait("sync", sS, halo_evac_S)
                    for of in range(16):
                        dma("sync", sOU, dbg["d_xbe"][of, :, 0:16], xbe[of][:, 0:16])
                    dbg_halo_tap = sOU.n
            xnf_freeT = sT.n

            wait("tensor", sS, sS.n)
            if i > 0:
                wait("tensor", sV, carryV_prev)
            for ik in range(NIK):
                bank = 3 + (ik % 2)
                gk = i * NIK + ik
                if gk >= 2:
                    wait("tensor", sS, conv_evac[gk - 2])
                for tap in range(4):
                    g = tap * 16 + ik
                    op("tensor", sT, lambda t, ik=ik, tap=tap, bank=bank, g=g: t.matmul(
                        p512[bank].ap()[:, 0:TA],
                        convd[:, 1024 * (g // 8) + 128 * (g % 8):1024 * (g // 8) + 128 * (g % 8) + 128],
                        xbe[ik][:, 1 + tap:1 + tap + TA],
                        start=(tap == 0), stop=(tap == 3)))
                cT = sT.n
                wait("vector", sT, cT)
                if DEBUG and i == 0 and ik == 0:
                    wait("vector", sOU, dbg_halo_tap)
                op("vector", sV, lambda v, ik=ik: v.tensor_copy(
                    xbe[ik][:, 1:4], xbe[ik][:, TA + 1:TA + 4]))
                wait("scalar", sT, cT)
                if gk >= 2:
                    wait("scalar", usl[gk % 2], u_spill[gk - 2])
                conv_evac[gk] = op("scalar", sS, lambda s, ik=ik, bank=bank: s.activation(
                    ufr[ik % 2], p512[bank].ap()[:, 0:TA], AT.Silu, bias=convb[:, ik:ik + 1]))
                wait("vector", sS, conv_evac[gk])
                op("vector", sV, lambda v, ik=ik: v.tensor_copy(ub[ik], ufr[ik % 2]))
                wait("sync", sS, conv_evac[gk])
                u_spill[gk] = dma("sync", usl[gk % 2], u_sp[ik, :, t0:t0 + TA], ufr[ik % 2])
            carryV_prev = sV.n
            conv_evS = sS.n

            wait("tensor", sV, sV.n)
            wait("tensor", sV, xp_freeV_prev)
            for ik in range(NIK):
                g, r = ik // 8, ik % 8
                op("tensor", sT, lambda t, ik=ik, g=g, r=r: t.matmul(
                    p512[5].ap()[0:96, 0:TA], xprojTb[:, 768 * g + 96 * r:768 * g + 96 * r + 96],
                    ub[ik], start=(ik == 0), stop=(ik == NIK - 1)))
            xp_T = sT.n
            wait("vector", sT, xp_T)
            if i > 0:
                wait("vector", bcsl, bc_spill_prev)
            op("vector", sV, lambda v: v.tensor_copy(drb[0:64, :], p512[5].ap()[0:64, 0:TA]))
            op("vector", sV, lambda v: v.tensor_scalar(
                bcrow[64:96, :], p512[5].ap()[64:96, 0:TA], maskf[64:96, 1:2], None, ALU.mult))
            bc_v = sV.n
            xp_freeV_prev = bc_v
            wait("gpsimd", sV, bc_v)
            dma("gpsimd", bcsl, b_sp[:, t0:t0 + TA], bcrow[64:80, :])
            bc_spill_prev = dma("gpsimd", bcsl, c_sp[:, t0:t0 + TA], bcrow[80:96, :])

            for of in range(NIK):
                bank = 3 + (of % 2)
                gk = i * NIK + of
                if of < 2:
                    wait("tensor", sV, bc_v)
                    wait("tensor", sS, conv_evS)
                if gk >= 2:
                    wait("tensor", sS, dt_evac[gk - 2])
                op("tensor", sT, lambda t, of=of, bank=bank: t.matmul(
                    p512[bank].ap()[:, 0:TA], dtTb[0:DTR, 128 * of:128 * (of + 1)],
                    drb[0:64, :], start=True, stop=True))
                dt_T = sT.n
                wait("scalar", sT, dt_T)
                op("scalar", sS, lambda s, of=of, bank=bank: s.activation(
                    sgm[:, :], p512[bank].ap()[:, 0:TA], AT.Sigmoid, scale=-1.0,
                    bias=dtbn[:, of:of + 1]))
                dt_evac[gk] = sS.n
                if gk >= 2:
                    wait("scalar", msl[gk % 2], m_spill[gk - 2])
                m_s = op("scalar", sS, lambda s, of=of: s.activation(
                    mrotA[of % 2], sgm[:, :], AT.Ln))
                wait("vector", sS, m_s)
                if gk >= 2:
                    wait("vector", msl[gk % 2], m_spill[gk - 2])
                op("vector", sV, lambda v, of=of: v.tensor_tensor_scan(
                    smrot[of % 2], onesA[:, :], mrotA[of % 2], smtot[:, of:of + 1],
                    ALU.mult, ALU.add))
                if of >= 1:
                    op("vector", sV, lambda v, of=of: v.tensor_copy(
                        smtot[:, of - 1:of], smrot[(of - 1) % 2][:, TA - 1:TA]))
                if of == NIK - 1:
                    op("vector", sV, lambda v: v.tensor_copy(gred[:, 11:12], onesA[:, 0:1]))
                    op("vector", sV, lambda v: v.tensor_copy(gred[:, 11:12], onesA[:, 0:1]))
                    op("vector", sV, lambda v, of=of: v.tensor_copy(
                        smtot[:, of:of + 1], smrot[of % 2][:, TA - 1:TA]))
                smv = sV.n
                wait("gpsimd", sV, smv)
                dma("gpsimd", msl[gk % 2], m_sp[of, :, t0:t0 + TA], mrotA[of % 2])
                m_spill[gk] = dma("gpsimd", msl[gk % 2], sm_sp[of, :, t0:t0 + TA], smrot[of % 2])

        a_S, a_V, a_T = sS.n, sV.n, sT.n
        a_spills = [(msl[0], msl[0].n), (msl[1], msl[1].n), (usl[0], usl[0].n),
                    (usl[1], usl[1].n), (srsl[0], srsl[0].n), (srsl[1], srsl[1].n),
                    (xnsl, xnsl.n), (bcsl, bcsl.n)]
        AR.reset()

        AR.mark()
        bbc = [AR.f32(TB) for _ in range(NST)]
        cbc = [AR.f32(TB) for _ in range(NST)]
        du2 = [AR.f32(TB) for _ in range(2)]
        mrotB = [AR.f32(TB) for _ in range(4)]
        uB = [AR.f32(TB) for _ in range(4)]
        dAb = [AR.f32(TB) for _ in range(16)]
        dBb = [AR.f32(TB) for _ in range(2)]
        hBb = [AR.f32(TB) for _ in range(8)]
        hcb = [AR.bf16(TB) for _ in range(8)]
        yev = [AR.f32(TB) for _ in range(2)]
        yld = AR.f32(TB)

        for sl, v in a_spills:
            wait("sync", sl, v)

        half_useV = {}
        mload = {}
        uload = {}
        duV = {}
        scanV = {}
        hc_T = {}
        ysp = {}
        exp_S = {}
        yfinV = {}
        key_by_idx = {}
        state_S = {}
        scan_svs = {}
        yev_guard = [0, 0]
        yld_guard = [0]

        wait("gpsimd", bcsl, bcsl.n)
        wait("scalar", usl[0], usl[0].n)
        wait("scalar", usl[1], usl[1].n)
        bc_pairs = [(s, h) for s in range(NSUB) for h in range(2)]
        bc_ld = {}

        def issue_bc(q):
            s2, h2 = bc_pairs[q]
            if s2 > 0:
                wait("gpsimd", sV, half_useV[(s2 - 1, h2)])
            for n in range(8 * h2, 8 * h2 + 8):
                dma("gpsimd", bbsl[h2], bbc[n][:, :],
                    b_sp[n:n + 1, TB * s2:TB * (s2 + 1)].partition_broadcast(128))
                dma("gpsimd", bbsl[h2], cbc[n][:, :],
                    c_sp[n:n + 1, TB * s2:TB * (s2 + 1)].partition_broadcast(128))
            bc_ld[bc_pairs[q]] = bbsl[h2].n

        issue_bc(0)
        issue_bc(1)
        for s in range(NSUB):
            t0 = TB * s
            for half in range(2):
                q = s * 2 + half
                for ik in range(NIK):
                    bank = ik % 2
                    key = (s, half, ik)
                    kidx = (s * 2 + half) * NIK + ik
                    pk4 = key_by_idx.get(kidx - 4)
                    if pk4 is not None:
                        wait("sync", sS, exp_S[pk4])
                        wait("sync", sV, duV[pk4])
                    msem = bmsl[kidx % 4]
                    usem = busl[kidx % 4]
                    mload[key] = dma("sync", msem, mrotB[kidx % 4][:, :], m_sp[ik, :, t0:t0 + TB])
                    if pk4 is not None:
                        wait("scalar", sV, duV[pk4])
                    uload[key] = dma("scalar", usem, uB[kidx % 4][:, :], u_sp[ik, :, t0:t0 + TB])
                    key_by_idx[kidx] = key
                    wait("vector", usem, uload[key])
                    wait("vector", msem, mload[key])
                    duV[key] = op("vector", sV, lambda v, sl=kidx % 4, d=kidx % 2: v.tensor_tensor(
                        du2[d], mrotB[sl], uB[sl], ALU.mult))
                    wait("scalar", msem, mload[key])
                    pk2 = key_by_idx.get(kidx - 2)
                    if pk2 is not None:
                        wait("scalar", sV, scanV[pk2])
                    ab = 8 * (kidx % 2)
                    for nn in range(8):
                        n = 8 * half + nn
                        op("scalar", sS, lambda s_, ik=ik, n=n, nn=nn, ab=ab, sl=kidx % 4: s_.activation(
                            dAb[ab + nn], mrotB[sl], AT.Exp,
                            scale=aP[:, NST * ik + n:NST * ik + n + 1]))
                    exp_S[key] = sS.n
                    pk1 = key_by_idx.get(kidx - 1)
                    if pk1 is not None:
                        h_prev, ik_prev = pk1[1], pk1[2]
                        svsp = scan_svs[pk1]
                        for nn in range(8):
                            wait("scalar", sV,
                                 svsp[nn + 2] if nn < 6 else duV[key])
                            op("scalar", sS, lambda s_, ikp=ik_prev, n=8 * h_prev + nn, nn=nn: s_.copy(
                                hstate[:, NST * ikp + n:NST * ikp + n + 1],
                                hBb[nn][:, TB - 1:TB]))
                        state_S[pk1] = sS.n
                    wait("vector", sS, exp_S[key])
                    wait("vector", bbsl[half], bc_ld[(s, half)])
                    pk_hc = pk1
                    if pk_hc is not None and pk_hc in hc_T:
                        wait("vector", sT, hc_T[pk_hc])
                    if pk_hc is not None and pk_hc in state_S:
                        wait("vector", sS, state_S[pk_hc])
                    svs = []
                    for nn in range(8):
                        n = 8 * half + nn
                        op("vector", sV, lambda v, ik=ik, n=n, nn=nn, d=kidx % 2: v.tensor_tensor(
                            dBb[nn % 2], du2[d], bbc[n], ALU.mult))
                        op("vector", sV, lambda v, ik=ik, n=n, nn=nn, ab=ab: v.tensor_tensor_scan(
                            hBb[nn], dAb[ab + nn], dBb[nn % 2],
                            hstate[:, NST * ik + n:NST * ik + n + 1], ALU.mult, ALU.add))
                        op("vector", sV, lambda v, ik=ik, n=n, nn=nn: v.tensor_tensor(
                            hcb[nn], hBb[nn], cbc[n], ALU.mult))
                        svs.append(sV.n)
                    scanV[key] = sV.n
                    scan_svs[key] = svs
                    if half == 1:
                        wait("gpsimd", yevsl[ik % 2], ysp[(s, 0, ik)])
                        if yld_guard[0] > 0:
                            wait("gpsimd", sV, yld_guard[0])
                        yl = dma("gpsimd", yldsl, yld[:, :], y_sp[ik, :, t0:t0 + TB])
                    wait("tensor", sV, scanV[key])
                    if half == 0:
                        pk = (s - 1, ik)
                        if s > 0:
                            wait("tensor", sV, yfinV[pk])
                    for nn in range(8):
                        n = 8 * half + nn
                        op("tensor", sT, lambda t, ik=ik, n=n, nn=nn, bank=bank, half=half: t.matmul(
                            p512[bank].ap()[:, 0:TB], identb[:, :], hcb[nn],
                            start=(nn == 0), stop=(nn == 7)))
                    hc_T[key] = sT.n
                    if half == 0:
                        wait("scalar", sT, hc_T[key])
                        if yev_guard[ik % 2] > 0:
                            wait("scalar", yevsl[ik % 2], yev_guard[ik % 2])
                        pv = op("scalar", sS, lambda s_, ik=ik, bank=bank: s_.activation(
                            yev[ik % 2], p512[bank].ap()[:, 0:TB], AT.Copy))
                        wait("tensor", sS, pv)
                        wait("gpsimd", sS, pv)
                        ysp[(s, 0, ik)] = dma("gpsimd", yevsl[ik % 2], y_sp[ik, :, t0:t0 + TB], yev[ik % 2])
                        yev_guard[ik % 2] = ysp[(s, 0, ik)]
                    else:
                        wait("vector", sT, hc_T[key])
                        wait("vector", yldsl, yl)
                        if yev_guard[ik % 2] > 0:
                            wait("vector", yevsl[ik % 2], yev_guard[ik % 2])
                        yfinV[(s, ik)] = op("vector", sV, lambda v, ik=ik, bank=bank: v.scalar_tensor_tensor(
                            yev[ik % 2], p512[bank].ap()[:, 0:TB], 1.0, yld[:, :],
                            ALU.mult, ALU.add))
                        wait("tensor", sV, yfinV[(s, ik)])
                        wait("gpsimd", sV, yfinV[(s, ik)])
                        ysp[(s, 1, ik)] = dma("gpsimd", yevsl[ik % 2], y_sp[ik, :, t0:t0 + TB], yev[ik % 2])
                        yev_guard[ik % 2] = ysp[(s, 1, ik)]
                        yld_guard[0] = yfinV[(s, ik)]
                half_useV[(s, half)] = sV.n
                if q + 2 < len(bc_pairs):
                    issue_bc(q + 2)

        lk = key_by_idx[NSUB * 2 * NIK - 1]
        svsp = scan_svs[lk]
        for nn in range(8):
            wait("scalar", sV,
                 svsp[nn + 2] if nn < 6 else yfinV[(NSUB - 1, lk[2])])
            op("scalar", sS, lambda s_, ikp=lk[2], n=8 * lk[1] + nn, nn=nn: s_.copy(
                hstate[:, NST * ikp + n:NST * ikp + n + 1],
                hBb[nn][:, TB - 1:TB]))
        state_S[lk] = sS.n

        b_V, b_S, b_T = sV.n, sS.n, sT.n
        b_yspills = [(yevsl[0], yevsl[0].n), (yevsl[1], yevsl[1].n)]

        wait("scalar", sV, b_V)
        for ik in range(NIK):
            op("scalar", sS, lambda s, ik=ik: s.activation(
                summ[:, NST * ik:NST * ik + NST], aP[:, NST * ik:NST * ik + NST],
                AT.Exp, scale=smtot[:, ik:ik + 1]))
        wait("vector", sS, sS.n)
        op("vector", sV, lambda v: v.tensor_copy(summ[:, 256:512], hstate[:, :]))
        wait("tensor", sV, b_V)
        op("tensor", sT, lambda t: t.transpose(p512[5].ap()[0:1, 0:128], gmax[:, 0:1], identf[:, :]))
        wait("vector", sT, sT.n)
        op("vector", sV, lambda v: v.tensor_reduce(
            summ[0:1, 512:513], p512[5].ap()[0:1, 0:128], AX.X, ALU.max))
        summ_v = sV.n
        wait("sync", sV, summ_v)
        dma("sync", ccst, cc_in.ap()[0:32, :], summ[:, 0:256])
        dma("sync", ccst, cc_in.ap()[32:64, :], summ[:, 256:512])
        dma("sync", ccst, cc_in.ap()[64:65, 0:1], summ[0:1, 512:513])
        cc_stage = ccst.n
        wait("gpsimd", ccst, cc_stage)
        for sl, v in b_yspills:
            wait("gpsimd", sl, v)

        def cc_fn(g):
            g.collective_compute(
                "AllGather", ALU.bypass,
                replica_groups=[list(range(NCORES))],
                ins=[cc_in.ap()], outs=[cc_out.ap()],
            ).then_inc(sCC.h, 1)
        em("gpsimd", cc_fn)
        sCC.n += 1
        cc_done = sCC.n

        AR.reset()
        AR.mark()
        decp = [AR.f32(256) for _ in range(NCORES)]
        hfp = [AR.f32(256) for _ in range(NCORES)]
        Hp = [AR.f32(256) for _ in range(NCORES)]
        tkb = AR.f32(256)
        wait("sync", sCC, cc_done)
        for r in range(NCORES):
            dma("sync", agld, ggrow[0:1, r:r + 1],
                cc_out.ap()[SUMR * r + 64:SUMR * r + 65, 0:1])
        for p in range(NCORES):
            dma("sync", agld, decp[p], cc_out.ap()[SUMR * p:SUMR * p + 32, :])
            dma("sync", agld, hfp[p], cc_out.ap()[SUMR * p + 32:SUMR * p + 64, :])
        pred_ld = agld.n
        wait("vector", agld, pred_ld)
        op("vector", sV, lambda v: v.tensor_reduce(gtmp[0:1, 2:3], ggrow[0:1, 0:NCORES], AX.X, ALU.max))
        op("vector", sV, lambda v: v.tensor_copy(gred[0:1, 11:12], onesA[0:1, 0:1]))
        op("vector", sV, lambda v: v.tensor_copy(gred[0:1, 11:12], onesA[0:1, 0:1]))
        gm_v = sV.n
        op("vector", sV, lambda v: v.reciprocal(gtmp[0:1, 0:1], gtmp[0:1, 2:3]))
        wait("scalar", sV, gm_v)
        op("scalar", sS, lambda s: s.activation(gtmp[0:1, 1:2], gtmp[0:1, 2:3], AT.Copy,
                                                scale=float(1.0 / QB)))
        g_s = sS.n
        wait("sync", sS, g_s)
        wait("sync", sV, sV.n)
        dma("sync", misc, bounce[0:1, 0:2], gtmp[0:1, 0:2])
        wait("sync", misc, misc.n)
        dma("sync", misc, gscal[:, 0:2], bounce[0:1, 0:2].partition_broadcast(128))
        gscal_ld = misc.n
        wait("vector", agld, pred_ld)
        for p in range(NCORES):
            if p % 4 == 0:
                op("vector", sV, lambda v, p=p: v.tensor_copy(Hp[p], hfp[p]))
            else:
                op("vector", sV, lambda v, p=p: v.tensor_tensor(tkb, decp[p], Hp[p - 1], ALU.mult))
                op("vector", sV, lambda v, p=p: v.tensor_tensor(Hp[p], tkb, hfp[p], ALU.add))
        op("vector", sV, lambda v: v.memset(hin[:, :], 0.0))
        for p in range(NCORES):
            op("vector", sV, lambda v, p=p: v.scalar_tensor_tensor(
                hin[:, :], Hp[p], maskf[:, 8 + p:9 + p], hin[:, :], ALU.mult, ALU.add))
        hin_v = sV.n
        AR.reset()

        AR.mark()
        woutb = [AR.bf16(D) for _ in range(NIK)]
        wqb = [AR.bf16(D) for _ in range(NDK)]
        cbcC = [AR.f32(TC) for _ in range(NST)]
        ySc = [AR.f32(TC) for _ in range(2)]
        smC = [AR.f32(TC) for _ in range(2)]
        srwhy = [AR.f32(TC) for _ in range(2)]
        uCv = [AR.f32(TC) for _ in range(2)]
        eCv = [AR.f32(TC) for _ in range(4)]
        zCv = [AR.bf16(TC) for _ in range(4)]
        gbuf = [AR.bf16(TC) for _ in range(NIK)]
        xqs = [AR.bf16(TC) for _ in range(NDK)]
        xn2C = [AR.bf16(TC) for _ in range(2)]
        xtC = [AR.f32(D) for _ in range(2)]
        outCb = AR.f32(512)

        for ik in range(NIK):
            dma("sync", cwsl, woutb[ik], woutbf_sp[ik, :, :])
        for g in range(NDK):
            dma("sync", cwsl, wqb[g], wqbf_sp[g, :, :])
        cw_ld = cwsl.n

        cbc_useV = 0
        xn2_useV = {}
        exp_useS = {}
        g_V = {}
        ev_V = {}
        zV = {}
        zT = {}
        ev_grp = {}
        gbuf_freeT = 0

        for i in range(NCT):
            t0 = TC * i
            wait("sync", sV, cbc_useV)
            for n in range(NST):
                dma("sync", cbsl2, cbcC[n][:, :], c_sp[n:n + 1, t0:t0 + TC].partition_broadcast(128))
            cbc_ld = cbsl2.n
            wait("vector", misc, gscal_ld)
            wait("vector", sT, gbuf_freeT)
            for dk in range(NDK):
                if dk >= 2:
                    wait("sync", sV, xn2_useV[dk - 2])
                dma("sync", x2sl[dk % 2], xn2C[dk % 2], xn2_sp[dk, :, t0:t0 + TC])
                wait("vector", x2sl[dk % 2], x2sl[dk % 2].n)
                op("vector", sV, lambda v, dk=dk: v.tensor_scalar(
                    xqs[dk], xn2C[dk % 2], gscal[:, 0:1], None, ALU.mult))
                op("vector", sV, lambda v, dk=dk: v.tensor_scalar(
                    xqs[dk], xqs[dk], float(QB), float(CLIP_HI), ALU.mult, ALU.min))
                xn2_useV[dk] = op("vector", sV, lambda v, dk=dk: v.tensor_scalar(
                    xqs[dk], xqs[dk], float(CLIP_LO), gscal[:, 1:2], ALU.max, ALU.mult))
                if DEBUG:
                    wait("sync", sV, xn2_useV[dk])
                    dma("sync", sOU, dbg["d_xq"][dk, :, t0:t0 + TC], xqs[dk])
            xq_v = sV.n

            wait("scalar", sV, hin_v)
            wait("vector", sV, hin_v)
            for ik in range(NIK):
                bank = ik % 2
                gk = i * NIK + ik
                if gk >= 2:
                    wait("sync", sS, exp_useS[gk - 2])
                    wait("sync", sV, g_V[gk - 2])
                dma("sync", cldsl[ik % 2], smC[ik % 2], sm_sp[ik, :, t0:t0 + TC])
                dma("sync", cldsl[ik % 2], ySc[ik % 2], y_sp[ik, :, t0:t0 + TC])
                dma("sync", cldsl[ik % 2], srwhy[ik % 2], sr_sp[ik, :, t0:t0 + TC])
                dma("sync", cldsl[ik % 2], uCv[ik % 2], u_sp[ik, :, t0:t0 + TC])
                all_ld = cldsl[ik % 2].n
                wait("scalar", cldsl[ik % 2], all_ld)
                for n in range(NST):
                    fn = gk * NST + n
                    if fn >= 4:
                        wait("scalar", sV, zV[fn - 4])
                    op("scalar", sS, lambda s, ik=ik, n=n: s.activation(
                        eCv[n % 4], smC[ik % 2], AT.Exp,
                        scale=aP[:, NST * ik + n:NST * ik + n + 1]))
                    eS = sS.n
                    wait("vector", sS, eS)
                    wait("vector", cbsl2, cbc_ld)
                    if fn >= 4:
                        wait("vector", sT, zT[fn - 4])
                    zV[fn] = op("vector", sV, lambda v, ik=ik, n=n: v.scalar_tensor_tensor(
                        zCv[n % 4], eCv[n % 4], hin[:, NST * ik + n:NST * ik + n + 1],
                        cbcC[n], ALU.mult, ALU.mult))
                    wait("tensor", sV, zV[fn])
                    if n == 0 and (prev_ev := ev_V.get(gk - 2)) is not None:
                        wait("tensor", sV, prev_ev)
                    zT[fn] = op("tensor", sT, lambda t, ik=ik, n=n, bank=bank: t.matmul(
                        p512[bank].ap()[:, 0:TC], identb[:, :], zCv[n % 4],
                        start=(n == 0), stop=(n == NST - 1)))
                exp_useS[gk] = sS.n
                nsum_T = sT.n
                wait("vector", sT, nsum_T)
                wait("vector", cldsl[ik % 2], all_ld)
                if DEBUG:
                    dcv = op("vector", sV, lambda v, ik=ik, bank=bank: v.tensor_copy(
                        eCv[3], p512[bank].ap()[:, 0:TC]))
                    wait("sync", sV, dcv)
                    dma("sync", sOU, dbg["d_corr"][ik, :, t0:t0 + TC], eCv[3])
                    wait("scalar", sOU, sOU.n)
                op("vector", sV, lambda v, ik=ik, bank=bank: v.scalar_tensor_tensor(
                    ySc[ik % 2], p512[bank].ap()[:, 0:TC], 1.0, ySc[ik % 2], ALU.mult, ALU.add))
                op("vector", sV, lambda v, ik=ik: v.tensor_scalar(
                    uCv[ik % 2], uCv[ik % 2], dP[:, ik:ik + 1], None, ALU.mult))
                op("vector", sV, lambda v, ik=ik: v.tensor_tensor(
                    ySc[ik % 2], ySc[ik % 2], uCv[ik % 2], ALU.add))
                if DEBUG:
                    dyv = sV.n
                    wait("sync", sV, dyv)
                    dma("sync", sOU, dbg["d_yf"][ik, :, t0:t0 + TC], ySc[ik % 2])
                ev_V[gk] = op("vector", sV, lambda v, ik=ik: v.tensor_tensor(
                    gbuf[ik], ySc[ik % 2], srwhy[ik % 2], ALU.mult))
                g_V[gk] = ev_V[gk]
                if DEBUG:
                    wait("sync", sV, ev_V[gk])
                    dma("sync", sOU, dbg["d_g"][ik, :, t0:t0 + TC], gbuf[ik])
                    wait("vector", sOU, sOU.n)
            corr_v = sV.n
            cbc_useV = corr_v

            wait("tensor", sV, corr_v)
            wait("tensor", sV, xq_v)
            wait("tensor", cwsl, cw_ld)
            xld_c = {}
            for tc4 in range(4):
                gt = i * 4 + tc4
                if gt >= 2:
                    wait("sync", sV, ev_grp[(gt - 2) * 2 + 1])
                dma("sync", xtsl[tc4 % 2], xtC[tc4 % 2][:, :],
                    x_h[3 + t0 + 128 * tc4: 3 + t0 + 128 * (tc4 + 1), :])
                xld_c[tc4] = xtsl[tc4 % 2].n
                wait("vector", xtsl[tc4 % 2], xld_c[tc4])
                xpb_v = op("vector", sV, lambda v, tc4=tc4: v.tensor_tensor(
                    xtC[tc4 % 2][:, :], xtC[tc4 % 2][:, :], boutb[:, :], ALU.add))
                for hf in range(2):
                    gg = gt * 2 + hf
                    bank = 2 + (gg % 2)
                    if gg >= 2:
                        wait("tensor", sV, ev_grp[gg - 2])
                    for ik in range(NIK):
                        op("tensor", sT, lambda t, ik=ik, tc4=tc4, hf=hf, bank=bank: t.matmul(
                            p512[bank].ap()[:, 0:512],
                            gbuf[ik][:, 128 * tc4:128 * (tc4 + 1)],
                            woutb[ik][:, 512 * hf:512 * (hf + 1)],
                            start=(ik == 0), stop=False))
                    for g in range(NDK):
                        op("tensor", sT, lambda t, g=g, tc4=tc4, hf=hf, bank=bank: t.matmul(
                            p512[bank].ap()[:, 0:512],
                            xqs[g][:, 128 * tc4:128 * (tc4 + 1)],
                            wqb[g][:, 512 * hf:512 * (hf + 1)],
                            start=False, stop=(g == NDK - 1)))
                    grp_T = sT.n
                    wait("vector", sT, grp_T)
                    wait("vector", sV, xpb_v)
                    if gg >= 1:
                        wait("vector", sOU, sOU.n)
                    ev_grp[gg] = op("vector", sV, lambda v, tc4=tc4, hf=hf, bank=bank:
                                    v.scalar_tensor_tensor(
                        outCb[:, :], p512[bank].ap()[:, 0:512], 1.0,
                        xtC[tc4 % 2][:, 512 * hf:512 * (hf + 1)], ALU.mult, ALU.add))
                    wait("sync", sV, ev_grp[gg])
                    dma("sync", sOU, out_h[t0 + 128 * tc4: t0 + 128 * (tc4 + 1),
                                           512 * hf:512 * (hf + 1)], outCb[:, :])
            gbuf_freeT = sT.n
        AR.reset()

        if DEBUG:
            wait("sync", sOU, sOU.n)
            wait("sync", sSP, sSP.n)
            wait("sync", sV, sV.n)
            wait("sync", sS, sS.n)
            for nm, src in [("d_m", m_sp[:, :, :]), ("d_u", u_sp[:, :, :]),
                            ("d_sr", sr_sp[:, :, :]), ("d_sm", sm_sp[:, :, :]),
                            ("d_y", y_sp[:, :, :]), ("d_xn2", xn2_sp[:, :, :]),
                            ("d_b", b_sp[:, :]), ("d_c", c_sp[:, :]),
                            ("d_ccout", cc_out.ap()[:, :])]:
                dma("sync", sOU, dbg[nm][tuple([slice(None)] * len(dbg[nm].shape))], src)
            def _tapsb(e):
                e.dma_start(out=dbg["d_xnh"], in_=xnh[:, :]).then_inc(sOU.h, 16)
                e.dma_start(out=dbg["d_stath"], in_=stath[:, :]).then_inc(sOU.h, 16)
                e.dma_start(out=dbg["d_hin"], in_=hin[:, :]).then_inc(sOU.h, 16)
                e.dma_start(out=dbg["d_hst"], in_=hstate[:, :]).then_inc(sOU.h, 16)
                e.dma_start(out=dbg["d_summ"], in_=summ[:, 0:528]).then_inc(sOU.h, 16)
                e.dma_start(out=dbg["d_gsc"], in_=gscal[:, :]).then_inc(sOU.h, 16)
                e.dma_start(out=dbg["d_drb"], in_=drb[0:128, 0:512]).then_inc(sOU.h, 16)
            em("sync", _tapsb)
            sOU.n += 112
        wait("sync", sOU, sOU.n)

        import os as _os
        _frac = float(_os.environ.get("KFRAC", "1"))
        _feng = _os.environ.get("KENG", "")
        if _frac < 1:
            for _k in prog:
                if not _feng or _k == _feng:
                    prog[_k] = prog[_k][:int(len(prog[_k]) * _frac)]
        if _os.environ.get("KDUMP"):
            L = prog["sync"]
            lo, hi = int(len(L) * 0.745), int(len(L) * 0.765)
            for idx in range(lo, hi):
                f = L[idx]
                code = f.__code__
                print("sync idx", idx, code.co_filename, code.co_firstlineno,
                      {k: (v.cell_contents if hasattr(v, "cell_contents") else v)
                       for k, v in zip(code.co_freevars, f.__closure__ or [])} if False else "")
        blk = ctx.enter_context(nc.Block())

        @blk.sync
        def _(sync):
            for f in prog["sync"]:
                f(sync)

        @blk.vector
        def _(vector):
            for f in prog["vector"]:
                f(vector)

        @blk.scalar
        def _(scalar):
            for f in prog["scalar"]:
                f(scalar)

        @blk.tensor
        def _(tensor):
            for f in prog["tensor"]:
                f(tensor)

        @blk.gpsimd
        def _(gpsimd):
            for f in prog["gpsimd"]:
                f(gpsimd)

    return nc


def _prep_host(inputs):
    x = np.asarray(inputs["L0_attn_x"])
    inw = np.asarray(inputs["L0_attn_in_proj_w"])
    inbi = np.asarray(inputs["L0_attn_in_proj_b"])
    convw = np.asarray(inputs["L0_attn_conv_w"])
    convbi = np.asarray(inputs["L0_attn_conv_b"])
    xpw = np.asarray(inputs["L0_attn_x_proj_w"])
    dtw = np.asarray(inputs["L0_attn_dt_proj_w"])
    dtbi = np.asarray(inputs["L0_attn_dt_proj_b"])
    alog = np.asarray(inputs["L0_attn_A_log"])
    dpar = np.asarray(inputs["L0_attn_D_param"])
    outw = np.asarray(inputs["L0_attn_out_proj_w"])
    outbi = np.asarray(inputs["L0_attn_out_proj_b"])
    ffnw = np.asarray(inputs["L0_ffn_weight"])
    lng = np.asarray(inputs["L0_ffn_ln_gamma"])
    lnb = np.asarray(inputs["L0_ffn_ln_beta"])

    winT = np.ascontiguousarray(
        inw.reshape(32, 128, 8, 128).transpose(0, 3, 2, 1).reshape(32, 128, 8 * 128))
    woutT = np.ascontiguousarray(outw.T.reshape(NIK, 128, D))
    ffnwT = np.ascontiguousarray(ffnw.reshape(NDK, 128, D))
    xpT = np.ascontiguousarray(
        xpw.T.reshape(2, 8, 128, 96).transpose(0, 2, 1, 3).reshape(2, 128, 8 * 96))
    dtT = np.ascontiguousarray(dtw.T)
    convd = np.zeros((64, 128, 128), dtype=np.float32)
    for tap in range(4):
        for ik in range(NIK):
            np.fill_diagonal(convd[tap * 16 + ik], convw[ik * 128:(ik + 1) * 128, 0, tap])
    convd = np.ascontiguousarray(convd.reshape(8, 8, 128, 128).transpose(0, 2, 1, 3).reshape(8, 128, 1024))
    alogT = np.ascontiguousarray(alog.reshape(NIK, 128, NST).transpose(1, 0, 2).reshape(128, NIK * NST))
    dparT = np.ascontiguousarray(dpar.reshape(NIK, 128).T)
    inbT = np.ascontiguousarray(inbi.reshape(32, 128).T)
    convbT = np.ascontiguousarray(convbi.reshape(NIK, 128).T)
    dtbT = np.ascontiguousarray(dtbi.reshape(NIK, 128).T)
    gamlnT = np.ascontiguousarray(lng.reshape(NDK, 128).T)
    betlnT = np.ascontiguousarray(lnb.reshape(NDK, 128).T)
    ident = np.eye(128, dtype=np.float32)
    outb2 = np.ascontiguousarray(outbi.reshape(1, D))

    common = dict(winT=winT, woutT=woutT, ffnw=ffnwT, xprojT=xpT, dtT=dtT,
                  convd=convd, alogT=alogT, dparT=dparT, inbT=inbT,
                  convbT=convbT, dtbT=dtbT, outb=outb2, gamlnT=gamlnT,
                  betlnT=betlnT, identity=ident)

    in_maps = []
    for c in range(NCORES):
        seq, chunk = c // 4, c % 4
        t0 = chunk * TN
        xc = np.zeros((TN + 3, D), dtype=np.float32)
        if chunk == 0:
            xc[3:] = x[seq, 0:TN]
        else:
            xc[0:3] = x[seq, t0 - 3:t0]
            xc[3:] = x[seq, t0:t0 + TN]
        maskf = np.zeros((128, 16), dtype=np.float32)
        maskf[:, 0] = 0.0 if chunk == 0 else 1.0
        maskf[64:80, 1] = -1.0
        maskf[80:96, 1] = 1.0
        if chunk > 0:
            maskf[:, 8 + (c - 1)] = 1.0
        meti = np.zeros((1, 8), dtype=np.int32)
        for j in range(1, 4):
            p = (c - j) % NCORES
            meti[0, j - 1] = p * 3
            meti[0, 3 + j - 1] = p * 3 + 1
        m = dict(common)
        m["x"] = xc
        m["maskf"] = maskf
        m["meti"] = meti
        in_maps.append(m)
    return in_maps


def kernel(**inputs):
    in_maps = _prep_host(inputs)
    nc = build_nc()
    res = run_bass_kernel_spmd(nc, in_maps, core_ids=list(range(NCORES)))
    outs = [np.asarray(res.results[c]["out"], dtype=np.float32) for c in range(NCORES)]
    full = np.concatenate(outs, axis=0).reshape(2, 16384, 1024)
    if DEBUG:
        return full, res
    return full


# version 10
# speedup vs baseline: 1.1968x; 1.0378x over previous
import numpy as np
import ml_dtypes
from contextlib import ExitStack

import concourse.bass as bass
import concourse.mybir as mybir
from concourse.bass_utils import run_bass_kernel_spmd

F32 = mybir.dt.float32
BF16 = mybir.dt.bfloat16
I32 = mybir.dt.int32
AT = mybir.ActivationFunctionType
ALU = mybir.AluOpType
AX = mybir.AxisListType

NCORES = 8
D = 1024
DI = 2048
NST = 16
DTR = 64
TN = 4096
TA = 512
NTILES = TN // TA
NDK = D // 128
NIK = DI // 128
TB = 512
NSUB = TN // TB
TC = 512
NCT = TN // TC
EPS1 = 1e-5
EPS2 = 1e-3
QB = 128.0
QEPS = 1e-5
CLIP_HI = QB - QEPS
CLIP_LO = -QB + QEPS
SUMR = 96
BF = ml_dtypes.bfloat16
DEBUG = False


class Sem:
    def __init__(self, nc, ctx, name):
        self.h = ctx.enter_context(nc.semaphore(name))
        self.n = 0


class Arena:
    def __init__(self, nc, ctx, words):
        self.t = ctx.enter_context(nc.sbuf_tensor("ARENA", [128, words], F32))
        self.words = words
        self.off = 0
        self.marks = []

    def mark(self):
        self.marks.append(self.off)

    def reset(self):
        self.off = self.marks.pop()

    def f32(self, n, parts=128):
        o = self.off
        assert o + n <= self.words, f"arena overflow {o}+{n}>{self.words}"
        self.off += n
        return self.t.ap()[0:parts, o:o + n]

    def bf16(self, n, parts=128):
        nw = (n + 1) // 2
        o = self.off
        assert o + nw <= self.words, f"arena overflow {o}+{nw}>{self.words}"
        self.off += nw
        return self.t.ap().bitcast(BF16)[0:parts, 2 * o:2 * o + n]


def build_nc():
    nc = bass.Bass("TRN2", target_bir_lowering=False)

    x_h = nc.dram_tensor("x", [TN + 3, D], F32, kind="ExternalInput").ap()
    winT_h = nc.dram_tensor("winT", [32, 128, NDK * 128], F32, kind="ExternalInput").ap()
    woutT_h = nc.dram_tensor("woutT", [NIK, 128, D], F32, kind="ExternalInput").ap()
    ffnw_h = nc.dram_tensor("ffnw", [NDK, 128, D], F32, kind="ExternalInput").ap()
    xprojT_h = nc.dram_tensor("xprojT", [2, 128, 8 * 96], F32, kind="ExternalInput").ap()
    dtT_h = nc.dram_tensor("dtT", [DTR, DI], F32, kind="ExternalInput").ap()
    convd_h = nc.dram_tensor("convd", [8, 128, 1024], F32, kind="ExternalInput").ap()
    alog_h = nc.dram_tensor("alogT", [128, NIK * NST], F32, kind="ExternalInput").ap()
    dpar_h = nc.dram_tensor("dparT", [128, NIK], F32, kind="ExternalInput").ap()
    inb_h = nc.dram_tensor("inbT", [128, 32], F32, kind="ExternalInput").ap()
    convb_h = nc.dram_tensor("convbT", [128, NIK], F32, kind="ExternalInput").ap()
    dtb_h = nc.dram_tensor("dtbT", [128, NIK], F32, kind="ExternalInput").ap()
    outb_h = nc.dram_tensor("outb", [1, D], F32, kind="ExternalInput").ap()
    gamln_h = nc.dram_tensor("gamlnT", [128, NDK], F32, kind="ExternalInput").ap()
    betln_h = nc.dram_tensor("betlnT", [128, NDK], F32, kind="ExternalInput").ap()
    ident_h = nc.dram_tensor("identity", [128, 128], F32, kind="ExternalInput").ap()
    maskf_h = nc.dram_tensor("maskf", [128, 16], F32, kind="ExternalInput").ap()
    meti_h = nc.dram_tensor("meti", [1, 8], I32, kind="ExternalInput").ap()

    out_h = nc.dram_tensor("out", [TN, D], F32, kind="ExternalOutput").ap()
    dbg = {}
    if DEBUG:
        for nm, shp, dt in [("d_m", [NIK, 128, TN], F32), ("d_u", [NIK, 128, TN], F32),
                            ("d_sr", [NIK, 128, TN], F32), ("d_sm", [NIK, 128, TN], F32),
                            ("d_y", [NIK, 128, TN], F32), ("d_xn2", [NDK, 128, TN], BF16),
                            ("d_b", [NST, TN], F32), ("d_c", [NST, TN], F32),
                            ("d_hin", [128, NIK * NST], F32), ("d_hst", [128, NIK * NST], F32),
                            ("d_summ", [128, 528], F32), ("d_gsc", [128, 4], F32),
                            ("d_ccout", [NCORES * SUMR, 1024], F32),
                            ("d_xbe", [NIK, 128, 16], BF16), ("d_xnh", [128, NDK * 4], BF16),
                            ("d_stath", [3, 12], F32), ("d_drb", [128, 512], BF16),
                            ("d_yf", [NIK, 128, TN], F32), ("d_g", [NIK, 128, TN], BF16),
                            ("d_xq", [NDK, 128, TN], BF16), ("d_corr", [NIK, 128, TN], F32)]:
            dbg[nm] = nc.dram_tensor(nm, shp, dt, kind="ExternalOutput").ap()

    m_sp = nc.dram_tensor("m_sp", [NIK, 128, TN], F32, kind="Internal").ap()
    u_sp = nc.dram_tensor("u_sp", [NIK, 128, TN], F32, kind="Internal").ap()
    sr_sp = nc.dram_tensor("sr_sp", [NIK, 128, TN], F32, kind="Internal").ap()
    sm_sp = nc.dram_tensor("sm_sp", [NIK, 128, TN], F32, kind="Internal").ap()
    y_sp = nc.dram_tensor("y_sp", [NIK, 128, TN], F32, kind="Internal").ap()
    xn2_sp = nc.dram_tensor("xn2_sp", [NDK, 128, TN], BF16, kind="Internal").ap()
    b_sp = nc.dram_tensor("b_sp", [NST, TN], F32, kind="Internal").ap()
    c_sp = nc.dram_tensor("c_sp", [NST, TN], F32, kind="Internal").ap()
    winbf_sp = nc.dram_tensor("winbf_sp", [32, 128, NDK * 128], BF16, kind="Internal").ap()
    woutbf_sp = nc.dram_tensor("woutbf_sp", [NIK, 128, D], BF16, kind="Internal").ap()
    wqbf_sp = nc.dram_tensor("wqbf_sp", [NDK, 128, D], BF16, kind="Internal").ap()
    bounce = nc.dram_tensor("bounce", [1, 128], F32, kind="Internal").ap()
    cc_in = nc.dram_tensor("cc_in", [SUMR, 1024], F32, kind="Internal")
    cc_out = nc.dram_tensor("cc_out", [NCORES * SUMR, 1024], F32, kind="Internal")

    ctx = ExitStack()
    with ctx:
        sb = lambda name, shape, dt=F32: ctx.enter_context(nc.sbuf_tensor(name, shape, dt))

        aP = sb("aP", [128, NIK * NST])
        dP = sb("dP", [128, NIK])
        inb = sb("inb", [128, 32])
        inbm = sb("inbm", [128, 16])
        convb = sb("convb", [128, NIK])
        dtbn = sb("dtbn", [128, NIK])
        gamln = sb("gamln", [128, NDK])
        betln = sb("betln", [128, NDK])
        identf = sb("identf", [128, 128])
        identb = sb("identb", [128, 128], BF16)
        onesA = sb("onesA", [128, TA])
        boutb = sb("boutb", [128, D])
        maskf = sb("maskf_sb", [128, 16])
        metis = sb("metis", [1, 8], I32)
        convd = sb("convd_sb", [128, 8 * 1024], BF16)
        xprojTb = sb("xprojTb", [128, 2 * 768], BF16)
        dtTb = sb("dtTb", [DTR, DI], BF16)
        hstate = sb("hstate", [128, NIK * NST])
        hin = sb("hin", [128, NIK * NST])
        smtot = sb("smtot", [128, NIK])
        gmax = sb("gmax", [128, 1])
        gred = sb("gred", [128, 12])
        gscal = sb("gscal", [128, 4])
        summ = sb("summ", [128, 2 * NIK * NST + 16])
        beta2 = sb("beta2", [128, 2])
        stat = sb("stat", [128, 48])
        xnh = sb("xnh", [128, NDK * 4], BF16)
        stath = sb("stath", [3, 12])
        epsv = sb("epsv", [128, 2])
        ggrow = sb("ggrow", [1, 1024])
        gtmp = sb("gtmp", [1, 4])
        sgm = sb("sgm", [128, TA])
        xhalo = sb("xhalo", [3, 1024])

        AR = Arena(nc, ctx, 38400)

        p512 = [ctx.enter_context(nc.psum_tensor(f"p{i}", [128, 512], F32)) for i in range(8)]
        pb6 = p512[6].ap().bitcast(BF16)
        pb7 = p512[7].ap().bitcast(BF16)

        sLD = Sem(nc, ctx, "sLD")
        sSP = Sem(nc, ctx, "sSP")
        sT = Sem(nc, ctx, "sT")
        sV = Sem(nc, ctx, "sV")
        sS = Sem(nc, ctx, "sS")
        sCC = Sem(nc, ctx, "sCC")
        sOU = Sem(nc, ctx, "sOU")
        mk = lambda nm: Sem(nc, ctx, nm)
        pfsl = [mk("pf0"), mk("pf1")]
        pbsl = [mk("pb0"), mk("pb1")]
        xsl = [mk("x0"), mk("x1")]
        wofsl = [mk("w0"), mk("w1")]
        msl = [mk("m0"), mk("m1")]
        usl = [mk("u0"), mk("u1")]
        srsl = [mk("sr0"), mk("sr1")]
        xnsl = mk("xnsp")
        bcsl = mk("bcsp")
        hlsl = mk("halo")
        bbsl = [mk("bb0"), mk("bb1")]
        bmsl = [mk("bm0"), mk("bm1"), mk("bm2"), mk("bm3")]
        busl = [mk("bu0"), mk("bu1"), mk("bu2"), mk("bu3")]
        yevsl = [mk("ye0"), mk("ye1")]
        yldsl = mk("yld")
        ccst = mk("ccst")
        agld = mk("agld")
        misc = mk("miscs")
        cldsl = [mk("cl0"), mk("cl1")]
        csl2 = mk("cl2")
        cbsl2 = mk("ccb")
        x2sl = [mk("x20"), mk("x21")]
        xtsl = [mk("xt0"), mk("xt1")]
        cwsl = mk("cw")

        prog = {k: [] for k in ("sync", "vector", "scalar", "tensor", "gpsimd")}

        def em(eng, fn):
            prog[eng].append(fn)

        def dma(eng, sem, out, in_):
            def f(e, out=out, in_=in_, sem=sem):
                e.dma_start(out=out, in_=in_).then_inc(sem.h, 16)
            em(eng, f)
            sem.n += 16
            return sem.n

        def wait(eng, sem, val):
            if val <= 0:
                return
            def f(e, sem=sem, val=val):
                e.wait_ge(sem.h, val)
            em(eng, f)

        def op(eng, sem, fn):
            def f(e, fn=fn, sem=sem):
                fn(e).then_inc(sem.h, 1)
            em(eng, f)
            sem.n += 1
            return sem.n

        for dst, src in [
            (aP[:, :], alog_h[:, :]), (dP[:, :], dpar_h[:, :]),
            (inb[:, :], inb_h[:, :]), (convb[:, :], convb_h[:, :]),
            (dtbn[:, :], dtb_h[:, :]), (gamln[:, :], gamln_h[:, :]),
            (betln[:, :], betln_h[:, :]), (identf[:, :], ident_h[:, :]),
            (maskf[:, :], maskf_h[:, :]), (metis[:, :], meti_h[:, :]),
            (boutb[:, :], outb_h[0:1, :].partition_broadcast(128)),
        ]:
            dma("sync", sLD, dst, src)
        lc0 = sLD.n

        wait("vector", sLD, lc0)
        op("vector", sV, lambda v: v.memset(onesA[:, :], 1.0))
        op("vector", sV, lambda v: v.memset(hstate[:, :], 0.0))
        op("vector", sV, lambda v: v.memset(smtot[:, :], 0.0))
        op("vector", sV, lambda v: v.memset(gmax[:, :], 0.0))
        op("vector", sV, lambda v: v.tensor_copy(identb[:, :], identf[:, :]))
        op("vector", sV, lambda v: v.tensor_scalar(dtbn[:, :], dtbn[:, :], -1.0, None, ALU.mult))
        op("vector", sV, lambda v: v.tensor_scalar(inbm[:, :], inb[:, 0:16], maskf[:, 0:1], None, ALU.mult))
        op("vector", sV, lambda v: v.memset(epsv[:, 0:1], float(EPS1)))
        op("vector", sV, lambda v: v.memset(epsv[:, 1:2], float(EPS2)))
        pro_v0 = sV.n
        wait("scalar", sLD, lc0)
        op("scalar", sS, lambda s: s.activation(aP[:, :], aP[:, :], AT.Exp))

        AR.mark()
        pf = [AR.f32(2048), AR.f32(2048)]
        pb = [AR.bf16(2048), AR.bf16(2048)]

        jobs = []
        for of in range(32):
            jobs.append((winT_h[of, :, :], None, winbf_sp[of, :, :], 128, 1024))
        for ik in range(NIK):
            jobs.append((woutT_h[ik, :, :], None, woutbf_sp[ik, :, :], 128, 1024))
        for g in range(8):
            jobs.append((convd_h[g, :, :], convd[:, 1024 * g:1024 * (g + 1)], None, 128, 1024))
        for g in range(2):
            jobs.append((xprojT_h[g, :, :], xprojTb[:, 768 * g:768 * (g + 1)], None, 128, 768))
        jobs.append((dtT_h[:, :], dtTb[0:DTR, :], None, DTR, 2048))

        vdone = []
        stdone = []
        for j, (src, sdst, ddst, parts, ncol) in enumerate(jobs):
            if j >= 2:
                wait("sync", sV, vdone[j - 2])
            dma("sync", pfsl[j % 2], pf[j % 2][0:parts, 0:ncol], src)
            wait("vector", pfsl[j % 2], pfsl[j % 2].n)
            if j >= 2 and stdone[j - 2] > 0:
                wait("vector", pbsl[j % 2], stdone[j - 2])
            if sdst is not None:
                vdone.append(op("vector", sV, lambda v, a=sdst, b=pf[j % 2], p=parts, n=ncol:
                                v.tensor_copy(a, b[0:p, 0:n])))
                stdone.append(0)
            else:
                vdone.append(op("vector", sV, lambda v, a=pb[j % 2], b=pf[j % 2], p=parts, n=ncol:
                                v.tensor_copy(a[0:p, 0:n], b[0:p, 0:n])))
                wait("sync", sV, vdone[-1])
                stdone.append(dma("sync", pbsl[j % 2], ddst, pb[j % 2][0:parts, 0:ncol]))

        for g in range(NDK):
            if g >= 2:
                wait("sync", sV, sV.n)
            dma("sync", pfsl[g % 2], pf[g % 2][:, 0:1024], ffnw_h[g, :, :])
            wait("vector", pfsl[g % 2], pfsl[g % 2].n)
            op("vector", sV, lambda v, g=g: v.tensor_reduce(
                gred[:, g:g + 1], pf[g % 2][:, 0:1024], AX.X, ALU.add, apply_absolute_value=True))
        op("vector", sV, lambda v: v.tensor_copy(gred[:, 9:10], onesA[:, 0:1]))
        op("vector", sV, lambda v: v.tensor_copy(gred[:, 9:10], onesA[:, 0:1]))
        op("vector", sV, lambda v: v.tensor_reduce(
            gred[:, 8:9], gred[:, 0:8], AX.X, ALU.add))
        abs_v = sV.n
        wait("tensor", sV, abs_v)
        op("tensor", sT, lambda t: t.matmul(p512[7].ap()[0:1, 0:1], gred[:, 8:9], onesA[:, 0:1],
                                            start=True, stop=True))
        wait("scalar", sT, sT.n)
        op("scalar", sS, lambda s: s.activation(gtmp[0:1, 1:2], p512[7].ap()[0:1, 0:1], AT.Copy,
                                                scale=float(1.0 / (1024.0 * 1024.0))))
        wait("vector", sS, sS.n)
        op("vector", sV, lambda v: v.tensor_scalar(gtmp[0:1, 3:4], gtmp[0:1, 1:2], float(QEPS), None, ALU.add))
        op("vector", sV, lambda v: v.tensor_copy(gred[0:1, 11:12], onesA[0:1, 0:1]))
        op("vector", sV, lambda v: v.tensor_copy(gred[0:1, 11:12], onesA[0:1, 0:1]))
        op("vector", sV, lambda v: v.reciprocal(gtmp[0:1, 0:1], gtmp[0:1, 3:4]))
        wait("sync", sS, sS.n)
        wait("sync", sV, sV.n)
        dma("sync", misc, bounce[0:1, 0:2], gtmp[0:1, 0:2])
        wait("sync", misc, misc.n)
        dma("sync", misc, beta2[:, 0:2], bounce[0:1, 0:2].partition_broadcast(128))
        beta_ld = misc.n
        wsgn = AR.f32(1024)
        wabs = AR.f32(1024)
        for g in range(NDK):
            wait("sync", sV, sV.n)
            dma("sync", pfsl[g % 2], pf[g % 2][:, 0:1024], ffnw_h[g, :, :])
            wait("scalar", pfsl[g % 2], pfsl[g % 2].n)
            wait("scalar", misc, beta_ld)
            op("scalar", sS, lambda s, g=g: s.activation(wabs, pf[g % 2][:, 0:1024], AT.Abs,
                                                         scale=beta2[:, 0:1]))
            op("scalar", sS, lambda s, g=g: s.activation(wsgn, pf[g % 2][:, 0:1024], AT.Sign,
                                                         scale=beta2[:, 0:1]))
            wait("vector", sS, sS.n)
            op("vector", sV, lambda v: v.tensor_scalar(wabs, wabs, 0.5, None, ALU.is_gt))
            op("vector", sV, lambda v: v.tensor_tensor(wsgn, wsgn, wabs, ALU.mult))
            wq_v = op("vector", sV, lambda v, g=g: v.tensor_scalar(
                pb[g % 2][:, 0:1024], wsgn, beta2[:, 1:2], None, ALU.mult))
            wait("sync", sV, wq_v)
            dma("sync", pbsl[g % 2], wqbf_sp[g, :, :], pb[g % 2][:, 0:1024])
            wait("vector", pbsl[g % 2], pbsl[g % 2].n)
        AR.reset()

        AR.mark()
        xt = [[AR.f32(D) for _ in range(4)] for _ in range(2)]
        sqb = AR.f32(D)
        xnt = [[AR.bf16(D) for _ in range(4)] for _ in range(2)]
        xn2t = [[AR.bf16(D) for _ in range(4)] for _ in range(2)]
        xnf = [AR.bf16(TA) for _ in range(NDK)]
        xn2f = [AR.bf16(TA) for _ in range(NDK)]
        wof = [AR.bf16(NDK * 128) for _ in range(2)]
        xbe = [AR.bf16(TA + 4) for _ in range(NIK)]
        ufr = [AR.f32(TA) for _ in range(2)]
        ub = [AR.bf16(TA) for _ in range(NIK)]
        drb = AR.bf16(TA, parts=128)
        bcrow = AR.f32(TA)
        mrotA = [AR.f32(TA) for _ in range(2)]
        smrot = [AR.f32(TA) for _ in range(2)]
        srot = [AR.f32(TA) for _ in range(2)]

        wait("vector", sV, pro_v0)
        for ik in range(NIK):
            op("vector", sV, lambda v, ik=ik: v.memset(xbe[ik][:, 0:4], 0.0))
        dma("sync", hlsl, xhalo[0:3, :], x_h[0:3, :])
        halo_ld = hlsl.n
        wait("scalar", hlsl, halo_ld)
        wait("scalar", sV, pro_v0)
        op("scalar", sS, lambda s: s.activation(sqb[0:3, :], xhalo[0:3, :], AT.Square,
                                                accum_out=stath[:, 0:1]))
        sq_h = sS.n
        wait("vector", hlsl, halo_ld)
        op("vector", sV, lambda v: v.tensor_reduce(stath[:, 1:2], xhalo[0:3, :], AX.X, ALU.add))
        wait("vector", sS, sq_h)
        op("vector", sV, lambda v: v.tensor_scalar(stath[:, 2:3], stath[:, 1:2], float(1.0 / D), None, ALU.mult))
        op("vector", sV, lambda v: v.tensor_scalar(stath[:, 3:4], stath[:, 0:1], float(1.0 / D), None, ALU.mult))
        op("vector", sV, lambda v: v.tensor_scalar(stath[:, 6:7], stath[:, 2:3], -1.0, None, ALU.mult))
        op("vector", sV, lambda v: v.tensor_tensor(stath[:, 4:5], stath[:, 2:3], stath[:, 2:3], ALU.mult))
        op("vector", sV, lambda v: v.tensor_copy(stath[:, 9:10], stath[:, 0:1]))
        op("vector", sV, lambda v: v.tensor_copy(stath[:, 9:10], stath[:, 0:1]))
        op("vector", sV, lambda v: v.tensor_tensor(stath[:, 5:6], stath[:, 3:4], stath[:, 4:5], ALU.subtract))
        var_h = sV.n
        wait("scalar", sV, var_h)
        op("scalar", sS, lambda s: s.activation(stath[:, 8:9], stath[:, 5:6], AT.Sqrt, bias=epsv[0:3, 0:1]))
        r1_h = sS.n
        wait("vector", sS, r1_h)
        op("vector", sV, lambda v: v.reciprocal(stath[:, 7:8], stath[:, 8:9]))
        op("vector", sV, lambda v: v.tensor_scalar(
            xnt[0][0][0:3, :], xhalo[0:3, :], stath[:, 6:7], stath[:, 7:8], ALU.add, ALU.mult))
        hxn_v = sV.n
        wait("tensor", sV, hxn_v)
        for dk in range(NDK):
            op("tensor", sT, lambda t, dk=dk: t.transpose(
                pb6[:, 4 * dk:4 * dk + 3], xnt[0][0][0:3, 128 * dk:128 * dk + 128],
                identb[0:3, 0:3]))
        htr_t = sT.n
        wait("vector", sT, htr_t)
        for dk in range(NDK):
            op("vector", sV, lambda v, dk=dk: v.tensor_copy(
                xnh[:, 4 * dk:4 * dk + 3], pb6[:, 4 * dk:4 * dk + 3]))
        halo_xn_v = sV.n

        for j in range(4):
            dma("sync", xsl[0], xt[0][j][:, :], x_h[3 + 128 * j: 3 + 128 * (j + 1), :])
        xld = [xsl[0].n]
        xn_v_hist = [0, 0]
        trans_T_hist = [0, 0]

        evac_bank = {0: 0, 1: 0, 2: 0}
        wof_cons = [0, 0]
        m_spill = {}
        u_spill = {}
        sr_spill = {}
        conv_evac = {}
        dt_evac = {}
        halo_evac_S = 0
        xn2sp_prev = 0
        bc_spill_prev = 0
        xnf_freeT = 0
        carryV_prev = 0
        xp_freeV_prev = 0

        for i in range(NTILES):
            t0 = TA * i
            buf = i % 2
            if i + 1 < NTILES:
                if i >= 1:
                    wait("sync", sV, xn_v_hist[(i - 1) % 2])
                for j in range(4):
                    dma("sync", xsl[(i + 1) % 2], xt[1 - buf][j][:, :],
                        x_h[3 + TA * (i + 1) + 128 * j: 3 + TA * (i + 1) + 128 * (j + 1), :])
                xld.append(xsl[(i + 1) % 2].n)

            wait("scalar", xsl[i % 2], xld[i])
            for j in range(4):
                op("scalar", sS, lambda s, j=j, buf=buf: s.activation(
                    sqb[:, :], xt[buf][j][:, :], AT.Square, accum_out=stat[:, 12 * j:12 * j + 1]))
            ssq = sS.n
            wait("vector", xsl[i % 2], xld[i])
            for j in range(4):
                op("vector", sV, lambda v, j=j, buf=buf: v.tensor_reduce(
                    stat[:, 12 * j + 1:12 * j + 2], xt[buf][j][:, :], AX.X, ALU.add))
            wait("vector", sS, ssq)
            for j in range(4):
                op("vector", sV, lambda v, j=j: v.tensor_scalar(
                    stat[:, 12 * j + 2:12 * j + 3], stat[:, 12 * j + 1:12 * j + 2],
                    float(1.0 / D), None, ALU.mult))
            for j in range(4):
                op("vector", sV, lambda v, j=j: v.tensor_scalar(
                    stat[:, 12 * j + 3:12 * j + 4], stat[:, 12 * j + 0:12 * j + 1],
                    float(1.0 / D), None, ALU.mult))
            for j in range(4):
                op("vector", sV, lambda v, j=j: v.tensor_scalar(
                    stat[:, 12 * j + 6:12 * j + 7], stat[:, 12 * j + 2:12 * j + 3],
                    -1.0, None, ALU.mult))
            for j in range(4):
                op("vector", sV, lambda v, j=j: v.tensor_tensor(
                    stat[:, 12 * j + 4:12 * j + 5], stat[:, 12 * j + 2:12 * j + 3],
                    stat[:, 12 * j + 2:12 * j + 3], ALU.mult))
            for j in range(4):
                op("vector", sV, lambda v, j=j: v.tensor_tensor(
                    stat[:, 12 * j + 5:12 * j + 6], stat[:, 12 * j + 3:12 * j + 4],
                    stat[:, 12 * j + 4:12 * j + 5], ALU.subtract))
            var_v = sV.n
            wait("scalar", sV, var_v)
            for j in range(4):
                op("scalar", sS, lambda s, j=j: s.activation(
                    stat[:, 12 * j + 8:12 * j + 9], stat[:, 12 * j + 5:12 * j + 6],
                    AT.Sqrt, bias=epsv[:, 0:1]))
            r1_s = sS.n
            wait("vector", sS, r1_s)
            for j in range(4):
                op("vector", sV, lambda v, j=j: v.reciprocal(
                    stat[:, 12 * j + 7:12 * j + 8], stat[:, 12 * j + 8:12 * j + 9]))
            for j in range(4):
                op("vector", sV, lambda v, j=j: v.tensor_tensor(
                    stat[:, 12 * j + 8:12 * j + 9], stat[:, 12 * j + 7:12 * j + 8],
                    stat[:, 12 * j + 7:12 * j + 8], ALU.mult))
            for j in range(4):
                op("vector", sV, lambda v, j=j: v.tensor_tensor(
                    stat[:, 12 * j + 9:12 * j + 10], stat[:, 12 * j + 5:12 * j + 6],
                    stat[:, 12 * j + 8:12 * j + 9], ALU.mult))
            var2_v = sV.n
            wait("scalar", sV, var2_v)
            for j in range(4):
                op("scalar", sS, lambda s, j=j: s.activation(
                    stat[:, 12 * j + 9:12 * j + 10], stat[:, 12 * j + 9:12 * j + 10],
                    AT.Sqrt, bias=epsv[:, 1:2]))
            r2_s = sS.n
            wait("vector", sS, r2_s)
            for j in range(4):
                op("vector", sV, lambda v, j=j: v.reciprocal(
                    stat[:, 12 * j + 10:12 * j + 11], stat[:, 12 * j + 9:12 * j + 10]))
            for j in range(4):
                op("vector", sV, lambda v, j=j: v.tensor_tensor(
                    stat[:, 12 * j + 11:12 * j + 12], stat[:, 12 * j + 7:12 * j + 8],
                    stat[:, 12 * j + 10:12 * j + 11], ALU.mult))
            if i >= 2:
                wait("vector", sT, trans_T_hist[i % 2])
            for j in range(4):
                op("vector", sV, lambda v, j=j, buf=buf, i=i: v.tensor_scalar(
                    xnt[i % 2][j], xt[buf][j][:, :], stat[:, 12 * j + 6:12 * j + 7],
                    stat[:, 12 * j + 7:12 * j + 8], ALU.add, ALU.mult))
            for j in range(4):
                op("vector", sV, lambda v, j=j, buf=buf, i=i: v.tensor_scalar(
                    xn2t[i % 2][j], xt[buf][j][:, :], stat[:, 12 * j + 6:12 * j + 7],
                    stat[:, 12 * j + 11:12 * j + 12], ALU.add, ALU.mult))
            xn_v = sV.n
            xn_v_hist[i % 2] = xn_v

            wait("tensor", sV, xn_v)
            if i == 1:
                wait("tensor", sS, halo_evac_S)
            for dk in range(NDK):
                for j in range(4):
                    op("tensor", sT, lambda t, dk=dk, j=j, i=i: t.transpose(
                        pb6[:, 128 * j:128 * (j + 1)],
                        xnt[i % 2][j][:, 128 * dk:128 * (dk + 1)], identb[:, :]))
                trA = sT.n
                for j in range(4):
                    op("tensor", sT, lambda t, dk=dk, j=j, i=i: t.transpose(
                        pb7[:, 128 * j:128 * (j + 1)],
                        xn2t[i % 2][j][:, 128 * dk:128 * (dk + 1)], identb[:, :]))
                trB = sT.n
                wait("scalar", sT, trA)
                if i > 0:
                    wait("scalar", xnsl, xn2sp_prev)
                op("scalar", sS, lambda s, dk=dk: s.activation(
                    xnf[dk], pb6[:, 0:512], AT.Copy))
                wait("scalar", sT, trB)
                evB = op("scalar", sS, lambda s, dk=dk: s.activation(
                    xn2f[dk], pb7[:, 0:512], AT.Identity,
                    scale=gamln[:, dk:dk + 1], bias=betln[:, dk:dk + 1]))
                wait("tensor", sS, evB)
            trans_S = sS.n
            trans_T_hist[i % 2] = sT.n
            wait("gpsimd", sS, trans_S)
            for dk in range(NDK):
                dma("gpsimd", xnsl, xn2_sp[dk, :, t0:t0 + TA], xn2f[dk])
            xn2sp_prev = xnsl.n
            wait("vector", sS, trans_S)
            for dk in range(NDK):
                op("vector", sV, lambda v, dk=dk: v.tensor_reduce(
                    gred[:, dk:dk + 1], xn2f[dk], AX.X, ALU.max, apply_absolute_value=True))
            op("vector", sV, lambda v: v.tensor_copy(gred[:, 9:10], gmax[:, 0:1]))
            op("vector", sV, lambda v: v.tensor_copy(gred[:, 10:11], gmax[:, 0:1]))
            op("vector", sV, lambda v: v.tensor_reduce(
                gred[:, 8:9], gred[:, 0:8], AX.X, ALU.max))
            op("vector", sV, lambda v: v.tensor_copy(gred[:, 9:10], gmax[:, 0:1]))
            op("vector", sV, lambda v: v.tensor_copy(gred[:, 10:11], gmax[:, 0:1]))
            op("vector", sV, lambda v: v.tensor_tensor(gmax[:, :], gmax[:, :], gred[:, 8:9], ALU.max))

            for of in range(32):
                slot = of % 2
                weng = "sync" if slot == 0 else "scalar"
                wait(weng, sT, wof_cons[slot])
                dma(weng, wofsl[slot], wof[slot], winbf_sp[of, :, :])
                wld = wofsl[slot].n
                bank = of % 3
                wait("tensor", wofsl[slot], wld)
                wait("tensor", sS, evac_bank[bank])
                if i == 0 and of == 0:
                    wait("tensor", sV, halo_xn_v)
                for dk in range(NDK):
                    op("tensor", sT, lambda t, of=of, dk=dk, bank=bank, slot=slot: t.matmul(
                        p512[bank].ap()[:, 0:TA], wof[slot][:, 128 * dk:128 * (dk + 1)],
                        xnf[dk], start=(dk == 0), stop=(dk == NDK - 1)))
                if i == 0 and of < 16:
                    for dk in range(NDK):
                        op("tensor", sT, lambda t, of=of, dk=dk, slot=slot: t.matmul(
                            p512[6].ap()[:, 3 * of:3 * of + 3],
                            wof[slot][:, 128 * dk:128 * (dk + 1)],
                            xnh[:, 4 * dk:4 * dk + 3], start=(dk == 0), stop=(dk == NDK - 1)))
                mm_T = sT.n
                wof_cons[slot] = mm_T
                wait("scalar", sT, mm_T)
                if of < 16:
                    ev = op("scalar", sS, lambda s, of=of, bank=bank: s.activation(
                        xbe[of][:, 4:4 + TA], p512[bank].ap()[:, 0:TA], AT.Identity,
                        bias=inb[:, of:of + 1]))
                else:
                    gof = i * 16 + (of - 16)
                    if gof >= 2:
                        wait("scalar", srsl[gof % 2], sr_spill[gof - 2])
                    ev = op("scalar", sS, lambda s, of=of, bank=bank: s.activation(
                        srot[of % 2], p512[bank].ap()[:, 0:TA], AT.Silu,
                        bias=inb[:, of:of + 1]))
                    wait("gpsimd", sS, ev)
                    sr_spill[gof] = dma("gpsimd", srsl[gof % 2], sr_sp[of - 16, :, t0:t0 + TA], srot[of % 2])
                evac_bank[bank] = ev
            if i == 0:
                wait("scalar", sT, sT.n)
                for of in range(16):
                    op("scalar", sS, lambda s, of=of: s.activation(
                        xbe[of][:, 1:4], p512[6].ap()[:, 3 * of:3 * of + 3], AT.Identity,
                        scale=maskf[:, 0:1], bias=inbm[:, of:of + 1]))
                halo_evac_S = sS.n
                if DEBUG:
                    wait("sync", sS, halo_evac_S)
                    for of in range(16):
                        dma("sync", sOU, dbg["d_xbe"][of, :, 0:16], xbe[of][:, 0:16])
                    dbg_halo_tap = sOU.n
            xnf_freeT = sT.n

            wait("tensor", sS, sS.n)
            if i > 0:
                wait("tensor", sV, carryV_prev)
            for ik in range(NIK):
                bank = 3 + (ik % 2)
                gk = i * NIK + ik
                if gk >= 2:
                    wait("tensor", sS, conv_evac[gk - 2])
                for tap in range(4):
                    g = tap * 16 + ik
                    op("tensor", sT, lambda t, ik=ik, tap=tap, bank=bank, g=g: t.matmul(
                        p512[bank].ap()[:, 0:TA],
                        convd[:, 1024 * (g // 8) + 128 * (g % 8):1024 * (g // 8) + 128 * (g % 8) + 128],
                        xbe[ik][:, 1 + tap:1 + tap + TA],
                        start=(tap == 0), stop=(tap == 3)))
                cT = sT.n
                wait("vector", sT, cT)
                if DEBUG and i == 0 and ik == 0:
                    wait("vector", sOU, dbg_halo_tap)
                op("vector", sV, lambda v, ik=ik: v.tensor_copy(
                    xbe[ik][:, 1:4], xbe[ik][:, TA + 1:TA + 4]))
                wait("scalar", sT, cT)
                if gk >= 2:
                    wait("scalar", usl[gk % 2], u_spill[gk - 2])
                conv_evac[gk] = op("scalar", sS, lambda s, ik=ik, bank=bank: s.activation(
                    ufr[ik % 2], p512[bank].ap()[:, 0:TA], AT.Silu, bias=convb[:, ik:ik + 1]))
                wait("vector", sS, conv_evac[gk])
                op("vector", sV, lambda v, ik=ik: v.tensor_copy(ub[ik], ufr[ik % 2]))
                wait("gpsimd", sS, conv_evac[gk])
                u_spill[gk] = dma("gpsimd", usl[gk % 2], u_sp[ik, :, t0:t0 + TA], ufr[ik % 2])
            carryV_prev = sV.n
            conv_evS = sS.n

            wait("tensor", sV, sV.n)
            wait("tensor", sV, xp_freeV_prev)
            for ik in range(NIK):
                g, r = ik // 8, ik % 8
                op("tensor", sT, lambda t, ik=ik, g=g, r=r: t.matmul(
                    p512[5].ap()[0:96, 0:TA], xprojTb[:, 768 * g + 96 * r:768 * g + 96 * r + 96],
                    ub[ik], start=(ik == 0), stop=(ik == NIK - 1)))
            xp_T = sT.n
            wait("vector", sT, xp_T)
            if i > 0:
                wait("vector", bcsl, bc_spill_prev)
            op("vector", sV, lambda v: v.tensor_copy(drb[0:64, :], p512[5].ap()[0:64, 0:TA]))
            op("vector", sV, lambda v: v.tensor_scalar(
                bcrow[64:96, :], p512[5].ap()[64:96, 0:TA], maskf[64:96, 1:2], None, ALU.mult))
            bc_v = sV.n
            xp_freeV_prev = bc_v
            wait("gpsimd", sV, bc_v)
            dma("gpsimd", bcsl, b_sp[:, t0:t0 + TA], bcrow[64:80, :])
            bc_spill_prev = dma("gpsimd", bcsl, c_sp[:, t0:t0 + TA], bcrow[80:96, :])

            for of in range(NIK):
                bank = 3 + (of % 2)
                gk = i * NIK + of
                if of < 2:
                    wait("tensor", sV, bc_v)
                    wait("tensor", sS, conv_evS)
                if gk >= 2:
                    wait("tensor", sS, dt_evac[gk - 2])
                op("tensor", sT, lambda t, of=of, bank=bank: t.matmul(
                    p512[bank].ap()[:, 0:TA], dtTb[0:DTR, 128 * of:128 * (of + 1)],
                    drb[0:64, :], start=True, stop=True))
                dt_T = sT.n
                wait("scalar", sT, dt_T)
                op("scalar", sS, lambda s, of=of, bank=bank: s.activation(
                    sgm[:, :], p512[bank].ap()[:, 0:TA], AT.Sigmoid, scale=-1.0,
                    bias=dtbn[:, of:of + 1]))
                dt_evac[gk] = sS.n
                if gk >= 2:
                    wait("scalar", msl[gk % 2], m_spill[gk - 2])
                m_s = op("scalar", sS, lambda s, of=of: s.activation(
                    mrotA[of % 2], sgm[:, :], AT.Ln))
                wait("vector", sS, m_s)
                if gk >= 2:
                    wait("vector", msl[gk % 2], m_spill[gk - 2])
                op("vector", sV, lambda v, of=of: v.tensor_tensor_scan(
                    smrot[of % 2], onesA[:, :], mrotA[of % 2], smtot[:, of:of + 1],
                    ALU.mult, ALU.add))
                if of >= 1:
                    op("vector", sV, lambda v, of=of: v.tensor_copy(
                        smtot[:, of - 1:of], smrot[(of - 1) % 2][:, TA - 1:TA]))
                if of == NIK - 1:
                    op("vector", sV, lambda v: v.tensor_copy(gred[:, 11:12], onesA[:, 0:1]))
                    op("vector", sV, lambda v: v.tensor_copy(gred[:, 11:12], onesA[:, 0:1]))
                    op("vector", sV, lambda v, of=of: v.tensor_copy(
                        smtot[:, of:of + 1], smrot[of % 2][:, TA - 1:TA]))
                smv = sV.n
                wait("gpsimd", sV, smv)
                dma("gpsimd", msl[gk % 2], m_sp[of, :, t0:t0 + TA], mrotA[of % 2])
                m_spill[gk] = dma("gpsimd", msl[gk % 2], sm_sp[of, :, t0:t0 + TA], smrot[of % 2])

        a_S, a_V, a_T = sS.n, sV.n, sT.n
        a_spills = [(msl[0], msl[0].n), (msl[1], msl[1].n), (usl[0], usl[0].n),
                    (usl[1], usl[1].n), (srsl[0], srsl[0].n), (srsl[1], srsl[1].n),
                    (xnsl, xnsl.n), (bcsl, bcsl.n)]
        AR.reset()

        AR.mark()
        bbc = [AR.f32(TB) for _ in range(NST)]
        cbc = [AR.f32(TB) for _ in range(NST)]
        du2 = [AR.f32(TB) for _ in range(2)]
        mrotB = [AR.f32(TB) for _ in range(4)]
        uB = [AR.f32(TB) for _ in range(4)]
        dAb = [AR.f32(TB) for _ in range(16)]
        dBb = [AR.f32(TB) for _ in range(2)]
        hBb = [AR.f32(TB) for _ in range(8)]
        hcb = [AR.bf16(TB) for _ in range(8)]
        yev = [AR.f32(TB) for _ in range(2)]
        yld = AR.f32(TB)

        for sl, v in a_spills:
            wait("sync", sl, v)

        half_useV = {}
        mload = {}
        uload = {}
        duV = {}
        scanV = {}
        hc_T = {}
        ysp = {}
        exp_S = {}
        yfinV = {}
        key_by_idx = {}
        state_S = {}
        scan_svs = {}
        expnn_by = {}
        stnn_by = {}
        hcTn_by = {}
        yev_guard = [0, 0]
        yld_guard = [0]

        wait("gpsimd", bcsl, bcsl.n)
        wait("scalar", usl[0], usl[0].n)
        wait("scalar", usl[1], usl[1].n)
        wait("scalar", srsl[0], srsl[0].n)
        wait("scalar", srsl[1], srsl[1].n)
        bc_pairs = [(s, h) for s in range(NSUB) for h in range(2)]
        bc_ld = {}

        def issue_bc(q):
            s2, h2 = bc_pairs[q]
            if s2 > 0:
                wait("gpsimd", sV, half_useV[(s2 - 1, h2)])
            for n in range(8 * h2, 8 * h2 + 8):
                dma("gpsimd", bbsl[h2], bbc[n][:, :],
                    b_sp[n:n + 1, TB * s2:TB * (s2 + 1)].partition_broadcast(128))
                dma("gpsimd", bbsl[h2], cbc[n][:, :],
                    c_sp[n:n + 1, TB * s2:TB * (s2 + 1)].partition_broadcast(128))
            bc_ld[bc_pairs[q]] = bbsl[h2].n

        issue_bc(0)
        issue_bc(1)
        for s in range(NSUB):
            t0 = TB * s
            for half in range(2):
                q = s * 2 + half
                for ik in range(NIK):
                    bank = ik % 2
                    key = (s, half, ik)
                    kidx = (s * 2 + half) * NIK + ik
                    pk4 = key_by_idx.get(kidx - 4)
                    if pk4 is not None:
                        wait("sync", sS, exp_S[pk4])
                        wait("sync", sV, duV[pk4])
                    msem = bmsl[kidx % 4]
                    usem = busl[kidx % 4]
                    mload[key] = dma("sync", msem, mrotB[kidx % 4][:, :], m_sp[ik, :, t0:t0 + TB])
                    if pk4 is not None:
                        wait("scalar", sV, duV[pk4])
                    uload[key] = dma("scalar", usem, uB[kidx % 4][:, :], u_sp[ik, :, t0:t0 + TB])
                    key_by_idx[kidx] = key
                    wait("vector", usem, uload[key])
                    wait("vector", msem, mload[key])
                    duV[key] = op("vector", sV, lambda v, sl=kidx % 4, d=kidx % 2: v.tensor_tensor(
                        du2[d], mrotB[sl], uB[sl], ALU.mult))
                    wait("scalar", msem, mload[key])
                    pk2 = key_by_idx.get(kidx - 2)
                    if pk2 is not None:
                        wait("scalar", sV, scanV[pk2])
                    ab = 8 * (kidx % 2)
                    exp_nn = []
                    for nn in range(8):
                        n = 8 * half + nn
                        op("scalar", sS, lambda s_, ik=ik, n=n, nn=nn, ab=ab, sl=kidx % 4: s_.activation(
                            dAb[ab + nn], mrotB[sl], AT.Exp,
                            scale=aP[:, NST * ik + n:NST * ik + n + 1]))
                        exp_nn.append(sS.n)
                    exp_S[key] = sS.n
                    expnn_by[key] = exp_nn
                    pk1 = key_by_idx.get(kidx - 1)
                    if pk1 is not None:
                        h_prev, ik_prev = pk1[1], pk1[2]
                        svsp = scan_svs[pk1]
                        st_nn = []
                        for nn in range(8):
                            wait("scalar", sV,
                                 svsp[nn + 2] if nn < 6 else duV[key])
                            op("scalar", sS, lambda s_, ikp=ik_prev, n=8 * h_prev + nn, nn=nn: s_.copy(
                                hstate[:, NST * ikp + n:NST * ikp + n + 1],
                                hBb[nn][:, TB - 1:TB]))
                            st_nn.append(sS.n)
                        state_S[pk1] = sS.n
                        stnn_by[pk1] = st_nn
                    wait("vector", bbsl[half], bc_ld[(s, half)])
                    pk_hc = pk1
                    svs = []
                    for nn in range(8):
                        n = 8 * half + nn
                        op("vector", sV, lambda v, ik=ik, n=n, nn=nn, d=kidx % 2: v.tensor_tensor(
                            dBb[nn % 2], du2[d], bbc[n], ALU.mult))
                        wait("vector", sS, expnn_by[key][nn])
                        if pk_hc is not None and pk_hc in stnn_by:
                            wait("vector", sS, stnn_by[pk_hc][nn])
                        op("vector", sV, lambda v, ik=ik, n=n, nn=nn, ab=ab: v.tensor_tensor_scan(
                            hBb[nn], dAb[ab + nn], dBb[nn % 2],
                            hstate[:, NST * ik + n:NST * ik + n + 1], ALU.mult, ALU.add))
                        if pk_hc is not None and pk_hc in hcTn_by:
                            wait("vector", sT, hcTn_by[pk_hc][nn])
                        op("vector", sV, lambda v, ik=ik, n=n, nn=nn: v.tensor_tensor(
                            hcb[nn], hBb[nn], cbc[n], ALU.mult))
                        svs.append(sV.n)
                    scanV[key] = sV.n
                    scan_svs[key] = svs
                    if half == 1:
                        wait("gpsimd", yevsl[ik % 2], ysp[(s, 0, ik)])
                        if yld_guard[0] > 0:
                            wait("gpsimd", sV, yld_guard[0])
                        yl = dma("gpsimd", yldsl, yld[:, :], y_sp[ik, :, t0:t0 + TB])
                    wait("tensor", sV, scanV[key])
                    if half == 0:
                        pk = (s - 1, ik)
                        if s > 0:
                            wait("tensor", sV, yfinV[pk])
                    hcTn = []
                    for nn in range(8):
                        n = 8 * half + nn
                        op("tensor", sT, lambda t, ik=ik, n=n, nn=nn, bank=bank, half=half: t.matmul(
                            p512[bank].ap()[:, 0:TB], identb[:, :], hcb[nn],
                            start=(nn == 0), stop=(nn == 7)))
                        hcTn.append(sT.n)
                    hc_T[key] = sT.n
                    hcTn_by[key] = hcTn
                    if half == 0:
                        wait("scalar", sT, hc_T[key])
                        if yev_guard[ik % 2] > 0:
                            wait("scalar", yevsl[ik % 2], yev_guard[ik % 2])
                        pv = op("scalar", sS, lambda s_, ik=ik, bank=bank: s_.activation(
                            yev[ik % 2], p512[bank].ap()[:, 0:TB], AT.Copy))
                        wait("tensor", sS, pv)
                        wait("gpsimd", sS, pv)
                        ysp[(s, 0, ik)] = dma("gpsimd", yevsl[ik % 2], y_sp[ik, :, t0:t0 + TB], yev[ik % 2])
                        yev_guard[ik % 2] = ysp[(s, 0, ik)]
                    else:
                        wait("vector", sT, hc_T[key])
                        wait("vector", yldsl, yl)
                        if yev_guard[ik % 2] > 0:
                            wait("vector", yevsl[ik % 2], yev_guard[ik % 2])
                        yfinV[(s, ik)] = op("vector", sV, lambda v, ik=ik, bank=bank: v.scalar_tensor_tensor(
                            yev[ik % 2], p512[bank].ap()[:, 0:TB], 1.0, yld[:, :],
                            ALU.mult, ALU.add))
                        wait("tensor", sV, yfinV[(s, ik)])
                        wait("gpsimd", sV, yfinV[(s, ik)])
                        ysp[(s, 1, ik)] = dma("gpsimd", yevsl[ik % 2], y_sp[ik, :, t0:t0 + TB], yev[ik % 2])
                        yev_guard[ik % 2] = ysp[(s, 1, ik)]
                        yld_guard[0] = yfinV[(s, ik)]
                half_useV[(s, half)] = sV.n
                if q + 2 < len(bc_pairs):
                    issue_bc(q + 2)

        lk = key_by_idx[NSUB * 2 * NIK - 1]
        svsp = scan_svs[lk]
        for nn in range(8):
            wait("scalar", sV,
                 svsp[nn + 2] if nn < 6 else yfinV[(NSUB - 1, lk[2])])
            op("scalar", sS, lambda s_, ikp=lk[2], n=8 * lk[1] + nn, nn=nn: s_.copy(
                hstate[:, NST * ikp + n:NST * ikp + n + 1],
                hBb[nn][:, TB - 1:TB]))
        state_S[lk] = sS.n

        b_V, b_S, b_T = sV.n, sS.n, sT.n
        b_yspills = [(yevsl[0], yevsl[0].n), (yevsl[1], yevsl[1].n)]

        wait("scalar", sV, b_V)
        for ik in range(NIK):
            op("scalar", sS, lambda s, ik=ik: s.activation(
                summ[:, NST * ik:NST * ik + NST], aP[:, NST * ik:NST * ik + NST],
                AT.Exp, scale=smtot[:, ik:ik + 1]))
        wait("vector", sS, sS.n)
        op("vector", sV, lambda v: v.tensor_copy(summ[:, 256:512], hstate[:, :]))
        wait("tensor", sV, b_V)
        op("tensor", sT, lambda t: t.transpose(p512[5].ap()[0:1, 0:128], gmax[:, 0:1], identf[:, :]))
        wait("vector", sT, sT.n)
        op("vector", sV, lambda v: v.tensor_reduce(
            summ[0:1, 512:513], p512[5].ap()[0:1, 0:128], AX.X, ALU.max))
        summ_v = sV.n
        wait("sync", sV, summ_v)
        dma("sync", ccst, cc_in.ap()[0:32, :], summ[:, 0:256])
        dma("sync", ccst, cc_in.ap()[32:64, :], summ[:, 256:512])
        dma("sync", ccst, cc_in.ap()[64:65, 0:1], summ[0:1, 512:513])
        cc_stage = ccst.n
        wait("gpsimd", ccst, cc_stage)
        for sl, v in b_yspills:
            wait("gpsimd", sl, v)

        def cc_fn(g):
            g.collective_compute(
                "AllGather", ALU.bypass,
                replica_groups=[list(range(NCORES))],
                ins=[cc_in.ap()], outs=[cc_out.ap()],
            ).then_inc(sCC.h, 1)
        em("gpsimd", cc_fn)
        sCC.n += 1
        cc_done = sCC.n

        AR.reset()
        AR.mark()
        decp = [AR.f32(256) for _ in range(NCORES)]
        hfp = [AR.f32(256) for _ in range(NCORES)]
        Hp = [AR.f32(256) for _ in range(NCORES)]
        tkb = AR.f32(256)
        wait("sync", sCC, cc_done)
        for r in range(NCORES):
            dma("sync", agld, ggrow[0:1, r:r + 1],
                cc_out.ap()[SUMR * r + 64:SUMR * r + 65, 0:1])
        for p in range(NCORES):
            dma("sync", agld, decp[p], cc_out.ap()[SUMR * p:SUMR * p + 32, :])
            dma("sync", agld, hfp[p], cc_out.ap()[SUMR * p + 32:SUMR * p + 64, :])
        pred_ld = agld.n
        wait("vector", agld, pred_ld)
        op("vector", sV, lambda v: v.tensor_reduce(gtmp[0:1, 2:3], ggrow[0:1, 0:NCORES], AX.X, ALU.max))
        op("vector", sV, lambda v: v.tensor_copy(gred[0:1, 11:12], onesA[0:1, 0:1]))
        op("vector", sV, lambda v: v.tensor_copy(gred[0:1, 11:12], onesA[0:1, 0:1]))
        gm_v = sV.n
        op("vector", sV, lambda v: v.reciprocal(gtmp[0:1, 0:1], gtmp[0:1, 2:3]))
        wait("scalar", sV, gm_v)
        op("scalar", sS, lambda s: s.activation(gtmp[0:1, 1:2], gtmp[0:1, 2:3], AT.Copy,
                                                scale=float(1.0 / QB)))
        g_s = sS.n
        wait("sync", sS, g_s)
        wait("sync", sV, sV.n)
        dma("sync", misc, bounce[0:1, 0:2], gtmp[0:1, 0:2])
        wait("sync", misc, misc.n)
        dma("sync", misc, gscal[:, 0:2], bounce[0:1, 0:2].partition_broadcast(128))
        gscal_ld = misc.n
        wait("vector", agld, pred_ld)
        for p in range(NCORES):
            if p % 4 == 0:
                op("vector", sV, lambda v, p=p: v.tensor_copy(Hp[p], hfp[p]))
            else:
                op("vector", sV, lambda v, p=p: v.tensor_tensor(tkb, decp[p], Hp[p - 1], ALU.mult))
                op("vector", sV, lambda v, p=p: v.tensor_tensor(Hp[p], tkb, hfp[p], ALU.add))
        op("vector", sV, lambda v: v.memset(hin[:, :], 0.0))
        for p in range(NCORES):
            op("vector", sV, lambda v, p=p: v.scalar_tensor_tensor(
                hin[:, :], Hp[p], maskf[:, 8 + p:9 + p], hin[:, :], ALU.mult, ALU.add))
        hin_v = sV.n
        AR.reset()

        AR.mark()
        woutb = [AR.bf16(D) for _ in range(NIK)]
        wqb = [AR.bf16(D) for _ in range(NDK)]
        cbcC = [AR.f32(TC) for _ in range(NST)]
        ySc = [AR.f32(TC) for _ in range(2)]
        smC = [AR.f32(TC) for _ in range(2)]
        srwhy = [AR.f32(TC) for _ in range(2)]
        uCv = [AR.f32(TC) for _ in range(2)]
        eCv = [AR.f32(TC) for _ in range(4)]
        zCv = [AR.bf16(TC) for _ in range(4)]
        gbuf = [AR.bf16(TC) for _ in range(NIK)]
        xqs = [AR.bf16(TC) for _ in range(NDK)]
        xn2C = [AR.bf16(TC) for _ in range(2)]
        xtC = [AR.f32(D) for _ in range(2)]
        outCb = AR.f32(512)

        for ik in range(NIK):
            dma("sync", cwsl, woutb[ik], woutbf_sp[ik, :, :])
        for g in range(NDK):
            dma("sync", cwsl, wqb[g], wqbf_sp[g, :, :])
        cw_ld = cwsl.n

        cbc_useV = 0
        xn2_useV = {}
        exp_useS = {}
        g_V = {}
        ev_V = {}
        zV = {}
        zT = {}
        ev_grp = {}
        gbuf_freeT = 0

        for i in range(NCT):
            t0 = TC * i
            wait("sync", sV, cbc_useV)
            for n in range(NST):
                dma("sync", cbsl2, cbcC[n][:, :], c_sp[n:n + 1, t0:t0 + TC].partition_broadcast(128))
            cbc_ld = cbsl2.n
            wait("vector", misc, gscal_ld)
            wait("vector", sT, gbuf_freeT)
            for dk in range(NDK):
                if dk >= 2:
                    wait("sync", sV, xn2_useV[dk - 2])
                dma("sync", x2sl[dk % 2], xn2C[dk % 2], xn2_sp[dk, :, t0:t0 + TC])
                wait("vector", x2sl[dk % 2], x2sl[dk % 2].n)
                op("vector", sV, lambda v, dk=dk: v.tensor_scalar(
                    xqs[dk], xn2C[dk % 2], gscal[:, 0:1], None, ALU.mult))
                op("vector", sV, lambda v, dk=dk: v.tensor_scalar(
                    xqs[dk], xqs[dk], float(QB), float(CLIP_HI), ALU.mult, ALU.min))
                xn2_useV[dk] = op("vector", sV, lambda v, dk=dk: v.tensor_scalar(
                    xqs[dk], xqs[dk], float(CLIP_LO), gscal[:, 1:2], ALU.max, ALU.mult))
                if DEBUG:
                    wait("sync", sV, xn2_useV[dk])
                    dma("sync", sOU, dbg["d_xq"][dk, :, t0:t0 + TC], xqs[dk])
            xq_v = sV.n

            wait("scalar", sV, hin_v)
            wait("vector", sV, hin_v)
            for ik in range(NIK):
                bank = ik % 2
                gk = i * NIK + ik
                if gk >= 2:
                    wait("sync", sS, exp_useS[gk - 2])
                    wait("sync", sV, g_V[gk - 2])
                dma("sync", cldsl[ik % 2], smC[ik % 2], sm_sp[ik, :, t0:t0 + TC])
                dma("sync", cldsl[ik % 2], ySc[ik % 2], y_sp[ik, :, t0:t0 + TC])
                if gk >= 2:
                    wait("scalar", sV, g_V[gk - 2])
                dma("scalar", csl2, srwhy[ik % 2], sr_sp[ik, :, t0:t0 + TC])
                dma("scalar", csl2, uCv[ik % 2], u_sp[ik, :, t0:t0 + TC])
                su_ld = csl2.n
                all_ld = cldsl[ik % 2].n
                wait("scalar", cldsl[ik % 2], all_ld)
                for n in range(NST):
                    fn = gk * NST + n
                    if fn >= 4:
                        wait("scalar", sV, zV[fn - 4])
                    op("scalar", sS, lambda s, ik=ik, n=n: s.activation(
                        eCv[n % 4], smC[ik % 2], AT.Exp,
                        scale=aP[:, NST * ik + n:NST * ik + n + 1]))
                    eS = sS.n
                    wait("vector", sS, eS)
                    wait("vector", cbsl2, cbc_ld)
                    if fn >= 4:
                        wait("vector", sT, zT[fn - 4])
                    zV[fn] = op("vector", sV, lambda v, ik=ik, n=n: v.scalar_tensor_tensor(
                        zCv[n % 4], eCv[n % 4], hin[:, NST * ik + n:NST * ik + n + 1],
                        cbcC[n], ALU.mult, ALU.mult))
                    wait("tensor", sV, zV[fn])
                    if n == 0 and (prev_ev := ev_V.get(gk - 2)) is not None:
                        wait("tensor", sV, prev_ev)
                    zT[fn] = op("tensor", sT, lambda t, ik=ik, n=n, bank=bank: t.matmul(
                        p512[bank].ap()[:, 0:TC], identb[:, :], zCv[n % 4],
                        start=(n == 0), stop=(n == NST - 1)))
                exp_useS[gk] = sS.n
                nsum_T = sT.n
                wait("vector", sT, nsum_T)
                wait("vector", cldsl[ik % 2], all_ld)
                wait("vector", csl2, su_ld)
                if DEBUG:
                    dcv = op("vector", sV, lambda v, ik=ik, bank=bank: v.tensor_copy(
                        eCv[3], p512[bank].ap()[:, 0:TC]))
                    wait("sync", sV, dcv)
                    dma("sync", sOU, dbg["d_corr"][ik, :, t0:t0 + TC], eCv[3])
                    wait("scalar", sOU, sOU.n)
                op("vector", sV, lambda v, ik=ik, bank=bank: v.scalar_tensor_tensor(
                    ySc[ik % 2], p512[bank].ap()[:, 0:TC], 1.0, ySc[ik % 2], ALU.mult, ALU.add))
                op("vector", sV, lambda v, ik=ik: v.tensor_scalar(
                    uCv[ik % 2], uCv[ik % 2], dP[:, ik:ik + 1], None, ALU.mult))
                op("vector", sV, lambda v, ik=ik: v.tensor_tensor(
                    ySc[ik % 2], ySc[ik % 2], uCv[ik % 2], ALU.add))
                if DEBUG:
                    dyv = sV.n
                    wait("sync", sV, dyv)
                    dma("sync", sOU, dbg["d_yf"][ik, :, t0:t0 + TC], ySc[ik % 2])
                ev_V[gk] = op("vector", sV, lambda v, ik=ik: v.tensor_tensor(
                    gbuf[ik], ySc[ik % 2], srwhy[ik % 2], ALU.mult))
                g_V[gk] = ev_V[gk]
                if DEBUG:
                    wait("sync", sV, ev_V[gk])
                    dma("sync", sOU, dbg["d_g"][ik, :, t0:t0 + TC], gbuf[ik])
                    wait("vector", sOU, sOU.n)
            corr_v = sV.n
            cbc_useV = corr_v

            wait("tensor", sV, corr_v)
            wait("tensor", sV, xq_v)
            wait("tensor", cwsl, cw_ld)
            xld_c = {}
            for tc4 in range(4):
                gt = i * 4 + tc4
                if gt >= 2:
                    wait("sync", sV, ev_grp[(gt - 2) * 2 + 1])
                dma("sync", xtsl[tc4 % 2], xtC[tc4 % 2][:, :],
                    x_h[3 + t0 + 128 * tc4: 3 + t0 + 128 * (tc4 + 1), :])
                xld_c[tc4] = xtsl[tc4 % 2].n
                wait("vector", xtsl[tc4 % 2], xld_c[tc4])
                xpb_v = op("vector", sV, lambda v, tc4=tc4: v.tensor_tensor(
                    xtC[tc4 % 2][:, :], xtC[tc4 % 2][:, :], boutb[:, :], ALU.add))
                for hf in range(2):
                    gg = gt * 2 + hf
                    bank = 2 + (gg % 2)
                    if gg >= 2:
                        wait("tensor", sV, ev_grp[gg - 2])
                    for ik in range(NIK):
                        op("tensor", sT, lambda t, ik=ik, tc4=tc4, hf=hf, bank=bank: t.matmul(
                            p512[bank].ap()[:, 0:512],
                            gbuf[ik][:, 128 * tc4:128 * (tc4 + 1)],
                            woutb[ik][:, 512 * hf:512 * (hf + 1)],
                            start=(ik == 0), stop=False))
                    for g in range(NDK):
                        op("tensor", sT, lambda t, g=g, tc4=tc4, hf=hf, bank=bank: t.matmul(
                            p512[bank].ap()[:, 0:512],
                            xqs[g][:, 128 * tc4:128 * (tc4 + 1)],
                            wqb[g][:, 512 * hf:512 * (hf + 1)],
                            start=False, stop=(g == NDK - 1)))
                    grp_T = sT.n
                    wait("vector", sT, grp_T)
                    wait("vector", sV, xpb_v)
                    if gg >= 1:
                        wait("vector", sOU, sOU.n)
                    ev_grp[gg] = op("vector", sV, lambda v, tc4=tc4, hf=hf, bank=bank:
                                    v.scalar_tensor_tensor(
                        outCb[:, :], p512[bank].ap()[:, 0:512], 1.0,
                        xtC[tc4 % 2][:, 512 * hf:512 * (hf + 1)], ALU.mult, ALU.add))
                    wait("sync", sV, ev_grp[gg])
                    dma("sync", sOU, out_h[t0 + 128 * tc4: t0 + 128 * (tc4 + 1),
                                           512 * hf:512 * (hf + 1)], outCb[:, :])
            gbuf_freeT = sT.n
        AR.reset()

        if DEBUG:
            wait("sync", sOU, sOU.n)
            wait("sync", sSP, sSP.n)
            wait("sync", sV, sV.n)
            wait("sync", sS, sS.n)
            for nm, src in [("d_m", m_sp[:, :, :]), ("d_u", u_sp[:, :, :]),
                            ("d_sr", sr_sp[:, :, :]), ("d_sm", sm_sp[:, :, :]),
                            ("d_y", y_sp[:, :, :]), ("d_xn2", xn2_sp[:, :, :]),
                            ("d_b", b_sp[:, :]), ("d_c", c_sp[:, :]),
                            ("d_ccout", cc_out.ap()[:, :])]:
                dma("sync", sOU, dbg[nm][tuple([slice(None)] * len(dbg[nm].shape))], src)
            def _tapsb(e):
                e.dma_start(out=dbg["d_xnh"], in_=xnh[:, :]).then_inc(sOU.h, 16)
                e.dma_start(out=dbg["d_stath"], in_=stath[:, :]).then_inc(sOU.h, 16)
                e.dma_start(out=dbg["d_hin"], in_=hin[:, :]).then_inc(sOU.h, 16)
                e.dma_start(out=dbg["d_hst"], in_=hstate[:, :]).then_inc(sOU.h, 16)
                e.dma_start(out=dbg["d_summ"], in_=summ[:, 0:528]).then_inc(sOU.h, 16)
                e.dma_start(out=dbg["d_gsc"], in_=gscal[:, :]).then_inc(sOU.h, 16)
                e.dma_start(out=dbg["d_drb"], in_=drb[0:128, 0:512]).then_inc(sOU.h, 16)
            em("sync", _tapsb)
            sOU.n += 112
        wait("sync", sOU, sOU.n)

        import os as _os
        _frac = float(_os.environ.get("KFRAC", "1"))
        _feng = _os.environ.get("KENG", "")
        if _frac < 1:
            for _k in prog:
                if not _feng or _k == _feng:
                    prog[_k] = prog[_k][:int(len(prog[_k]) * _frac)]
        if _os.environ.get("KDUMP"):
            L = prog["sync"]
            lo, hi = int(len(L) * 0.745), int(len(L) * 0.765)
            for idx in range(lo, hi):
                f = L[idx]
                code = f.__code__
                print("sync idx", idx, code.co_filename, code.co_firstlineno,
                      {k: (v.cell_contents if hasattr(v, "cell_contents") else v)
                       for k, v in zip(code.co_freevars, f.__closure__ or [])} if False else "")
        blk = ctx.enter_context(nc.Block())

        @blk.sync
        def _(sync):
            for f in prog["sync"]:
                f(sync)

        @blk.vector
        def _(vector):
            for f in prog["vector"]:
                f(vector)

        @blk.scalar
        def _(scalar):
            for f in prog["scalar"]:
                f(scalar)

        @blk.tensor
        def _(tensor):
            for f in prog["tensor"]:
                f(tensor)

        @blk.gpsimd
        def _(gpsimd):
            for f in prog["gpsimd"]:
                f(gpsimd)

    return nc


def _prep_host(inputs):
    x = np.asarray(inputs["L0_attn_x"])
    inw = np.asarray(inputs["L0_attn_in_proj_w"])
    inbi = np.asarray(inputs["L0_attn_in_proj_b"])
    convw = np.asarray(inputs["L0_attn_conv_w"])
    convbi = np.asarray(inputs["L0_attn_conv_b"])
    xpw = np.asarray(inputs["L0_attn_x_proj_w"])
    dtw = np.asarray(inputs["L0_attn_dt_proj_w"])
    dtbi = np.asarray(inputs["L0_attn_dt_proj_b"])
    alog = np.asarray(inputs["L0_attn_A_log"])
    dpar = np.asarray(inputs["L0_attn_D_param"])
    outw = np.asarray(inputs["L0_attn_out_proj_w"])
    outbi = np.asarray(inputs["L0_attn_out_proj_b"])
    ffnw = np.asarray(inputs["L0_ffn_weight"])
    lng = np.asarray(inputs["L0_ffn_ln_gamma"])
    lnb = np.asarray(inputs["L0_ffn_ln_beta"])

    winT = np.ascontiguousarray(
        inw.reshape(32, 128, 8, 128).transpose(0, 3, 2, 1).reshape(32, 128, 8 * 128))
    woutT = np.ascontiguousarray(outw.T.reshape(NIK, 128, D))
    ffnwT = np.ascontiguousarray(ffnw.reshape(NDK, 128, D))
    xpT = np.ascontiguousarray(
        xpw.T.reshape(2, 8, 128, 96).transpose(0, 2, 1, 3).reshape(2, 128, 8 * 96))
    dtT = np.ascontiguousarray(dtw.T)
    convd = np.zeros((64, 128, 128), dtype=np.float32)
    for tap in range(4):
        for ik in range(NIK):
            np.fill_diagonal(convd[tap * 16 + ik], convw[ik * 128:(ik + 1) * 128, 0, tap])
    convd = np.ascontiguousarray(convd.reshape(8, 8, 128, 128).transpose(0, 2, 1, 3).reshape(8, 128, 1024))
    alogT = np.ascontiguousarray(alog.reshape(NIK, 128, NST).transpose(1, 0, 2).reshape(128, NIK * NST))
    dparT = np.ascontiguousarray(dpar.reshape(NIK, 128).T)
    inbT = np.ascontiguousarray(inbi.reshape(32, 128).T)
    convbT = np.ascontiguousarray(convbi.reshape(NIK, 128).T)
    dtbT = np.ascontiguousarray(dtbi.reshape(NIK, 128).T)
    gamlnT = np.ascontiguousarray(lng.reshape(NDK, 128).T)
    betlnT = np.ascontiguousarray(lnb.reshape(NDK, 128).T)
    ident = np.eye(128, dtype=np.float32)
    outb2 = np.ascontiguousarray(outbi.reshape(1, D))

    common = dict(winT=winT, woutT=woutT, ffnw=ffnwT, xprojT=xpT, dtT=dtT,
                  convd=convd, alogT=alogT, dparT=dparT, inbT=inbT,
                  convbT=convbT, dtbT=dtbT, outb=outb2, gamlnT=gamlnT,
                  betlnT=betlnT, identity=ident)

    in_maps = []
    for c in range(NCORES):
        seq, chunk = c // 4, c % 4
        t0 = chunk * TN
        xc = np.zeros((TN + 3, D), dtype=np.float32)
        if chunk == 0:
            xc[3:] = x[seq, 0:TN]
        else:
            xc[0:3] = x[seq, t0 - 3:t0]
            xc[3:] = x[seq, t0:t0 + TN]
        maskf = np.zeros((128, 16), dtype=np.float32)
        maskf[:, 0] = 0.0 if chunk == 0 else 1.0
        maskf[64:80, 1] = -1.0
        maskf[80:96, 1] = 1.0
        if chunk > 0:
            maskf[:, 8 + (c - 1)] = 1.0
        meti = np.zeros((1, 8), dtype=np.int32)
        for j in range(1, 4):
            p = (c - j) % NCORES
            meti[0, j - 1] = p * 3
            meti[0, 3 + j - 1] = p * 3 + 1
        m = dict(common)
        m["x"] = xc
        m["maskf"] = maskf
        m["meti"] = meti
        in_maps.append(m)
    return in_maps


def kernel(**inputs):
    in_maps = _prep_host(inputs)
    nc = build_nc()
    res = run_bass_kernel_spmd(nc, in_maps, core_ids=list(range(NCORES)))
    outs = [np.asarray(res.results[c]["out"], dtype=np.float32) for c in range(NCORES)]
    full = np.concatenate(outs, axis=0).reshape(2, 16384, 1024)
    if DEBUG:
        return full, res
    return full
